# version 11
# baseline (speedup 1.0000x reference)
import numpy as np
import concourse.bass as bass
import concourse.mybir as mybir

F32 = mybir.dt.float32
BF16 = mybir.dt.bfloat16
AF = mybir.ActivationFunctionType
ALU = mybir.AluOpType
AX = mybir.AxisListType

C = 1024
NCH = 8
NE = 4
EPS = 1e-5

TOK = 2048
TT = 512
TH = 1024
RG = [[0, 1, 2, 3], [4, 5, 6, 7]]


def configure(tok=2048, th=1024, rg=None):
    global TOK, TH, RG, NTT, NTH, NTHT, NTC
    TOK, TH = tok, th
    if rg is not None:
        RG = rg
    NTT = TOK // TT
    NTH = TOK // TH
    NTHT = TH // TT
    NTC = TOK // 128


configure()

ENGS = ("sync", "tensor", "vector", "scalar", "gpsimd")


class Em:
    def __init__(self):
        self.prog = {e: [] for e in ENGS}
        self.cnt = {}

    def newsem(self, name):
        self.cnt[name] = 0

    def op(self, eng, fn, inc=None, waits=()):
        ws = []
        for t in waits:
            if t is None:
                continue
            (s, v) = t
            if v > 0:
                ws.append((s, v))
        self.prog[eng].append((tuple(ws), fn, inc))
        if inc is not None:
            self.cnt[inc[0]] += inc[1]
            return (inc[0], self.cnt[inc[0]])
        return None

    def tok(self, sem):
        return (sem, self.cnt[sem])

    def replay(self, block, semh):
        def mk(name):
            def body(e):
                lastw = {}
                for ws, fn, inc in self.prog[name]:
                    for (s, v) in ws:
                        if lastw.get(s, 0) < v:
                            e.wait_ge(semh[s], v)
                            lastw[s] = v
                    ins = fn(e)
                    if inc is not None:
                        ins.then_inc(semh[inc[0]], inc[1])
            return body
        block.sync(mk("sync"))
        block.tensor(mk("tensor"))
        block.vector(mk("vector"))
        block.scalar(mk("scalar"))
        block.gpsimd(mk("gpsimd"))


class PBank:
    def __init__(self, ap):
        self.t = ap
        self.opened = False
        self.pending = []

    def ap(self):
        return self.t.ap()

    def begin(self):
        if not self.opened:
            self.opened = True
            w = self.pending
            self.pending = []
            return True, w
        return False, []

    def consumed(self, tok):
        if tok is not None:
            self.pending.append(tok)
        self.opened = False


class Slots:
    def __init__(self, n):
        self.n = n
        self.last = [None] * n
        self.i = 0

    def acquire(self):
        i = self.i % self.n
        self.i += 1
        w = self.last[i]
        self.last[i] = None
        return i, ([w] if w else [])

    def done(self, i, tok):
        self.last[i] = tok


class Row:
    def __init__(self, t, r, off, n, rows=1):
        self.t, self.r, self.off, self.n, self.rows = t, r, off, n, rows

    def __getitem__(self, idx):
        sp, sf = idx
        p0 = sp.start or 0
        p1 = sp.stop if sp.stop is not None else self.rows
        f0 = sf.start or 0
        f1 = sf.stop if sf.stop is not None else self.n
        return self.t[self.r + p0:self.r + p1, self.off + f0:self.off + f1]


class MK:
    def __init__(self, dbg=None, phase_limit=99):
        self.dbg = dbg
        self.phase_limit = phase_limit
        self.nc = bass.Bass("TRN2", target_bir_lowering=False, debug=False)
        self.em = Em()
        for s in ("LD", "LR", "PE", "DV", "AC", "PL", "SG", "ST", "CC"):
            self.em.newsem(s)

    def dma(self, queue, out, in_, sem="LD", waits=()):
        return self.em.op(queue, lambda e, o=out, i=in_: e.dma_start(out=o, in_=i),
                          inc=(sem, 16), waits=waits)

    def mm(self, bank, out, lhsT, rhs, stop, waits=(), inc=None, tile=None, fstart=None):
        st, bw = bank.begin()
        if fstart is not None:
            st = fstart
        return self.em.op("tensor",
                          lambda e, o=out, l=lhsT, r=rhs, s=st, sp=stop, tp=tile: e.matmul(
                              o, l, r, start=s, stop=sp, tile_position=tp),
                          inc=inc, waits=list(waits) + bw)

    def v(self, fn, waits=()):
        return self.em.op("vector", fn, inc=("DV", 1), waits=waits)

    def a(self, fn, waits=()):
        return self.em.op("scalar", fn, inc=("AC", 1), waits=waits)

    def p(self, fn, waits=()):
        return self.em.op("gpsimd", fn, inc=("PL", 1), waits=waits)

    def bar(self):
        e = self.em
        return [e.tok("PE"), e.tok("DV"), e.tok("AC"), e.tok("PL")]

    def build(self):
        nc = self.nc
        ap = lambda t: t.ap()
        self.d_xT = ap(nc.dram_tensor("xT", [C, TOK], F32, kind="ExternalInput"))
        self.d_wqT = ap(nc.dram_tensor("wqT", [C, C], F32, kind="ExternalInput"))
        self.d_wkT = ap(nc.dram_tensor("wkT", [C, C], F32, kind="ExternalInput"))
        self.d_wvT = ap(nc.dram_tensor("wvT", [C, C], F32, kind="ExternalInput"))
        self.d_woT = ap(nc.dram_tensor("woT", [C, C], F32, kind="ExternalInput"))
        self.d_w1 = ap(nc.dram_tensor("w1", [NE, C, C], F32, kind="ExternalInput"))
        self.d_w2 = ap(nc.dram_tensor("w2", [NE, C, C], F32, kind="ExternalInput"))
        self.d_gatec = ap(nc.dram_tensor("gatec", [128, 32], F32, kind="ExternalInput"))
        self.d_gateb = ap(nc.dram_tensor("gateb", [1, 4], F32, kind="ExternalInput"))
        self.d_bq_c = ap(nc.dram_tensor("bq_c", [128, 8], F32, kind="ExternalInput"))
        self.d_bo_c = ap(nc.dram_tensor("bo_c", [128, 8], F32, kind="ExternalInput"))
        self.d_bk_r = ap(nc.dram_tensor("bk_r", [1, C], F32, kind="ExternalInput"))
        self.d_bv_r = ap(nc.dram_tensor("bv_r", [1, C], F32, kind="ExternalInput"))
        self.d_b1cat = ap(nc.dram_tensor("b1cat", [1, NE * C], F32, kind="ExternalInput"))
        self.d_g1cat = ap(nc.dram_tensor("g1cat", [1, NE * C], F32, kind="ExternalInput"))
        self.d_g2cat = ap(nc.dram_tensor("g2cat", [1, NE * C], F32, kind="ExternalInput"))
        self.d_b2cat = ap(nc.dram_tensor("b2cat", [1, NE * C], F32, kind="ExternalInput"))
        self.d_g1c = ap(nc.dram_tensor("g1c", [128, 32], F32, kind="ExternalInput"))
        self.d_be1c = ap(nc.dram_tensor("be1c", [128, 32], F32, kind="ExternalInput"))
        self.d_g2c = ap(nc.dram_tensor("g2c", [128, 32], F32, kind="ExternalInput"))
        self.d_rkl = ap(nc.dram_tensor("rkl", [8, C], F32, kind="ExternalInput"))
        self.d_idn = ap(nc.dram_tensor("idn", [128, 128], F32, kind="ExternalInput"))
        self.d_cvec = ap(nc.dram_tensor("cvec", [128, 4], F32, kind="ExternalInput"))
        self.d_out = ap(nc.dram_tensor("outT", [C, TOK], F32, kind="ExternalOutput"))
        self.d_ccin = ap(nc.dram_tensor("ccin", [8 * 128, 129], F32, kind="Internal"))
        self.d_ccout = ap(nc.dram_tensor("ccout", [8 * 128, 129], F32, kind="Internal"))

        from contextlib import ExitStack
        self.ctx = ExitStack()
        sb = lambda n, s, d: self.ctx.enter_context(nc.sbuf_tensor(n, s, d))
        SLAB = max(NCH * TOK, 16384)
        self.slab1 = sb("slab1", [128, SLAB], BF16)
        self.slab2 = sb("slab2", [128, SLAB], BF16)
        self.slab3 = sb("slab3", [128, SLAB], BF16)
        self.slab4 = sb("slab4", [128, 4096], F32)
        self.slab5 = sb("slab5", [128, SLAB], BF16)
        self.tmpbf = sb("tmpbf", [128, 6 * TT], BF16)
        self.tmpf = sb("tmpf", [128, 2 * TT], F32)
        self.kvf = sb("kvf", [128, 8 * 130], F32)
        self.kv_bf = sb("kv_bf", [128, 8 * 130], BF16)
        self.g1g1b1 = sb("g1g1b1", [2, C], BF16)
        self.ones = sb("ones", [128, 512], BF16)
        self.idnb = sb("idnb", [128, 128], BF16)
        self.cvec = sb("cvecs", [128, 4], F32)
        self.bq_c = sb("bq_cs", [128, 8], F32)
        self.bo_c = sb("bo_cs", [128, 8], F32)
        self.g1c = sb("g1cs", [128, 32], F32)
        self.be1c = sb("be1cs", [128, 32], F32)
        self.g2c = sb("g2cs", [128, 32], F32)
        self.invg1 = sb("invg1", [128, 32], F32)
        self.ivg1sq = sb("ivg1sq", [128, 32], BF16)
        self.ivg2 = sb("ivg2", [128, 32], BF16)
        self.ivg2sq = sb("ivg2sq", [128, 32], BF16)
        self.gsc = sb("gsc", [128, 32], F32)
        self.gatec = sb("gatecs", [128, 32], BF16)
        self.gatecf = sb("gatecf", [128, 32], F32)
        self.rklb = sb("rklb", [8, C], BF16)
        self.w1s = sb("w1s", [128, 8], BF16)
        self.w1sf = sb("w1sf", [128, 8], F32)
        self.wtm = sb("wtm", [128, 4], BF16)
        self.rowar = sb("rowar", [1, 3 * TT + TH + 2 * C + 8], F32)
        o = 0
        self.rowf = Row(self.rowar, 0, 0, 3 * TT); o += 3 * TT
        self.m1f = Row(self.rowar, 0, o, TH); o += TH
        self.rowpair = Row(self.rowar, 0, o, 2 * C); o += 2 * C
        self.sb1s = Row(self.rowar, 0, o, 4); o += 4
        self.gatebf = Row(self.rowar, 0, o, 4)
        self.rowab = sb("rowab", [1, 2 * NE * TH + 3 * TT + 2 * C + 8], BF16)
        o = 0
        self.wcat = Row(self.rowab, 0, 0, NE * TH); o += NE * TH
        self.m2acat = Row(self.rowab, 0, o, NE * TH); o += NE * TH
        self.rowb = Row(self.rowab, 0, o, 3 * TT); o += 3 * TT
        self.g1b1row = Row(self.rowab, 0, o, C); o += C
        self.g2b2row = Row(self.rowab, 0, o, C); o += C
        self.gateb = Row(self.rowab, 0, o, 4)
        self.bk_r = Row(self.g1g1b1, 0, 0, C)
        self.bv_r = Row(self.kv_bf, 0, 0, C)
        self.rnkr = self.kv_bf
        nc.const_aps.register(F32, EPS, self.cvec[:, 0:1])
        nc.const_aps.register(F32, -0.5, self.cvec[:, 1:2])
        nc.const_aps.register(F32, 1.0 / 1024.0, self.cvec[:, 2:3])
        nc.const_aps.register(F32, -1.0, self.cvec[:, 3:4])

        self.pb = [PBank(self.ctx.enter_context(nc.psum_tensor(f"pb{i}", [128, 512], F32)))
                   for i in range(8)]
        self.bfs = Slots(6)
        self.ffs = Slots(2)
        self.rfs = Slots(3)
        self.rbs = Slots(3)
        self.stage_free = []

        self.phase0()
        if self.phase_limit >= 1:
            self.tk_done = self.proj_tm(0, self.t_wk, self.bk_r, self.slab2, True)
            self.tv_done = self.proj_tm(8192, self.t_wv, self.bv_r, self.slab3, False)
            self.t_wq = self.load_weight(self.d_wqT, self.slab5, 0,
                                         extra_wait=self.em.tok("PE"))
        if self.phase_limit >= 2:
            self.phaseA_kv_collective()
            self.phaseA_q()
        if self.phase_limit >= 3:
            self.phaseA_attn()
        if self.phase_limit >= 4:
            self.phaseA_wo_ln1()
        if self.phase_limit >= 5:
            self.phaseB()
        self.finish()
        return nc

    def phase0(self):
        dma, v = self.dma, self.v
        t_idn = dma("sync", self.slab4[0:128, 0:128], self.d_idn)
        t_rkl = dma("sync", self.slab4[0:8, 128:1152], self.d_rkl)
        dma("sync", self.cvec[:], self.d_cvec)
        t_bk = dma("sync", self.slab4[0:1, 1152:2176], self.d_bk_r)
        t_bv = dma("sync", self.slab4[0:1, 2176:3200], self.d_bv_r)
        toks = []
        for (dst, src) in [(self.bq_c[:, :], self.d_bq_c), (self.bo_c[:, :], self.d_bo_c),
                           (self.g1c[:, :], self.d_g1c), (self.be1c[:, :], self.d_be1c),
                           (self.g2c[:, :], self.d_g2c), (self.gatecf[:, :], self.d_gatec),
                           (self.gatebf[0:1, :], self.d_gateb)]:
            toks.append(dma("sync", dst, src))
        t_sm = toks[-1]
        v(lambda e: e.memset(self.ones[:], 1.0))
        v(lambda e: e.memset(self.kvf[:], 0.0))
        t_idc = v(lambda e: e.tensor_copy(self.idnb[:], self.slab4[0:128, 0:128]),
                  waits=[t_sm])
        t_rkc = v(lambda e: e.tensor_copy(self.rklb[:], self.slab4[0:8, 128:1152]),
                  waits=[t_sm])
        t_bkc = v(lambda e: e.tensor_copy(self.bk_r[0:1, :], self.slab4[0:1, 1152:2176]),
                  waits=[t_sm])
        t_bvc = v(lambda e: e.tensor_copy(self.bv_r[0:1, :], self.slab4[0:1, 2176:3200]),
                  waits=[t_sm])
        tc = v(lambda e: e.tensor_copy(self.gatec[:, :], self.gatecf[:, :]), waits=[t_sm])
        tc = v(lambda e: e.tensor_copy(self.gateb[0:1, :], self.gatebf[0:1, :]), waits=[t_sm])
        self.t_casts = tc
        self.t_stage0 = [t_idc, t_rkc, t_bkc, t_bvc]
        t1 = v(lambda e: e.tensor_copy(self.ivg2[:], self.g1c[:]), waits=[t_sm])
        t2 = v(lambda e: e.tensor_copy(self.gsc[:], self.ivg2[:]), waits=[t1])
        t3 = v(lambda e: e.reciprocal(self.invg1[:], self.gsc[:]), waits=[t2])
        t4 = v(lambda e: e.tensor_mul(self.gsc[:], self.invg1[:], self.invg1[:]), waits=[t3])
        t5 = v(lambda e: e.tensor_copy(self.ivg1sq[:], self.gsc[:]), waits=[t4])
        t6 = v(lambda e: e.tensor_copy(self.ivg2[:], self.g2c[:]), waits=[t5])
        t7 = v(lambda e: e.tensor_copy(self.gsc[:], self.ivg2[:]), waits=[t6])
        t8 = v(lambda e: e.reciprocal(self.gsc[:], self.gsc[:]), waits=[t7])
        t9 = v(lambda e: e.tensor_copy(self.ivg2[:], self.gsc[:]), waits=[t8])
        t10 = v(lambda e: e.tensor_mul(self.gsc[:], self.gsc[:], self.gsc[:]), waits=[t9])
        self.t_ginv = v(lambda e: e.tensor_copy(self.ivg2sq[:], self.gsc[:]), waits=[t10])
        XS = 2048 if TOK > 2048 else TOK
        gate = list(self.t_stage0)
        tc = None
        for w in range(4):
            for s in range(2):
                dma("sync", self.slab4[:, s * XS:s * XS + TOK],
                    self.d_xT[(2 * w + s) * 128:(2 * w + s + 1) * 128, :],
                    waits=gate if s == 0 else [])
            tokw = self.em.tok("LD")
            gate = []
            for s in range(2):
                tc = v(lambda e, k=2 * w + s, s=s: e.tensor_copy(
                    self.slab1[:, k * TOK:(k + 1) * TOK],
                    self.slab4[:, s * XS:s * XS + TOK]), waits=[tokw])
                gate.append(tc)
        self.t_xbf = tc
        self.stage_free = [tc] + gate
        self.stage_slot_free = None
        self.t_wk = self.load_weight(self.d_wkT, self.slab5, 0)
        self.t_wv = self.load_weight(self.d_wvT, self.slab5, 8192)

    def load_weight(self, dsrc, dest, doff, scale=None, w1s=False,
                    extra_wait=None, cast_waits=()):
        t = None
        tw = None
        gate = list(self.stage_free)
        if extra_wait is not None:
            gate.append(extra_wait)
        for wave in range(2):
            for j in range(4):
                k = wave * 4 + j
                self.dma("sync", self.slab4[:, j * 1024:(j + 1) * 1024],
                         dsrc[k * 128:(k + 1) * 128, :], waits=gate if j == 0 else [])
            tokw = self.em.tok("LD")
            gate = []
            for j in range(4):
                k = wave * 4 + j
                cw = list(cast_waits) if k == 0 else []
                if w1s:
                    tw = self.v(lambda e, k=k, j=j: e.tensor_reduce(
                        self.w1sf[:, k:k + 1], self.slab4[:, j * 1024:(j + 1) * 1024],
                        AX.X, ALU.add), waits=[tokw] + cw)
                    gate.append(tw)
                if scale is None:
                    t = self.p(lambda e, k=k, j=j: e.tensor_copy(
                        dest[:, doff + k * 1024: doff + (k + 1) * 1024],
                        self.slab4[:, j * 1024:(j + 1) * 1024]), waits=[tokw] + cw)
                else:
                    t = self.p(lambda e, k=k, j=j, g=scale: e.tensor_tensor(
                        dest[:, doff + k * 1024: doff + (k + 1) * 1024],
                        self.slab4[:, j * 1024:(j + 1) * 1024], g, ALU.mult),
                        waits=[tokw] + cw)
                gate.append(t)
        self.stage_free = gate
        if w1s:
            self.t_w1s = self.v(lambda e: e.tensor_copy(self.w1s[:], self.w1sf[:]),
                                waits=[tw, getattr(self, "w1s_free", None)])
        return t

    def proj_tm(self, wofs, w_tok, biasrow, dest, do_elu):
        mm, v, a = self.mm, self.v, self.a
        last = None
        for j in range(NTC):
            for h2 in range(2):
                bank = self.pb[(j * 2 + h2) % 4]
                ps = bank.ap()
                for ci in range(8):
                    w = [w_tok, self.t_xbf] if ci == 0 else []
                    mm(bank, ps[:, 0:TT],
                       self.slab1[:, ci * TOK + j * 128: ci * TOK + (j + 1) * 128],
                       self.slab5[:, wofs + ci * 1024 + h2 * 512: wofs + ci * 1024 + h2 * 512 + 512],
                       stop=False, waits=w)
                tmm = mm(bank, ps[:, 0:TT], self.ones[0:1, 0:128],
                         biasrow[0:1, h2 * 512: h2 * 512 + 512], stop=True, inc=("PE", 1))
                dcol = j * 1024 + h2 * 512
                if do_elu:
                    s1, w1_ = self.bfs.acquire()
                    s2, w2_ = self.bfs.acquire()
                    ta = a(lambda e, ps=ps, s=s1: e.activation(
                        self.tmpbf[:, s * TT:(s + 1) * TT], ps[:, 0:TT], AF.Exp),
                        waits=[tmm] + w1_)
                    t1 = v(lambda e, ps=ps, s=s2: e.tensor_scalar(
                        self.tmpbf[:, s * TT:(s + 1) * TT], ps[:, 0:TT], 0.0, None, ALU.max),
                        waits=[ta] + w2_)
                    tk = v(lambda e, s1=s1, s2=s2, d=dcol: e.scalar_tensor_tensor(
                        dest[:, d:d + 512], self.tmpbf[:, s1 * TT:(s1 + 1) * TT], 1.0,
                        self.tmpbf[:, s2 * TT:(s2 + 1) * TT], ALU.min, ALU.add),
                        waits=[ta, t1])
                    self.bfs.done(s1, tk)
                    self.bfs.done(s2, tk)
                    bank.consumed(ta)
                    bank.consumed(t1)
                    last = tk
                else:
                    tk = a(lambda e, ps=ps, d=dcol: e.copy(dest[:, d:d + 512], ps[:, 0:TT]),
                           waits=[tmm])
                    bank.consumed(tk)
                    last = tk
        return last

    def phaseA_kv_collective(self):
        em, mm, v = self.em, self.mm, self.v
        evac = []
        for p_ in range(8):
            bank = self.pb[p_]
            ps = bank.ap()
            tmm = None
            for j in range(NTC):
                kcol = j * 1024 + p_ * 128
                w = [self.tk_done, self.tv_done] if j == 0 else []
                mm(bank, ps[:, 0:128], self.slab2[:, kcol:kcol + 128],
                   self.slab3[:, kcol:kcol + 128], stop=False, waits=w)
                tmm = mm(bank, ps[:, 128:129], self.slab2[:, kcol:kcol + 128],
                         self.ones[:, 0:1], stop=(j == NTC - 1),
                         inc=("PE", 1) if j == NTC - 1 else None)
            te = v(lambda e, p_=p_, ps=ps: e.tensor_copy(
                self.kvf[:, p_ * 130:p_ * 130 + 129], ps[:, 0:129]), waits=[tmm])
            bank.consumed(te)
            evac.append(self.dma("gpsimd", self.d_ccin[p_ * 128:(p_ + 1) * 128, :],
                                 self.kvf[:, p_ * 130:p_ * 130 + 129], sem="SG", waits=[te]))
        self.t_cc = em.op("gpsimd", lambda e: e.collective_compute(
            "AllReduce", ALU.add, RG, [self.d_ccin.opt()], [self.d_ccout.opt()]),
            inc=("CC", 1), waits=[evac[-1]])

    def phaseA_q(self):
        em, mm, v, a = self.em, self.mm, self.v, self.a
        t_kvPE = em.tok("PE")
        last = None
        for ch in range(NCH):
            for tt in range(NTT):
                bank = self.pb[(ch * NTT + tt) % 4]
                ps = bank.ap()
                for ci in range(8):
                    w = [self.t_wq] if (ch == 0 and tt == 0 and ci == 0) else []
                    mm(bank, ps[:, 0:TT],
                       self.slab5[:, ci * 1024 + ch * 128: ci * 1024 + ch * 128 + 128],
                       self.slab1[:, ci * TOK + tt * TT: ci * TOK + (tt + 1) * TT],
                       stop=(ci == 7), waits=w, inc=("PE", 1) if ci == 7 else None)
                tmm = em.tok("PE")
                s1, w1_ = self.bfs.acquire()
                s2, w2_ = self.bfs.acquire()
                ta = a(lambda e, ps=ps, ch=ch, s=s1: e.activation(
                    self.tmpbf[:, s * TT:(s + 1) * TT], ps[:, 0:TT], AF.Exp,
                    bias=self.bq_c[:, ch:ch + 1]), waits=[tmm] + w1_)
                t1 = v(lambda e, ps=ps, ch=ch, s=s2: e.tensor_scalar(
                    self.tmpbf[:, s * TT:(s + 1) * TT], ps[:, 0:TT],
                    self.bq_c[:, ch:ch + 1], 0.0, ALU.add, ALU.max), waits=[ta] + w2_)
                dcol = ch * TOK + tt * TT
                wq0 = [t_kvPE] if (ch == 0 and tt == 0) else []
                tk = v(lambda e, s1=s1, s2=s2, d=dcol: e.scalar_tensor_tensor(
                    self.slab2[:, d:d + TT], self.tmpbf[:, s1 * TT:(s1 + 1) * TT], 1.0,
                    self.tmpbf[:, s2 * TT:(s2 + 1) * TT], ALU.min, ALU.add),
                    waits=[ta, t1] + wq0)
                self.bfs.done(s1, tk)
                self.bfs.done(s2, tk)
                bank.consumed(ta)
                bank.consumed(t1)
                last = tk
        self.q_done = last
        self.t_wo = self.load_weight(self.d_woT, self.slab1, 0, extra_wait=self.em.tok("PE"))

    def phaseA_attn(self):
        em, mm, v, a = self.em, self.mm, self.v, self.a
        t2 = None
        for p_ in range(8):
            t2 = self.dma("gpsimd", self.kvf[:, p_ * 130:p_ * 130 + 129],
                          self.d_ccout[p_ * 128:(p_ + 1) * 128, :], sem="SG",
                          waits=[self.t_cc] if p_ == 0 else [])
        t_kvbf = v(lambda e: e.tensor_copy(self.kv_bf[:], self.kvf[:]), waits=[t2])
        units = [(p_, h2, tt) for p_ in range(8) for h2 in range(2) for tt in range(NTT)]
        D = 2
        st = {}
        last = None
        rslots = Slots(8)
        rrow = lambda i: self.m2acat[0:1, i * TT:(i + 1) * TT]

        def stageA(u):
            p_, h2, tt = units[u]
            bank = self.pb[u % 6]
            ps = bank.ap()
            lo, hi = h2 * 64, h2 * 64 + 64
            lh = self.kv_bf[lo:hi, p_ * 130 + h2 * 64: p_ * 130 + h2 * 64 + 64]
            ld = self.kv_bf[lo:hi, p_ * 130 + 128: p_ * 130 + 129]
            qs = self.slab2[lo:hi, p_ * TOK + tt * TT: p_ * TOK + (tt + 1) * TT]
            w = [t_kvbf, self.q_done] if u == 0 else []
            mm(bank, ps[lo:hi, 0:TT], lh, qs, stop=True, waits=w,
               tile=(h2 * 64, h2 * 64))
            dnp = 64 if h2 == 0 else 0
            tmm = mm(bank, ps[dnp:dnp + 1, 0:TT], ld, qs, stop=True,
                     inc=("PE", 1), tile=(h2 * 64, dnp), fstart=True)
            rs, wr = self.rfs.acquire()
            td = v(lambda e, ps=ps, rs=rs, dnp=dnp: e.tensor_scalar(
                self.rowf[0:1, rs * TT:rs * TT + TT], ps[dnp:dnp + 1, 0:TT], 1e-6, None,
                ALU.max), waits=[tmm] + wr)
            ri, wri = rslots.acquire()
            def _recip(e, rs=rs, ri=ri):
                with self.nc.allow_low_precision(reason="bf16 recip row feeds bf16 matmul"):
                    return e.reciprocal(rrow(ri), self.rowf[0:1, rs * TT:rs * TT + TT])
            tb = v(_recip, waits=[td] + wri)
            self.rfs.done(rs, tb)
            st[u] = (bank, ps, ri, tb, td)

        def stageC(u):
            nonlocal last
            p_, h2, tt = units[u]
            bank, ps, ri, tb, td = st.pop(u)
            lo, hi = h2 * 64, h2 * 64 + 64
            rbank = self.pb[6 + u % 2]
            rp = rbank.ap()
            trep = mm(rbank, rp[lo:hi, 0:TT], self.ones[0:1, 0:64], rrow(ri), stop=True,
                      waits=[tb], inc=("PE", 1), tile=(0, h2 * 64))
            rslots.done(ri, trep)
            sl, wsl = self.bfs.acquire()
            tev = a(lambda e, rp=rp, s=sl, lo=lo, hi=hi: e.copy(
                self.tmpbf[lo:hi, s * TT:(s + 1) * TT], rp[lo:hi, 0:TT]),
                waits=[trep] + wsl)
            rbank.consumed(tev)
            dcol = p_ * TOK + tt * TT
            tmul = v(lambda e, ps=ps, s=sl, lo=lo, hi=hi, d=dcol: e.tensor_mul(
                self.slab3[lo:hi, d:d + TT], ps[lo:hi, 0:TT],
                self.tmpbf[lo:hi, s * TT:(s + 1) * TT]), waits=[tev, td])
            self.bfs.done(sl, tmul)
            bank.consumed(tmul)
            last = tmul

        for u in range(len(units)):
            stageA(u)
            if u >= D:
                stageC(u - D)
        for u in range(len(units) - D, len(units)):
            stageC(u)
        self.am_done = last

    def phaseA_wo_ln1(self):
        em, mm, v, a = self.em, self.mm, self.v, self.a
        y_done = None
        for tt in range(NTT):
            tx = None
            for ch in range(NCH):
                w = []
                if ch == 0:
                    w = [y_done] if tt > 0 else list(self.stage_free)
                tx = self.dma("sync", self.slab4[:, ch * TT // 1:(ch) * TT + TT][:, 0:TT] if False else self.slab4[:, (ch % 8) * 512:(ch % 8) * 512 + TT],
                              self.d_xT[ch * 128:(ch + 1) * 128, tt * TT:(tt + 1) * TT],
                              waits=w)
            s_toks = []
            bf_state = {}
            sumb, sqb = self.pb[4], self.pb[5]
            pss, psq = sumb.ap(), sqb.ap()

            def ln1stat(ch, pss=None, psq=None):
                pss = sumb.ap() if pss is None else pss
                psq = sqb.ap() if psq is None else psq
                s1, s2, tb, tq = bf_state.pop(ch)
                mm(sumb, pss[0:1, 0:TT], self.ones[:, 0:1],
                   self.tmpbf[:, s1 * TT:(s1 + 1) * TT], stop=(ch == NCH - 1),
                   waits=[tb], inc=("PE", 1))
                tq2 = mm(sqb, psq[0:1, 0:TT], self.ones[:, 0:1],
                         self.tmpbf[:, s2 * TT:(s2 + 1) * TT], stop=(ch == NCH - 1),
                         waits=[tq], inc=("PE", 1))
                self.bfs.done(s1, tq2)
                self.bfs.done(s2, tq2)

            for ch in range(NCH):
                bank = self.pb[ch % 4]
                ps = bank.ap()
                for ci in range(8):
                    w = [self.t_wo, self.am_done] if (tt == 0 and ch == 0 and ci == 0) else []
                    mm(bank, ps[:, 0:TT],
                       self.slab1[:, ci * 1024 + ch * 128: ci * 1024 + ch * 128 + 128],
                       self.slab3[:, ci * TOK + tt * TT: ci * TOK + (tt + 1) * TT],
                       stop=(ci == 7), waits=w, inc=("PE", 1) if ci == 7 else None)
                tmm = em.tok("PE")
                ts_ = v(lambda e, ps=ps, ch=ch: e.scalar_tensor_tensor(
                    self.slab4[:, ch * 512:ch * 512 + TT], ps[:, 0:TT],
                    self.bo_c[:, ch:ch + 1], self.slab4[:, ch * 512:ch * 512 + TT],
                    ALU.add, ALU.add), waits=[tmm, tx])
                s_toks.append(ts_)
                bank.consumed(ts_)
                s1, w1_ = self.bfs.acquire()
                s2, w2_ = self.bfs.acquire()
                tb = a(lambda e, ch=ch, s=s1: e.copy(
                    self.tmpbf[:, s * TT:(s + 1) * TT],
                    self.slab4[:, ch * 512:ch * 512 + TT]),
                    waits=[ts_] + w1_)
                tq = v(lambda e, s1=s1, s2=s2: e.tensor_mul(
                    self.tmpbf[:, s2 * TT:(s2 + 1) * TT], self.tmpbf[:, s1 * TT:(s1 + 1) * TT],
                    self.tmpbf[:, s1 * TT:(s1 + 1) * TT]), waits=[tb] + w2_)
                bf_state[ch] = (s1, s2, tb, tq)
                if ch >= 1:
                    ln1stat(ch - 1)
            ln1stat(NCH - 1)
            tstat = em.tok("PE")
            r0, wr0 = self.rfs.acquire()
            r1, wr1 = self.rfs.acquire()
            r2, wr2 = self.rfs.acquire()
            R = lambda i: self.rowf[0:1, i * TT:(i + 1) * TT]
            tm_ = v(lambda e, r0=r0: e.tensor_scalar(R(r0), pss[0:1, 0:TT], 1.0 / 1024.0,
                                                     None, ALU.mult), waits=[tstat] + wr0)
            tv_ = v(lambda e, r1=r1: e.tensor_scalar(R(r1), psq[0:1, 0:TT], 1.0 / 1024.0,
                                                     None, ALU.mult), waits=[tstat] + wr1)
            sumb.consumed(tm_)
            sqb.consumed(tv_)
            tm2 = v(lambda e, r0=r0, r2=r2: e.tensor_mul(R(r2), R(r0), R(r0)),
                    waits=[tm_] + wr2)
            tvv = v(lambda e, r1=r1, r2=r2: e.tensor_sub(R(r1), R(r1), R(r2)),
                    waits=[tv_, tm2])
            tl = a(lambda e, r1=r1: e.activation(R(r1), R(r1), AF.Ln, bias=EPS), waits=[tvv])
            tr_ = a(lambda e, r1=r1: e.activation(R(r1), R(r1), AF.Exp, scale=-0.5),
                    waits=[tl])
            tmr = v(lambda e, r0=r0, r1=r1, r2=r2: e.tensor_mul(R(r2), R(r0), R(r1)),
                    waits=[tr_])
            b0, wb0 = self.rbs.acquire()
            b1, wb1 = self.rbs.acquire()
            RB = lambda i: self.rowb[0:1, i * TT:(i + 1) * TT]
            tbr = v(lambda e, b0=b0, r1=r1: e.tensor_copy(RB(b0), R(r1)), waits=[tr_] + wb0)
            tbm = v(lambda e, b1=b1, r2=r2: e.tensor_copy(RB(b1), R(r2)), waits=[tmr] + wb1)
            self.rfs.done(r0, tmr)
            self.rfs.done(r1, tbr)
            self.rfs.done(r2, tbm)
            rrb, rmb = self.pb[6], self.pb[7]
            prr, prm = rrb.ap(), rmb.ap()
            t_rr = mm(rrb, prr[:, 0:TT], self.ones[0:1, 0:128], RB(b0), stop=True,
                      waits=[tbr], inc=("PE", 1))
            t_rm = mm(rmb, prm[:, 0:TT], self.ones[0:1, 0:128], RB(b1), stop=True,
                      waits=[tbm], inc=("PE", 1))
            self.rbs.done(b0, t_rr)
            self.rbs.done(b1, t_rm)
            for ch in range(NCH):
                f0, wf0 = self.ffs.acquire()
                t1 = v(lambda e, ch=ch, f=f0, prr=prr: e.tensor_mul(
                    self.tmpf[:, f * TT:(f + 1) * TT],
                    self.slab4[:, ch * 512:ch * 512 + TT], prr[:, 0:TT]),
                    waits=[t_rr] + wf0)
                y_done = v(lambda e, ch=ch, f=f0, tt=tt, prm=prm: e.tensor_sub(
                    self.slab5[:, ch * TOK + tt * TT: ch * TOK + (tt + 1) * TT],
                    self.tmpf[:, f * TT:(f + 1) * TT], prm[:, 0:TT]), waits=[t_rm, t1])
                self.ffs.done(f0, y_done)
            rrb.consumed(y_done)
            rmb.consumed(y_done)
        self.y_done = y_done
        self.stage_free = [y_done]

    def phaseB(self):
        em, mm, v, a, p = self.em, self.mm, self.v, self.a, self.p
        bar0 = self.bar()
        self.xp = {k: list(bar0) for k in
                   ("rowpair", "g1g1b1", "g1b1row", "g2b2row", "grep", "sb1s",
                    "m1f", "m1on", "uc", "zt", "m2acat")}
        self.xp["w1s"] = []
        self.m2a_last = []
        prev_out = None
        for hh in range(NTH):
            tb0 = hh * TH
            self.t_m1on = v(lambda e: e.memset(self.kv_bf[0:2, 0:TH], 1.0),
                            waits=([self.am_done, self.em.tok("PE")] if hh == 0 else
                                   [prev_out] + self.bar()) + self.xp["m1on"])
            self.xp["m1on"] = [self.t_m1on]
            tgate = None
            for j in range(TH // 128):
                jb = tb0 + j * 128
                bank = self.pb[j % 2]
                ps = bank.ap()
                for ci in range(8):
                    w = [self.y_done, self.t_casts, self.t_m1on] if (j == 0 and ci == 0) else []
                    if hh > 0 and j == 0 and ci == 0:
                        w = [prev_out]
                    mm(bank, ps[:, 0:4],
                       self.slab5[:, ci * TOK + jb: ci * TOK + jb + 128],
                       self.gatec[:, ci * 4:(ci + 1) * 4], stop=False, waits=w)
                tmm = mm(bank, ps[:, 0:4], self.ones[0:1, 0:128], self.gateb[0:1, 0:4],
                         stop=True, inc=("PE", 1))
                f0, wf0 = self.ffs.acquire()
                Ft = self.tmpf[:, f0 * TT:(f0 + 1) * TT]
                tM1 = v(lambda e, ps=ps, Ft=Ft: e.tensor_reduce(
                    Ft[:, 0:1], ps[:, 0:4], AX.X, ALU.max, negate=True), waits=[tmm] + wf0)
                tmsk = v(lambda e, ps=ps, Ft=Ft: e.tensor_scalar(
                    Ft[:, 1:5], ps[:, 0:4], Ft[:, 0:1], 0.0, ALU.add, ALU.is_ge), waits=[tM1])
                ts2 = v(lambda e, ps=ps, Ft=Ft: e.scalar_tensor_tensor(
                    Ft[:, 5:9], Ft[:, 1:5], -1e9, ps[:, 0:4], ALU.mult, ALU.add), waits=[tmsk])
                tM2 = v(lambda e, Ft=Ft: e.tensor_reduce(
                    Ft[:, 9:10], Ft[:, 5:9], AX.X, ALU.max, negate=True), waits=[ts2])
                tsel = v(lambda e, ps=ps, Ft=Ft: e.tensor_scalar(
                    Ft[:, 14:18], ps[:, 0:4], Ft[:, 9:10], 0.0, ALU.add, ALU.is_ge),
                    waits=[tM2])
                tE = a(lambda e, ps=ps, Ft=Ft: e.activation(
                    Ft[:, 10:14], ps[:, 0:4], AF.Exp, bias=Ft[:, 0:1]), waits=[tsel])
                bank.consumed(tsel)
                bank.consumed(tE)
                tEs = v(lambda e, Ft=Ft: e.tensor_mul(Ft[:, 10:14], Ft[:, 10:14],
                                                      Ft[:, 14:18]), waits=[tE, tsel])
                tZ = v(lambda e, Ft=Ft: e.tensor_reduce(Ft[:, 18:19], Ft[:, 10:14], AX.X,
                                                        ALU.add), waits=[tEs])
                tZi = v(lambda e, Ft=Ft: e.reciprocal(Ft[:, 19:20], Ft[:, 18:19]),
                        waits=[tZ])
                tw = v(lambda e, Ft=Ft: e.tensor_scalar(
                    self.wtm[:, 0:4], Ft[:, 10:14], Ft[:, 19:20], None, ALU.mult),
                    waits=[tZi, getattr(self, "wtm_free", None)])
                self.ffs.done(f0, tw)
                wb2 = self.pb[2]
                pw = wb2.ap()
                tt_ = None
                for e_ in range(NE):
                    tt_ = mm(wb2, pw[0:1, e_ * 128:(e_ + 1) * 128], self.wtm[:, e_:e_ + 1],
                             self.idnb[:], stop=(e_ == NE - 1), waits=[tw],
                             inc=("PE", 1) if e_ == NE - 1 else None)
                self.wtm_free = tt_
                for e_ in range(NE):
                    tgate = v(lambda e, e_=e_, j=j, pw=pw: e.tensor_copy(
                        self.wcat[0:1, e_ * TH + j * 128: e_ * TH + (j + 1) * 128],
                        pw[0:1, e_ * 128:(e_ + 1) * 128]), waits=[tt_])
                wb2.consumed(tgate)
            self.t_gate = tgate
            t_acc = None
            for ch in range(NCH):
                for q in range(NTHT):
                    w = []
                    if ch == 0 and q == 0:
                        w = [self.t_gate]
                    t_acc = p(lambda e, ch=ch, q=q, tb=tb0: e.tensor_scalar(
                        self.slab3[:, ch * TH + q * TT: ch * TH + (q + 1) * TT],
                        self.slab5[:, ch * TOK + tb + q * TT: ch * TOK + tb + (q + 1) * TT],
                        2.0, None, ALU.mult), waits=w)
            acc_tok = t_acc
            wslot = [0]
            prev_w_pe = [None, None]
            for e_ in range(NE):
                acc_tok = self.expert(e_, hh, wslot, prev_w_pe, acc_tok)
            prev_out = self.final_ln(hh, acc_tok)
        self.out_done = prev_out

    def expert(self, e_, hh, wslot, prev_w_pe, acc_tok):
        em, mm, v, a, p = self.em, self.mm, self.v, self.a, self.p
        tb0 = hh * TH
        ws = wslot[0]
        wslot[0] ^= 1
        xp = self.xp
        wdst, woff = (self.slab3, 8192) if ws == 0 else (self.slab1, 0)
        WL = lambda ci, ch: wdst[:, woff + ci * 1024 + ch * 128: woff + ci * 1024 + ch * 128 + 128]

        RP0 = self.rowpair[0:1, 0:C]
        RP1 = self.rowpair[0:1, C:2 * C]
        self.dma("sync", RP0, self.d_b1cat[0:1, e_ * C:(e_ + 1) * C], sem="LR",
                 waits=xp["rowpair"])
        t_p2 = self.dma("sync", RP1, self.d_g1cat[0:1, e_ * C:(e_ + 1) * C], sem="LR")
        tg1 = v(lambda e: e.tensor_copy(self.g1g1b1[0:1, :], RP1),
                waits=[t_p2] + xp["g1g1b1"])
        tgb = v(lambda e: e.tensor_tensor(self.g1b1row[0:1, :], RP1, RP0, ALU.mult),
                waits=[t_p2] + xp["g1b1row"])
        t_stk = self.dma("gpsimd", self.g1g1b1[1:2, 0:C], self.g1b1row[0:1, 0:C],
                         sem="SG", waits=[tgb] + xp["g1g1b1"])
        tsb1 = v(lambda e: e.tensor_reduce(self.sb1s[0:1, 0:1], RP0, AX.X, ALU.add),
                 waits=[t_p2] + xp["sb1s"])
        tsb1b = v(lambda e: e.tensor_scalar(self.sb1s[0:1, 1:2], self.sb1s[0:1, 0:1],
                                            1.0 / 1024.0, None, ALU.mult), waits=[tsb1])
        gb = self.pb[6]
        pg = gb.ap()
        tr1 = mm(gb, pg[:, 0:512], self.ones[0:1, 0:128], self.g1g1b1[0:1, 0:512],
                 stop=True, waits=[tg1], inc=("PE", 1))
        te1 = a(lambda e: e.copy(self.kvf[:, 0:512], pg[:, 0:512]),
                waits=[tr1] + xp["grep"])
        gb.consumed(te1)
        tr2 = mm(gb, pg[:, 0:512], self.ones[0:1, 0:128], self.g1g1b1[0:1, 512:1024],
                 stop=True, waits=[tg1], inc=("PE", 1))
        te2 = a(lambda e: e.copy(self.kvf[:, 512:1024], pg[:, 0:512]), waits=[tr2])
        gb.consumed(te2)
        self.w1s_free = xp["w1s"][0] if xp["w1s"] else None
        t_w1g = self.load_weight(self.d_w1[e_], wdst, woff, scale=self.kvf[:, 0:1024],
                                 w1s=True, extra_wait=prev_w_pe[ws], cast_waits=[te1, te2])
        t_w1s = self.t_w1s
        self.dma("sync", RP0, self.d_g2cat[0:1, e_ * C:(e_ + 1) * C], sem="LR",
                 waits=[tgb, tsb1, tg1])
        t_p4 = self.dma("sync", RP1, self.d_b2cat[0:1, e_ * C:(e_ + 1) * C], sem="LR")
        tg2b2 = v(lambda e: e.tensor_tensor(self.g2b2row[0:1, :], RP0, RP1, ALU.mult),
                  waits=[t_p4] + xp["g2b2row"])
        tg2r = v(lambda e: e.tensor_copy(self.g1b1row[0:1, :], RP0),
                 waits=[t_stk, t_p4])
        gb2 = self.pb[6]
        pg2 = gb2.ap()
        tw21 = mm(gb2, pg2[:, 0:512], self.ones[0:1, 0:128], self.g1b1row[0:1, 0:512],
                  stop=True, waits=[tg2r], inc=("PE", 1))
        tge1 = a(lambda e: e.copy(self.kvf[:, 0:512], pg2[:, 0:512]),
                 waits=[tw21, t_w1g])
        gb2.consumed(tge1)
        tw22 = mm(gb2, pg2[:, 0:512], self.ones[0:1, 0:128], self.g1b1row[0:1, 512:1024],
                  stop=True, waits=[tg2r], inc=("PE", 1))
        tge2 = a(lambda e: e.copy(self.kvf[:, 512:1024], pg2[:, 0:512]), waits=[tw22])
        gb2.consumed(tge2)
        ows = ws ^ 1
        owdst, owoff = (self.slab3, 8192) if ows == 0 else (self.slab1, 0)
        t_w2g = self.load_weight(self.d_w2[e_], owdst, owoff, scale=self.kvf[:, 0:1024],
                                 extra_wait=prev_w_pe[ows], cast_waits=[tge1, tge2])
        W2L = lambda ci, ch: owdst[:, owoff + ci * 1024 + ch * 128: owoff + ci * 1024 + ch * 128 + 128]
        xp["rowpair"] = [tg2r, tg2b2]
        xp["grep"] = [t_w2g]
        xp["g1b1row"] = [tw22]
        xp["sb1s"] = []

        m1toks = []
        sB = self.pb[4]
        tm1s = []
        for q in range(NTHT):
            psS = sB.ap()
            for ci in range(8):
                w = [t_w1s, self.y_done] if ci == 0 else []
                mm(sB, psS[0:1, 0:TT], self.w1s[:, ci:ci + 1],
                   self.slab5[:, ci * TOK + tb0 + q * TT: ci * TOK + tb0 + (q + 1) * TT],
                   stop=(ci == 7), waits=w, inc=("PE", 1) if ci == 7 else None)
            tm1 = a(lambda e, q=q, psS=psS: e.activation(
                self.m1f[0:1, q * TT:(q + 1) * TT], psS[0:1, 0:TT], AF.Identity,
                scale=1.0 / 1024.0, bias=self.sb1s[0:1, 1:2]),
                waits=[em.tok("PE"), tsb1b] + xp["m1f"])
            tneg = a(lambda e, q=q: e.activation(
                self.kv_bf[0:1, q * TT:(q + 1) * TT], self.m1f[0:1, q * TT:(q + 1) * TT],
                AF.Copy, scale=-1.0), waits=[tm1] + xp["m1on"])
            sB.consumed(tm1)
            m1toks.append(tneg)
            tm1s.append(tm1)
        self.w1s_next = em.tok("PE")
        xp["w1s"] = [self.w1s_next]
        uc_toks = {}
        sq_toks = {}
        q2b = [self.pb[4], self.pb[5]]
        units = [(ch, q) for ch in range(NCH) for q in range(NTHT)]
        done_per_q = {q: 0 for q in range(NTHT)}

        def u_stat(idx):
            ch, q = units[idx]
            s1, tsq = sq_toks[(ch, q)]
            tqm = mm(q2b[q], q2b[q].ap()[0:1, 0:TT],
                     self.ivg1sq[:, e_ * 8 + ch:e_ * 8 + ch + 1],
                     self.tmpbf[:, s1 * TT:(s1 + 1) * TT],
                     stop=(done_per_q[q] == NCH - 1), waits=[tsq], inc=("PE", 1))
            done_per_q[q] += 1
            self.bfs.done(s1, tqm)

        for idx, (ch, q) in enumerate(units):
            bank = self.pb[idx % 4]
            ps = bank.ap()
            for ci in range(8):
                w = [t_w1g] if idx == 0 and ci == 0 else []
                mm(bank, ps[:, 0:TT], WL(ci, ch),
                   self.slab5[:, ci * TOK + tb0 + q * TT: ci * TOK + tb0 + (q + 1) * TT],
                   stop=False, waits=w)
            tmm = mm(bank, ps[:, 0:TT], self.g1g1b1[0:2, ch * 128:(ch + 1) * 128],
                     self.kv_bf[0:2, q * TT:(q + 1) * TT], stop=True,
                     waits=[m1toks[q], t_stk], inc=("PE", 1))
            ucol = 8192 + ch * TH + q * TT
            tuc = a(lambda e, ps=ps, d=ucol: e.copy(self.slab1[:, d:d + TT], ps[:, 0:TT]),
                    waits=[tmm] + (xp["uc"] if idx == 0 else []))
            bank.consumed(tuc)
            s1, w1_ = self.bfs.acquire()
            tsq = v(lambda e, d=ucol, s=s1: e.tensor_mul(
                self.tmpbf[:, s * TT:(s + 1) * TT], self.slab1[:, d:d + TT],
                self.slab1[:, d:d + TT]), waits=[tuc] + w1_)
            uc_toks[(ch, q)] = tuc
            sq_toks[(ch, q)] = (s1, tsq)
            if idx >= 5:
                u_stat(idx - 5)
        t_umm = em.tok("PE")
        for idx in range(len(units) - 5, len(units)):
            u_stat(idx)
        t_uq = em.tok("PE")
        h_toks = {}
        for q in range(NTHT):
            r0, wr0 = self.rfs.acquire()
            r1_, wr1 = self.rfs.acquire()
            R0 = self.rowf[0:1, r0 * TT:(r0 + 1) * TT]
            R1 = self.rowf[0:1, r1_ * TT:(r1_ + 1) * TT]
            tE2 = v(lambda e, q=q, R0=R0: e.tensor_scalar(
                R0, q2b[q].ap()[0:1, 0:TT], 1.0 / 1024.0, None, ALU.mult),
                waits=[t_uq] + wr0)
            q2b[q].consumed(tE2)
            tm2 = v(lambda e, q=q, R1=R1: e.tensor_mul(
                R1, self.m1f[0:1, q * TT:(q + 1) * TT], self.m1f[0:1, q * TT:(q + 1) * TT]),
                waits=wr1 + [m1toks[q]])
            tvv = v(lambda e, R0=R0, R1=R1: e.tensor_sub(R0, R0, R1), waits=[tE2, tm2])
            tl = a(lambda e, R0=R0: e.activation(R0, R0, AF.Ln, bias=EPS), waits=[tvv])
            tr_ = a(lambda e, R0=R0: e.activation(R0, R0, AF.Exp, scale=-0.5), waits=[tl])
            b0, wb0 = self.rbs.acquire()
            RB0 = self.rowb[0:1, b0 * TT:(b0 + 1) * TT]
            tbr = v(lambda e, RB0=RB0, R0=R0: e.tensor_copy(RB0, R0), waits=[tr_] + wb0)
            self.rfs.done(r0, tbr)
            self.rfs.done(r1_, tvv)
            if q == NTHT - 1:
                xp["m1f"] = [tm2]
            repb = q2b[q]
            prp = repb.ap()
            t_rep = mm(repb, prp[:, 0:TT], self.ones[0:1, 0:128], RB0, stop=True,
                       waits=[tbr], inc=("PE", 1))
            self.rbs.done(b0, t_rep)
            lasth = None
            for ch in range(NCH):
                ucol = 8192 + ch * TH + q * TT
                s1, w1_ = self.bfs.acquire()
                t3 = v(lambda e, d=ucol, s=s1, prp=prp: e.tensor_mul(
                    self.tmpbf[:, s * TT:(s + 1) * TT], self.slab1[:, d:d + TT],
                    prp[:, 0:TT]), waits=[t_rep, uc_toks[(ch, q)]] + w1_)
                th = a(lambda e, s=s1, ch=ch, q=q: e.activation(
                    self.slab2[:, ch * TH + q * TT: ch * TH + (q + 1) * TT],
                    self.tmpbf[:, s * TT:(s + 1) * TT], AF.Relu,
                    bias=self.be1c[:, e_ * 8 + ch:e_ * 8 + ch + 1]), waits=[t3])
                self.bfs.done(s1, th)
                h_toks[(ch, q)] = th
                lasth = th
            repb.consumed(lasth)
        xp["uc"] = [lasth]
        xp["m1on"] = [t_umm]

        zt_toks = {}
        zq_toks = {}
        zsb = {0: (self.pb[4], self.pb[5]), 1: (self.pb[6], self.pb[7])}
        zdone = {q: 0 for q in range(NTHT)}

        def z_stat(idx):
            ch, q = units[idx]
            s1, tsq = zq_toks[(ch, q)]
            sumb, sqb = zsb[q]
            last = zdone[q] == NCH - 1
            mm(sumb, sumb.ap()[0:1, 0:TT], self.ivg2[:, e_ * 8 + ch:e_ * 8 + ch + 1],
               self.slab2[:, 8192 + ch * TH + q * TT: 8192 + ch * TH + q * TT + TT],
               stop=last, waits=[zt_toks[(ch, q)]], inc=("PE", 1))
            tqm = mm(sqb, sqb.ap()[0:1, 0:TT],
                     self.ivg2sq[:, e_ * 8 + ch:e_ * 8 + ch + 1],
                     self.tmpbf[:, s1 * TT:(s1 + 1) * TT], stop=last,
                     waits=[tsq], inc=("PE", 1))
            zdone[q] += 1
            self.bfs.done(s1, tqm)

        for idx, (ch, q) in enumerate(units):
            bank = self.pb[idx % 4]
            ps = bank.ap()
            for ci in range(8):
                w = [h_toks[(ci, q)]]
                if idx == 0 and ci == 0:
                    w.append(t_w2g)
                mm(bank, ps[:, 0:TT], W2L(ci, ch),
                   self.slab2[:, ci * TH + q * TT: ci * TH + (q + 1) * TT],
                   stop=False, waits=w)
            tmm = mm(bank, ps[:, 0:TT], self.g2b2row[0:1, ch * 128:(ch + 1) * 128],
                     self.ones[0:1, 0:TT], stop=True, waits=[tg2b2], inc=("PE", 1))
            zcol = 8192 + ch * TH + q * TT
            tze = a(lambda e, ps=ps, d=zcol: e.copy(self.slab2[:, d:d + TT], ps[:, 0:TT]),
                    waits=[tmm] + (xp["zt"] if idx == 0 else []))
            bank.consumed(tze)
            s1, w1_ = self.bfs.acquire()
            tsq = v(lambda e, d=zcol, s=s1: e.tensor_mul(
                self.tmpbf[:, s * TT:(s + 1) * TT], self.slab2[:, d:d + TT],
                self.slab2[:, d:d + TT]), waits=[tze] + w1_)
            zt_toks[(ch, q)] = tze
            zq_toks[(ch, q)] = (s1, tsq)
            if idx >= 5:
                z_stat(idx - 5)
        t_zmm = em.tok("PE")
        for idx in range(len(units) - 5, len(units)):
            z_stat(idx)
        t_zq = em.tok("PE")
        tacc = acc_tok
        m2a_toks = []
        for q in range(NTHT):
            sumb, sqb = zsb[q]
            r0, wr0 = self.rfs.acquire()
            r1_, wr1 = self.rfs.acquire()
            r2_, wr2 = self.rfs.acquire()
            R0 = self.rowf[0:1, r0 * TT:(r0 + 1) * TT]
            R1 = self.rowf[0:1, r1_ * TT:(r1_ + 1) * TT]
            R2 = self.rowf[0:1, r2_ * TT:(r2_ + 1) * TT]
            tm2_ = v(lambda e, R0=R0, sumb=sumb: e.tensor_scalar(
                R0, sumb.ap()[0:1, 0:TT], 1.0 / 1024.0, None, ALU.mult),
                waits=[t_zq] + wr0)
            tE2 = v(lambda e, R1=R1, sqb=sqb: e.tensor_scalar(
                R1, sqb.ap()[0:1, 0:TT], 1.0 / 1024.0, None, ALU.mult),
                waits=[t_zq] + wr1)
            sumb.consumed(tm2_)
            sqb.consumed(tE2)
            tsq2 = v(lambda e, R0=R0, R2=R2: e.tensor_mul(R2, R0, R0), waits=[tm2_] + wr2)
            tvv = v(lambda e, R1=R1, R2=R2: e.tensor_sub(R1, R1, R2), waits=[tE2, tsq2])
            tl = a(lambda e, R1=R1: e.activation(R1, R1, AF.Ln, bias=EPS), waits=[tvv])
            tr2_ = a(lambda e, R1=R1: e.activation(R1, R1, AF.Exp, scale=-0.5), waits=[tl])
            tA = v(lambda e, R1=R1, R2=R2, q=q: e.tensor_tensor(
                R2, self.wcat[0:1, e_ * TH + q * TT: e_ * TH + (q + 1) * TT],
                R1, ALU.mult), waits=[tr2_, self.t_gate])
            b0, wb0 = self.rbs.acquire()
            RB0 = self.rowb[0:1, b0 * TT:(b0 + 1) * TT]
            tAb = v(lambda e, RB0=RB0, R2=R2: e.tensor_copy(RB0, R2), waits=[tA] + wb0)
            tma = v(lambda e, R0=R0, R2=R2: e.tensor_mul(R0, R0, R2), waits=[tA, tsq2])
            tmab = v(lambda e, R0=R0, q=q: e.tensor_scalar(
                self.m2acat[0:1, e_ * TH + q * TT: e_ * TH + (q + 1) * TT], R0,
                -1.0, None, ALU.mult), waits=[tma] + xp["m2acat"])
            m2a_toks.append(tmab)
            self.rfs.done(r0, tmab)
            self.rfs.done(r1_, tA)
            self.rfs.done(r2_, tma)
            repb = sumb
            prp = repb.ap()
            t_rep = mm(repb, prp[:, 0:TT], self.ones[0:1, 0:128], RB0, stop=True,
                       waits=[tAb], inc=("PE", 1))
            self.rbs.done(b0, t_rep)
            lastc = None
            for ch in range(NCH):
                zcol = 8192 + ch * TH + q * TT
                s1, w1_ = self.bfs.acquire()
                tmul = v(lambda e, d=zcol, s=s1, prp=prp: e.tensor_mul(
                    self.tmpbf[:, s * TT:(s + 1) * TT], self.slab2[:, d:d + TT],
                    prp[:, 0:TT]), waits=[t_rep, zt_toks[(ch, q)]] + w1_)
                tadd = p(lambda e, ch=ch, q=q, s=s1: e.tensor_tensor(
                    self.slab3[:, ch * TH + q * TT: ch * TH + (q + 1) * TT],
                    self.slab3[:, ch * TH + q * TT: ch * TH + (q + 1) * TT],
                    self.tmpbf[:, s * TT:(s + 1) * TT], ALU.add),
                    waits=[tmul, acc_tok])
                self.bfs.done(s1, tadd)
                lastc = tadd
            repb.consumed(lastc)
            tacc = lastc
        xp["zt"] = [lastc]
        self.m2a_last = m2a_toks
        xp["g2b2row"] = [t_zmm]
        xp["g1g1b1"] = [t_umm, tw22]
        prev_w_pe[ws] = t_umm
        self.last_umm = t_umm
        prev_w_pe[ows] = t_zmm
        return tacc

    def final_ln(self, hh, acc_tok):
        em, mm, v, a = self.em, self.mm, self.v, self.a
        tb0 = hh * TH
        last_out = None
        rko = {0: 0, 1: 520}
        w0 = [getattr(self, "prev_rnk", None), self.t_gate,
              getattr(self, "last_umm", None)] + self.m2a_last
        tstk = None
        for q in range(NTHT):
            for e_ in range(NE):
                tstk = self.dma("gpsimd",
                                self.kv_bf[2 * e_:2 * e_ + 1, rko[q]:rko[q] + TT],
                                self.m2acat[0:1, e_ * TH + q * TT: e_ * TH + (q + 1) * TT],
                                sem="SG", waits=w0 if tstk is None else [])
                tstk = self.dma("gpsimd",
                                self.kv_bf[2 * e_ + 1:2 * e_ + 2, rko[q]:rko[q] + TT],
                                self.wcat[0:1, e_ * TH + q * TT: e_ * TH + (q + 1) * TT],
                                sem="SG")
        self.xp["m2acat"] = [tstk]
        units = [(q, ch) for q in range(NTHT) for ch in range(NCH)]
        sf_toks = {}
        sq_toks = {}
        zsb = {0: (self.pb[4], self.pb[5]), 1: (self.pb[6], self.pb[7])}
        fdone = {q: 0 for q in range(NTHT)}

        def fstat(idx):
            q, ch = units[idx]
            s1, tsq = sq_toks[(q, ch)]
            sumb, sqb = zsb[q]
            lastf = fdone[q] == NCH - 1
            mm(sumb, sumb.ap()[0:1, 0:TT], self.ones[:, 0:1],
               self.slab2[:, ch * TH + q * TT: ch * TH + (q + 1) * TT],
               stop=lastf, waits=[sf_toks[(q, ch)]], inc=("PE", 1))
            tqm = mm(sqb, sqb.ap()[0:1, 0:TT], self.ones[:, 0:1],
                     self.tmpbf[:, s1 * TT:(s1 + 1) * TT], stop=lastf,
                     waits=[tsq], inc=("PE", 1))
            fdone[q] += 1
            self.bfs.done(s1, tqm)

        for idx, (q, ch) in enumerate(units):
            bank = self.pb[idx % 4]
            ps = bank.ap()
            trk = mm(bank, ps[:, 0:TT], self.rklb[0:8, ch * 128:(ch + 1) * 128],
                     self.kv_bf[0:8, rko[q]:rko[q] + TT], stop=True,
                     waits=[tstk, self.t_casts], inc=("PE", 1))
            tsf = v(lambda e, ps=ps, ch=ch, q=q: e.tensor_add(
                self.slab2[:, ch * TH + q * TT: ch * TH + (q + 1) * TT], ps[:, 0:TT],
                self.slab3[:, ch * TH + q * TT: ch * TH + (q + 1) * TT]),
                waits=[trk, acc_tok])
            sf_toks[(q, ch)] = tsf
            bank.consumed(tsf)
            s1, w1_ = self.bfs.acquire()
            tsq = v(lambda e, ch=ch, q=q, s=s1: e.tensor_mul(
                self.tmpbf[:, s * TT:(s + 1) * TT],
                self.slab2[:, ch * TH + q * TT: ch * TH + (q + 1) * TT],
                self.slab2[:, ch * TH + q * TT: ch * TH + (q + 1) * TT]),
                waits=[tsf] + w1_)
            sq_toks[(q, ch)] = (s1, tsq)
            if idx >= 5:
                fstat(idx - 5)
        for idx in range(len(units) - 5, len(units)):
            fstat(idx)
        self.prev_rnk = em.tok("PE")
        tstat = em.tok("PE")
        repbanks = {0: (self.pb[4], self.pb[5]), 1: (self.pb[6], self.pb[7])}
        for q in range(NTHT):
            sumb, sqb = zsb[q]
            r0, wr0 = self.rfs.acquire()
            r1_, wr1 = self.rfs.acquire()
            r2_, wr2 = self.rfs.acquire()
            R0 = self.rowf[0:1, r0 * TT:(r0 + 1) * TT]
            R1 = self.rowf[0:1, r1_ * TT:(r1_ + 1) * TT]
            R2 = self.rowf[0:1, r2_ * TT:(r2_ + 1) * TT]
            tm_ = v(lambda e, R0=R0, sumb=sumb: e.tensor_scalar(
                R0, sumb.ap()[0:1, 0:TT], 1.0 / 1024.0, None, ALU.mult),
                waits=[tstat] + wr0)
            tv_ = v(lambda e, R1=R1, sqb=sqb: e.tensor_scalar(
                R1, sqb.ap()[0:1, 0:TT], 1.0 / 1024.0, None, ALU.mult),
                waits=[tstat] + wr1)
            sumb.consumed(tm_)
            sqb.consumed(tv_)
            tm2 = v(lambda e, R0=R0, R2=R2: e.tensor_mul(R2, R0, R0), waits=[tm_] + wr2)
            tvv = v(lambda e, R1=R1, R2=R2: e.tensor_sub(R1, R1, R2), waits=[tv_, tm2])
            tl = a(lambda e, R1=R1: e.activation(R1, R1, AF.Ln, bias=EPS), waits=[tvv])
            tr_ = a(lambda e, R1=R1: e.activation(R1, R1, AF.Exp, scale=-0.5), waits=[tl])
            tmr = v(lambda e, R0=R0, R1=R1, R2=R2: e.tensor_mul(R2, R0, R1), waits=[tr_])
            b0, wb0 = self.rbs.acquire()
            b1, wb1 = self.rbs.acquire()
            RB0 = self.rowb[0:1, b0 * TT:(b0 + 1) * TT]
            RB1 = self.rowb[0:1, b1 * TT:(b1 + 1) * TT]
            tbr = v(lambda e, RB0=RB0, R1=R1: e.tensor_copy(RB0, R1), waits=[tr_] + wb0)
            tbm = v(lambda e, RB1=RB1, R2=R2: e.tensor_copy(RB1, R2), waits=[tmr] + wb1)
            self.rfs.done(r0, tmr)
            self.rfs.done(r1_, tbr)
            self.rfs.done(r2_, tbm)
            rrb, rmb = repbanks[q]
            prr, prm = rrb.ap(), rmb.ap()
            t_rr = mm(rrb, prr[:, 0:TT], self.ones[0:1, 0:128], RB0, stop=True,
                      waits=[tbr], inc=("PE", 1))
            t_rm = mm(rmb, prm[:, 0:TT], self.ones[0:1, 0:128], RB1, stop=True,
                      waits=[tbm], inc=("PE", 1))
            self.rbs.done(b0, t_rr)
            self.rbs.done(b1, t_rm)
            for ch in range(NCH):
                f0, wf0 = self.ffs.acquire()
                f1, wf1 = self.ffs.acquire()
                t1 = v(lambda e, ch=ch, q=q, f=f0, prr=prr: e.tensor_mul(
                    self.tmpf[:, f * TT:(f + 1) * TT],
                    self.slab2[:, ch * TH + q * TT: ch * TH + (q + 1) * TT], prr[:, 0:TT]),
                    waits=[t_rr] + wf0)
                to = v(lambda e, f0=f0, f1=f1, prm=prm: e.tensor_sub(
                    self.tmpf[:, f1 * TT:(f1 + 1) * TT], self.tmpf[:, f0 * TT:(f0 + 1) * TT],
                    prm[:, 0:TT]), waits=[t_rm, t1] + wf1)
                self.ffs.done(f0, to)
                tdo = self.dma("gpsimd", self.d_out[ch * 128:(ch + 1) * 128,
                                                    tb0 + q * TT: tb0 + (q + 1) * TT],
                               self.tmpf[:, f1 * TT:(f1 + 1) * TT], sem="ST", waits=[to])
                self.ffs.done(f1, tdo)
                last_out = tdo
            rrb.consumed(last_out)
            rmb.consumed(last_out)
        return last_out

    def finish(self):
        em = self.em
        if self.dbg is not None:
            slab = {"k": self.slab2, "q": self.slab2, "v": self.slab3,
                    "attnmix": self.slab3, "y": self.slab5}[self.dbg]
            last = {"k": getattr(self, "tk_done", None), "q": getattr(self, "q_done", None),
                    "v": getattr(self, "tv_done", None),
                    "attnmix": getattr(self, "am_done", None),
                    "y": getattr(self, "y_done", None)}[self.dbg]
            tdma = None
            for ch in range(8):
                for w_ in range(TOK // TT):
                    sl = w_ % 2
                    wl = [last] + ([tdma] if tdma else [])
                    tc = self.v(lambda e, ch=ch, w_=w_, s=sl: e.tensor_copy(
                        self.tmpf[:, s * TT:(s + 1) * TT],
                        slab[:, ch * TOK + w_ * TT: ch * TOK + (w_ + 1) * TT]), waits=wl)
                    tdma = self.dma("gpsimd",
                                    self.d_out[ch * 128:(ch + 1) * 128,
                                               w_ * TT:(w_ + 1) * TT],
                                    self.tmpf[:, sl * TT:(sl + 1) * TT], sem="ST",
                                    waits=[tc])
        em.op("gpsimd", lambda e: e.engine_nop(),
              waits=[em.tok("SG"), em.tok("LD"), em.tok("LR"), em.tok("ST"),
                     em.tok("CC")])
        handles = {}
        cms = []
        for name in ("LD", "LR", "PE", "DV", "AC", "PL", "SG", "ST", "CC"):
            cm = self.nc.semaphore(name)
            handles[name] = cm.__enter__()
            cms.append(cm)
        blk = self.nc.Block()
        block = blk.__enter__()
        self.em.replay(block, handles)
        blk.__exit__(None, None, None)
        for cm in reversed(cms):
            cm.__exit__(None, None, None)
        self.ctx.close()


def build(phase_limit=99, dbg=None):
    return MK(dbg=dbg, phase_limit=phase_limit).build()


def prep_inputs(core, inp, tok=TOK):
    asc = np.ascontiguousarray
    f = lambda a: np.asarray(a, dtype=np.float32)
    xf = f(inp["L0_attn_x"]).reshape(-1, C)
    xs = xf[core * tok:(core + 1) * tok]
    cols = lambda b: asc(f(b).reshape(8, 128).T)
    rows = lambda b: f(b).reshape(1, -1)
    catr = lambda key: asc(np.concatenate([f(inp[key][e]).reshape(1, C) for e in range(NE)], axis=1))
    colse = lambda key: asc(np.stack([f(inp[key][e]).reshape(8, 128) for e in range(NE)])
                            .transpose(2, 0, 1).reshape(128, 32))
    def colse2(key):
        v = np.stack([f(inp[key][e]).reshape(8, 128) for e in range(NE)])
        out = np.zeros((128, 32), np.float32)
        for e in range(NE):
            for ch in range(8):
                out[:, e * 8 + ch] = v[e, ch]
        return asc(out)
    gatec = asc(f(inp["L0_ffn_gate_W"]).reshape(8, 128, 4).transpose(1, 0, 2).reshape(128, 32))
    rkl = np.zeros((8, C), np.float32)
    for e in range(NE):
        rkl[2 * e] = f(inp["L0_ffn_g2"][e])
        rkl[2 * e + 1] = f(inp["L0_ffn_be2"][e])
    return dict(
        xT=asc(xs.T),
        wqT=asc(f(inp["L0_attn_Wq"]).T), wkT=asc(f(inp["L0_attn_Wk"]).T),
        wvT=asc(f(inp["L0_attn_Wv"]).T), woT=asc(f(inp["L0_attn_Wo"]).T),
        w1=asc(f(inp["L0_ffn_W1"])), w2=asc(f(inp["L0_ffn_W2"])),
        gatec=gatec, gateb=f(inp["L0_ffn_gate_b"]).reshape(1, 4),
        bq_c=cols(inp["L0_attn_bq"]), bo_c=cols(inp["L0_attn_bo"]),
        bk_r=rows(inp["L0_attn_bk"]), bv_r=rows(inp["L0_attn_bv"]),
        b1cat=catr("L0_ffn_b1"), g1cat=catr("L0_ffn_g1"),
        g2cat=catr("L0_ffn_g2"), b2cat=catr("L0_ffn_b2"),
        g1c=colse2("L0_ffn_g1"), be1c=colse2("L0_ffn_be1"), g2c=colse2("L0_ffn_g2"),
        rkl=asc(rkl),
        idn=np.eye(128, dtype=np.float32),
        cvec=np.tile(np.array([[EPS, -0.5, 1.0 / 1024.0, -1.0]], np.float32), (128, 1)),
    )


from concourse.bass_utils import run_bass_kernel_spmd


def kernel(**inputs):
    configure(tok=2048, th=1024, rg=[[0, 1, 2, 3], [4, 5, 6, 7]])
    nc = build(phase_limit=99)
    in_maps = [prep_inputs(c, inputs, tok=2048) for c in range(8)]
    res = run_bass_kernel_spmd(nc, in_maps, core_ids=list(range(8)))
    outs = [np.asarray(res.results[c]["outT"]) for c in range(8)]
    full = np.concatenate([o.T for o in outs], axis=0)
    return np.ascontiguousarray(full.reshape(2, 8192, 1024)).astype(np.float32)


# version 12
# speedup vs baseline: 1.0634x; 1.0634x over previous
import numpy as np
import concourse.bass as bass
import concourse.mybir as mybir

F32 = mybir.dt.float32
BF16 = mybir.dt.bfloat16
AF = mybir.ActivationFunctionType
ALU = mybir.AluOpType
AX = mybir.AxisListType

C = 1024
NCH = 8
NE = 4
EPS = 1e-5

TOK = 2048
TT = 512
TH = 1024
RG = [[0, 1, 2, 3], [4, 5, 6, 7]]


def configure(tok=2048, th=1024, rg=None):
    global TOK, TH, RG, NTT, NTH, NTHT, NTC
    TOK, TH = tok, th
    if rg is not None:
        RG = rg
    NTT = TOK // TT
    NTH = TOK // TH
    NTHT = TH // TT
    NTC = TOK // 128


configure()

ENGS = ("sync", "tensor", "vector", "scalar", "gpsimd")


class Em:
    def __init__(self):
        self.prog = {e: [] for e in ENGS}
        self.cnt = {}

    def newsem(self, name):
        self.cnt[name] = 0

    def op(self, eng, fn, inc=None, waits=()):
        ws = []
        for t in waits:
            if t is None:
                continue
            (s, v) = t
            if v > 0:
                ws.append((s, v))
        self.prog[eng].append((tuple(ws), fn, inc))
        if inc is not None:
            self.cnt[inc[0]] += inc[1]
            return (inc[0], self.cnt[inc[0]])
        return None

    def tok(self, sem):
        return (sem, self.cnt[sem])

    def replay(self, block, semh):
        def mk(name):
            def body(e):
                lastw = {}
                for ws, fn, inc in self.prog[name]:
                    for (s, v) in ws:
                        if lastw.get(s, 0) < v:
                            e.wait_ge(semh[s], v)
                            lastw[s] = v
                    ins = fn(e)
                    if inc is not None:
                        ins.then_inc(semh[inc[0]], inc[1])
            return body
        block.sync(mk("sync"))
        block.tensor(mk("tensor"))
        block.vector(mk("vector"))
        block.scalar(mk("scalar"))
        block.gpsimd(mk("gpsimd"))


class PBank:
    def __init__(self, ap):
        self.t = ap
        self.opened = False
        self.pending = []

    def ap(self):
        return self.t.ap()

    def begin(self):
        if not self.opened:
            self.opened = True
            w = self.pending
            self.pending = []
            return True, w
        return False, []

    def consumed(self, tok):
        if tok is not None:
            self.pending.append(tok)
        self.opened = False


class Slots:
    def __init__(self, n):
        self.n = n
        self.last = [None] * n
        self.i = 0

    def acquire(self):
        i = self.i % self.n
        self.i += 1
        w = self.last[i]
        self.last[i] = None
        return i, ([w] if w else [])

    def done(self, i, tok):
        self.last[i] = tok


class Row:
    def __init__(self, t, r, off, n, rows=1):
        self.t, self.r, self.off, self.n, self.rows = t, r, off, n, rows

    def __getitem__(self, idx):
        sp, sf = idx
        p0 = sp.start or 0
        p1 = sp.stop if sp.stop is not None else self.rows
        f0 = sf.start or 0
        f1 = sf.stop if sf.stop is not None else self.n
        return self.t[self.r + p0:self.r + p1, self.off + f0:self.off + f1]


class MK:
    def __init__(self, dbg=None, phase_limit=99):
        self.dbg = dbg
        self.phase_limit = phase_limit
        self.nc = bass.Bass("TRN2", target_bir_lowering=False, debug=False)
        self.em = Em()
        for s in ("LD", "LR", "PE", "DV", "AC", "PL", "SG", "ST", "CC"):
            self.em.newsem(s)

    def dma(self, queue, out, in_, sem="LD", waits=()):
        return self.em.op(queue, lambda e, o=out, i=in_: e.dma_start(out=o, in_=i),
                          inc=(sem, 16), waits=waits)

    def mm(self, bank, out, lhsT, rhs, stop, waits=(), inc=None, tile=None, fstart=None):
        st, bw = bank.begin()
        if fstart is not None:
            st = fstart
        return self.em.op("tensor",
                          lambda e, o=out, l=lhsT, r=rhs, s=st, sp=stop, tp=tile: e.matmul(
                              o, l, r, start=s, stop=sp, tile_position=tp),
                          inc=inc, waits=list(waits) + bw)

    def v(self, fn, waits=()):
        return self.em.op("vector", fn, inc=("DV", 1), waits=waits)

    def a(self, fn, waits=()):
        return self.em.op("scalar", fn, inc=("AC", 1), waits=waits)

    def p(self, fn, waits=()):
        return self.em.op("gpsimd", fn, inc=("PL", 1), waits=waits)

    def bar(self):
        e = self.em
        return [e.tok("PE"), e.tok("DV"), e.tok("AC"), e.tok("PL")]

    def build(self):
        nc = self.nc
        ap = lambda t: t.ap()
        self.d_xT = ap(nc.dram_tensor("xT", [C, TOK], F32, kind="ExternalInput"))
        self.d_wqT = ap(nc.dram_tensor("wqT", [C, C], F32, kind="ExternalInput"))
        self.d_wkT = ap(nc.dram_tensor("wkT", [C, C], F32, kind="ExternalInput"))
        self.d_wvT = ap(nc.dram_tensor("wvT", [C, C], F32, kind="ExternalInput"))
        self.d_woT = ap(nc.dram_tensor("woT", [C, C], F32, kind="ExternalInput"))
        self.d_w1 = ap(nc.dram_tensor("w1", [NE, C, C], F32, kind="ExternalInput"))
        self.d_w2 = ap(nc.dram_tensor("w2", [NE, C, C], F32, kind="ExternalInput"))
        self.d_gatec = ap(nc.dram_tensor("gatec", [128, 32], F32, kind="ExternalInput"))
        self.d_gateb = ap(nc.dram_tensor("gateb", [1, 4], F32, kind="ExternalInput"))
        self.d_bq_c = ap(nc.dram_tensor("bq_c", [128, 8], F32, kind="ExternalInput"))
        self.d_bo_c = ap(nc.dram_tensor("bo_c", [128, 8], F32, kind="ExternalInput"))
        self.d_bk_r = ap(nc.dram_tensor("bk_r", [1, C], F32, kind="ExternalInput"))
        self.d_bv_r = ap(nc.dram_tensor("bv_r", [1, C], F32, kind="ExternalInput"))
        self.d_b1cat = ap(nc.dram_tensor("b1cat", [1, NE * C], F32, kind="ExternalInput"))
        self.d_g1cat = ap(nc.dram_tensor("g1cat", [1, NE * C], F32, kind="ExternalInput"))
        self.d_g2cat = ap(nc.dram_tensor("g2cat", [1, NE * C], F32, kind="ExternalInput"))
        self.d_b2cat = ap(nc.dram_tensor("b2cat", [1, NE * C], F32, kind="ExternalInput"))
        self.d_g1c = ap(nc.dram_tensor("g1c", [128, 32], F32, kind="ExternalInput"))
        self.d_be1c = ap(nc.dram_tensor("be1c", [128, 32], F32, kind="ExternalInput"))
        self.d_g2c = ap(nc.dram_tensor("g2c", [128, 32], F32, kind="ExternalInput"))
        self.d_rkl = ap(nc.dram_tensor("rkl", [8, C], F32, kind="ExternalInput"))
        self.d_idn = ap(nc.dram_tensor("idn", [128, 128], F32, kind="ExternalInput"))
        self.d_cvec = ap(nc.dram_tensor("cvec", [128, 2], F32, kind="ExternalInput"))
        self.d_out = ap(nc.dram_tensor("outT", [C, TOK], F32, kind="ExternalOutput"))
        self.d_ccin = ap(nc.dram_tensor("ccin", [8 * 128, 129], F32, kind="Internal"))
        self.d_ccout = ap(nc.dram_tensor("ccout", [8 * 128, 129], F32, kind="Internal"))

        from contextlib import ExitStack
        self.ctx = ExitStack()
        sb = lambda n, s, d: self.ctx.enter_context(nc.sbuf_tensor(n, s, d))
        SLAB = max(NCH * TOK, 16384)
        self.slab1 = sb("slab1", [128, SLAB], BF16)
        self.slab2 = sb("slab2", [128, SLAB], BF16)
        self.slab3 = sb("slab3", [128, SLAB], BF16)
        self.slab4 = sb("slab4", [128, 4096], F32)
        self.slab5 = sb("slab5", [128, SLAB], BF16)
        self.tmpbf = sb("tmpbf", [128, 7 * TT], BF16)
        self.tmpf = sb("tmpf", [128, 2 * TT], F32)
        self.kvf = sb("kvf", [128, 8 * 130], F32)
        self.kv_bf = sb("kv_bf", [128, 8 * 130], BF16)
        self.g1g1b1 = sb("g1g1b1", [2, C], BF16)
        self.ones = sb("ones", [128, 512], BF16)
        self.idnb = sb("idnb", [128, 128], BF16)
        self.cvec = sb("cvecs", [128, 2], F32)
        self.bq_c = sb("bq_cs", [128, 8], F32)
        self.bo_c = sb("bo_cs", [128, 8], F32)
        self.g1c = sb("g1cs", [128, 32], F32)
        self.be1c = sb("be1cs", [128, 32], F32)
        self.g2c = sb("g2cs", [128, 32], F32)
        self.invg1 = sb("invg1", [128, 32], F32)
        self.ivg1sq = sb("ivg1sq", [128, 32], BF16)
        self.ivg2 = sb("ivg2", [128, 32], BF16)
        self.ivg2sq = sb("ivg2sq", [128, 32], BF16)
        self.gsc = sb("gsc", [128, 32], F32)
        self.gatec = sb("gatecs", [128, 32], BF16)
        self.gatecf = sb("gatecf", [128, 32], F32)
        self.rklb = sb("rklb", [8, C], BF16)
        self.w1s = sb("w1s", [128, 8], BF16)
        self.w1sf = sb("w1sf", [128, 8], F32)
        self.wtm = sb("wtm", [128, 4], BF16)
        self.rowar = sb("rowar", [1, 3 * TT + TH + 2 * C + 8], F32)
        o = 0
        self.rowf = Row(self.rowar, 0, 0, 3 * TT); o += 3 * TT
        self.m1f = Row(self.rowar, 0, o, TH); o += TH
        self.rowpair = Row(self.rowar, 0, o, 2 * C); o += 2 * C
        self.sb1s = Row(self.rowar, 0, o, 4); o += 4
        self.gatebf = Row(self.rowar, 0, o, 4)
        self.rowab = sb("rowab", [1, 2 * NE * TH + 2 * TT + 2 * C + 8], BF16)
        o = 0
        self.wcat = Row(self.rowab, 0, 0, NE * TH); o += NE * TH
        self.m2acat = Row(self.rowab, 0, o, NE * TH); o += NE * TH
        self.rowb = Row(self.rowab, 0, o, 2 * TT); o += 2 * TT
        self.g1b1row = Row(self.rowab, 0, o, C); o += C
        self.g2b2row = Row(self.rowab, 0, o, C); o += C
        self.gateb = Row(self.rowab, 0, o, 4)
        self.bk_r = Row(self.g1g1b1, 0, 0, C)
        self.bv_r = Row(self.kv_bf, 0, 0, C)
        self.rnkr = self.kv_bf
        nc.const_aps.register(F32, EPS, self.cvec[:, 0:1])
        nc.const_aps.register(F32, -0.5, self.cvec[:, 1:2])

        self.pb = [PBank(self.ctx.enter_context(nc.psum_tensor(f"pb{i}", [128, 512], F32)))
                   for i in range(8)]
        self.bfs = Slots(7)
        self.ffs = Slots(2)
        self.rfs = Slots(3)
        self.rbs = Slots(2)
        self.stage_free = []

        self.phase0()
        if self.phase_limit >= 1:
            self.tk_done = self.proj_tm(0, self.t_wk, self.bk_r, self.slab2, True)
            self.tv_done = self.proj_tm(8192, self.t_wv, self.bv_r, self.slab3, False)
            self.t_wq = self.load_weight(self.d_wqT, self.slab5, 0,
                                         extra_wait=self.em.tok("PE"))
        if self.phase_limit >= 2:
            self.phaseA_kv_collective()
            self.phaseA_q()
        if self.phase_limit >= 3:
            self.phaseA_attn()
        if self.phase_limit >= 4:
            self.phaseA_wo_ln1()
        if self.phase_limit >= 5:
            self.phaseB()
        self.finish()
        return nc

    def phase0(self):
        dma, v = self.dma, self.v
        t_idn = dma("sync", self.slab4[0:128, 0:128], self.d_idn)
        t_rkl = dma("sync", self.slab4[0:8, 128:1152], self.d_rkl)
        dma("sync", self.cvec[:], self.d_cvec)
        t_bk = dma("sync", self.slab4[0:1, 1152:2176], self.d_bk_r)
        t_bv = dma("sync", self.slab4[0:1, 2176:3200], self.d_bv_r)
        toks = []
        for (dst, src) in [(self.bq_c[:, :], self.d_bq_c), (self.bo_c[:, :], self.d_bo_c),
                           (self.g1c[:, :], self.d_g1c), (self.be1c[:, :], self.d_be1c),
                           (self.g2c[:, :], self.d_g2c), (self.gatecf[:, :], self.d_gatec),
                           (self.gatebf[0:1, :], self.d_gateb)]:
            toks.append(dma("sync", dst, src))
        t_sm = toks[-1]
        v(lambda e: e.memset(self.ones[:], 1.0))
        v(lambda e: e.memset(self.kvf[:], 0.0))
        t_idc = v(lambda e: e.tensor_copy(self.idnb[:], self.slab4[0:128, 0:128]),
                  waits=[t_sm])
        t_rkc = v(lambda e: e.tensor_copy(self.rklb[:], self.slab4[0:8, 128:1152]),
                  waits=[t_sm])
        t_bkc = v(lambda e: e.tensor_copy(self.bk_r[0:1, :], self.slab4[0:1, 1152:2176]),
                  waits=[t_sm])
        t_bvc = v(lambda e: e.tensor_copy(self.bv_r[0:1, :], self.slab4[0:1, 2176:3200]),
                  waits=[t_sm])
        tc = v(lambda e: e.tensor_copy(self.gatec[:, :], self.gatecf[:, :]), waits=[t_sm])
        tc = v(lambda e: e.tensor_copy(self.gateb[0:1, :], self.gatebf[0:1, :]), waits=[t_sm])
        self.t_casts = tc
        self.t_stage0 = [t_idc, t_rkc, t_bkc, t_bvc]
        t1 = v(lambda e: e.tensor_copy(self.ivg2[:], self.g1c[:]), waits=[t_sm])
        t2 = v(lambda e: e.tensor_copy(self.gsc[:], self.ivg2[:]), waits=[t1])
        t3 = v(lambda e: e.reciprocal(self.invg1[:], self.gsc[:]), waits=[t2])
        t4 = v(lambda e: e.tensor_mul(self.gsc[:], self.invg1[:], self.invg1[:]), waits=[t3])
        t5 = v(lambda e: e.tensor_copy(self.ivg1sq[:], self.gsc[:]), waits=[t4])
        t6 = v(lambda e: e.tensor_copy(self.ivg2[:], self.g2c[:]), waits=[t5])
        t7 = v(lambda e: e.tensor_copy(self.gsc[:], self.ivg2[:]), waits=[t6])
        t8 = v(lambda e: e.reciprocal(self.gsc[:], self.gsc[:]), waits=[t7])
        t9 = v(lambda e: e.tensor_copy(self.ivg2[:], self.gsc[:]), waits=[t8])
        t10 = v(lambda e: e.tensor_mul(self.gsc[:], self.gsc[:], self.gsc[:]), waits=[t9])
        self.t_ginv = v(lambda e: e.tensor_copy(self.ivg2sq[:], self.gsc[:]), waits=[t10])
        XS = 2048 if TOK > 2048 else TOK
        gate = list(self.t_stage0)
        tc = None
        for w in range(4):
            for s in range(2):
                dma("sync", self.slab4[:, s * XS:s * XS + TOK],
                    self.d_xT[(2 * w + s) * 128:(2 * w + s + 1) * 128, :],
                    waits=gate if s == 0 else [])
            tokw = self.em.tok("LD")
            gate = []
            for s in range(2):
                tc = v(lambda e, k=2 * w + s, s=s: e.tensor_copy(
                    self.slab1[:, k * TOK:(k + 1) * TOK],
                    self.slab4[:, s * XS:s * XS + TOK]), waits=[tokw])
                gate.append(tc)
        self.t_xbf = tc
        self.stage_free = [tc] + gate
        self.stage_slot_free = None
        self.t_wk = self.load_weight(self.d_wkT, self.slab5, 0)
        self.t_wv = self.load_weight(self.d_wvT, self.slab5, 8192)

    def load_weight(self, dsrc, dest, doff, scale=None, w1s=False,
                    extra_wait=None, cast_waits=()):
        t = None
        tw = None
        gate = list(self.stage_free)
        if extra_wait is not None:
            gate.append(extra_wait)
        for wave in range(2):
            for j in range(4):
                k = wave * 4 + j
                self.dma("sync", self.slab4[:, j * 1024:(j + 1) * 1024],
                         dsrc[k * 128:(k + 1) * 128, :], waits=gate if j == 0 else [])
            tokw = self.em.tok("LD")
            gate = []
            for j in range(4):
                k = wave * 4 + j
                cw = list(cast_waits) if k == 0 else []
                if w1s:
                    tw = self.v(lambda e, k=k, j=j: e.tensor_reduce(
                        self.w1sf[:, k:k + 1], self.slab4[:, j * 1024:(j + 1) * 1024],
                        AX.X, ALU.add), waits=[tokw] + cw)
                    gate.append(tw)
                if scale is None:
                    t = self.p(lambda e, k=k, j=j: e.tensor_copy(
                        dest[:, doff + k * 1024: doff + (k + 1) * 1024],
                        self.slab4[:, j * 1024:(j + 1) * 1024]), waits=[tokw] + cw)
                else:
                    t = self.p(lambda e, k=k, j=j, g=scale: e.tensor_tensor(
                        dest[:, doff + k * 1024: doff + (k + 1) * 1024],
                        self.slab4[:, j * 1024:(j + 1) * 1024], g, ALU.mult),
                        waits=[tokw] + cw)
                gate.append(t)
        self.stage_free = gate
        if w1s:
            self.t_w1s = self.v(lambda e: e.tensor_copy(self.w1s[:], self.w1sf[:]),
                                waits=[tw, getattr(self, "w1s_free", None)])
        return t

    def proj_tm(self, wofs, w_tok, biasrow, dest, do_elu):
        mm, v, a = self.mm, self.v, self.a
        last = None
        for j in range(NTC):
            for h2 in range(2):
                bank = self.pb[(j * 2 + h2) % 4]
                ps = bank.ap()
                for ci in range(8):
                    w = [w_tok, self.t_xbf] if ci == 0 else []
                    mm(bank, ps[:, 0:TT],
                       self.slab1[:, ci * TOK + j * 128: ci * TOK + (j + 1) * 128],
                       self.slab5[:, wofs + ci * 1024 + h2 * 512: wofs + ci * 1024 + h2 * 512 + 512],
                       stop=False, waits=w)
                tmm = mm(bank, ps[:, 0:TT], self.ones[0:1, 0:128],
                         biasrow[0:1, h2 * 512: h2 * 512 + 512], stop=True, inc=("PE", 1))
                dcol = j * 1024 + h2 * 512
                if do_elu:
                    s1, w1_ = self.bfs.acquire()
                    s2, w2_ = self.bfs.acquire()
                    ta = a(lambda e, ps=ps, s=s1: e.activation(
                        self.tmpbf[:, s * TT:(s + 1) * TT], ps[:, 0:TT], AF.Exp),
                        waits=[tmm] + w1_)
                    t1 = v(lambda e, ps=ps, s=s2: e.tensor_scalar(
                        self.tmpbf[:, s * TT:(s + 1) * TT], ps[:, 0:TT], 0.0, None, ALU.max),
                        waits=[ta] + w2_)
                    tk = v(lambda e, s1=s1, s2=s2, d=dcol: e.scalar_tensor_tensor(
                        dest[:, d:d + 512], self.tmpbf[:, s1 * TT:(s1 + 1) * TT], 1.0,
                        self.tmpbf[:, s2 * TT:(s2 + 1) * TT], ALU.min, ALU.add),
                        waits=[ta, t1])
                    self.bfs.done(s1, tk)
                    self.bfs.done(s2, tk)
                    bank.consumed(ta)
                    bank.consumed(t1)
                    last = tk
                else:
                    tk = a(lambda e, ps=ps, d=dcol: e.copy(dest[:, d:d + 512], ps[:, 0:TT]),
                           waits=[tmm])
                    bank.consumed(tk)
                    last = tk
        return last

    def phaseA_kv_collective(self):
        em, mm, v = self.em, self.mm, self.v
        evac = []
        for p_ in range(8):
            bank = self.pb[p_]
            ps = bank.ap()
            tmm = None
            for j in range(NTC):
                kcol = j * 1024 + p_ * 128
                w = [self.tk_done, self.tv_done] if j == 0 else []
                mm(bank, ps[:, 0:128], self.slab2[:, kcol:kcol + 128],
                   self.slab3[:, kcol:kcol + 128], stop=False, waits=w)
                tmm = mm(bank, ps[:, 128:129], self.slab2[:, kcol:kcol + 128],
                         self.ones[:, 0:1], stop=(j == NTC - 1),
                         inc=("PE", 1) if j == NTC - 1 else None)
            te = v(lambda e, p_=p_, ps=ps: e.tensor_copy(
                self.kvf[:, p_ * 130:p_ * 130 + 129], ps[:, 0:129]), waits=[tmm])
            bank.consumed(te)
            evac.append(self.dma("gpsimd", self.d_ccin[p_ * 128:(p_ + 1) * 128, :],
                                 self.kvf[:, p_ * 130:p_ * 130 + 129], sem="SG", waits=[te]))
        self.t_cc = em.op("gpsimd", lambda e: e.collective_compute(
            "AllReduce", ALU.add, RG, [self.d_ccin.opt()], [self.d_ccout.opt()]),
            inc=("CC", 1), waits=[evac[-1]])

    def phaseA_q(self):
        em, mm, v, a = self.em, self.mm, self.v, self.a
        t_kvPE = em.tok("PE")
        last = None
        for ch in range(NCH):
            for tt in range(NTT):
                bank = self.pb[(ch * NTT + tt) % 4]
                ps = bank.ap()
                for ci in range(8):
                    w = [self.t_wq] if (ch == 0 and tt == 0 and ci == 0) else []
                    mm(bank, ps[:, 0:TT],
                       self.slab5[:, ci * 1024 + ch * 128: ci * 1024 + ch * 128 + 128],
                       self.slab1[:, ci * TOK + tt * TT: ci * TOK + (tt + 1) * TT],
                       stop=(ci == 7), waits=w, inc=("PE", 1) if ci == 7 else None)
                tmm = em.tok("PE")
                s1, w1_ = self.bfs.acquire()
                s2, w2_ = self.bfs.acquire()
                ta = a(lambda e, ps=ps, ch=ch, s=s1: e.activation(
                    self.tmpbf[:, s * TT:(s + 1) * TT], ps[:, 0:TT], AF.Exp,
                    bias=self.bq_c[:, ch:ch + 1]), waits=[tmm] + w1_)
                t1 = v(lambda e, ps=ps, ch=ch, s=s2: e.tensor_scalar(
                    self.tmpbf[:, s * TT:(s + 1) * TT], ps[:, 0:TT],
                    self.bq_c[:, ch:ch + 1], 0.0, ALU.add, ALU.max), waits=[ta] + w2_)
                dcol = ch * TOK + tt * TT
                wq0 = [t_kvPE] if (ch == 0 and tt == 0) else []
                tk = v(lambda e, s1=s1, s2=s2, d=dcol: e.scalar_tensor_tensor(
                    self.slab2[:, d:d + TT], self.tmpbf[:, s1 * TT:(s1 + 1) * TT], 1.0,
                    self.tmpbf[:, s2 * TT:(s2 + 1) * TT], ALU.min, ALU.add),
                    waits=[ta, t1] + wq0)
                self.bfs.done(s1, tk)
                self.bfs.done(s2, tk)
                bank.consumed(ta)
                bank.consumed(t1)
                last = tk
        self.q_done = last
        self.t_wo = self.load_weight(self.d_woT, self.slab1, 0, extra_wait=self.em.tok("PE"))

    def phaseA_attn(self):
        em, mm, v, a = self.em, self.mm, self.v, self.a
        t2 = None
        for p_ in range(8):
            t2 = self.dma("gpsimd", self.kvf[:, p_ * 130:p_ * 130 + 129],
                          self.d_ccout[p_ * 128:(p_ + 1) * 128, :], sem="SG",
                          waits=[self.t_cc] if p_ == 0 else [])
        t_kvbf = v(lambda e: e.tensor_copy(self.kv_bf[:], self.kvf[:]), waits=[t2])
        units = [(p_, h2, tt) for p_ in range(8) for h2 in range(2) for tt in range(NTT)]
        D = 2
        st = {}
        last = None
        rslots = Slots(8)
        rrow = lambda i: self.m2acat[0:1, i * TT:(i + 1) * TT]

        def stageA(u):
            p_, h2, tt = units[u]
            bank = self.pb[u % 6]
            ps = bank.ap()
            lo, hi = h2 * 64, h2 * 64 + 64
            lh = self.kv_bf[lo:hi, p_ * 130 + h2 * 64: p_ * 130 + h2 * 64 + 64]
            ld = self.kv_bf[lo:hi, p_ * 130 + 128: p_ * 130 + 129]
            qs = self.slab2[lo:hi, p_ * TOK + tt * TT: p_ * TOK + (tt + 1) * TT]
            w = [t_kvbf, self.q_done] if u == 0 else []
            mm(bank, ps[lo:hi, 0:TT], lh, qs, stop=True, waits=w,
               tile=(h2 * 64, h2 * 64))
            dnp = 64 if h2 == 0 else 0
            tmm = mm(bank, ps[dnp:dnp + 1, 0:TT], ld, qs, stop=True,
                     inc=("PE", 1), tile=(h2 * 64, dnp), fstart=True)
            rs, wr = self.rfs.acquire()
            td = v(lambda e, ps=ps, rs=rs, dnp=dnp: e.tensor_scalar(
                self.rowf[0:1, rs * TT:rs * TT + TT], ps[dnp:dnp + 1, 0:TT], 1e-6, None,
                ALU.max), waits=[tmm] + wr)
            ri, wri = rslots.acquire()
            def _recip(e, rs=rs, ri=ri):
                with self.nc.allow_low_precision(reason="bf16 recip row feeds bf16 matmul"):
                    return e.reciprocal(rrow(ri), self.rowf[0:1, rs * TT:rs * TT + TT])
            tb = v(_recip, waits=[td] + wri)
            self.rfs.done(rs, tb)
            st[u] = (bank, ps, ri, tb, td)

        def stageC(u):
            nonlocal last
            p_, h2, tt = units[u]
            bank, ps, ri, tb, td = st.pop(u)
            lo, hi = h2 * 64, h2 * 64 + 64
            rbank = self.pb[6 + u % 2]
            rp = rbank.ap()
            trep = mm(rbank, rp[lo:hi, 0:TT], self.ones[0:1, 0:64], rrow(ri), stop=True,
                      waits=[tb], inc=("PE", 1), tile=(0, h2 * 64))
            rslots.done(ri, trep)
            sl, wsl = self.bfs.acquire()
            tev = a(lambda e, rp=rp, s=sl, lo=lo, hi=hi: e.copy(
                self.tmpbf[lo:hi, s * TT:(s + 1) * TT], rp[lo:hi, 0:TT]),
                waits=[trep] + wsl)
            rbank.consumed(tev)
            dcol = p_ * TOK + tt * TT
            tmul = v(lambda e, ps=ps, s=sl, lo=lo, hi=hi, d=dcol: e.tensor_mul(
                self.slab3[lo:hi, d:d + TT], ps[lo:hi, 0:TT],
                self.tmpbf[lo:hi, s * TT:(s + 1) * TT]), waits=[tev, td])
            self.bfs.done(sl, tmul)
            bank.consumed(tmul)
            last = tmul

        for u in range(len(units)):
            stageA(u)
            if u >= D:
                stageC(u - D)
        for u in range(len(units) - D, len(units)):
            stageC(u)
        self.am_done = last

    def phaseA_wo_ln1(self):
        em, mm, v, a = self.em, self.mm, self.v, self.a
        y_done = None
        for tt in range(NTT):
            tx = None
            for ch in range(NCH):
                w = []
                if ch == 0:
                    w = [y_done] if tt > 0 else list(self.stage_free)
                tx = self.dma("sync", self.slab4[:, ch * TT // 1:(ch) * TT + TT][:, 0:TT] if False else self.slab4[:, (ch % 8) * 512:(ch % 8) * 512 + TT],
                              self.d_xT[ch * 128:(ch + 1) * 128, tt * TT:(tt + 1) * TT],
                              waits=w)
            s_toks = []
            bf_state = {}
            sumb, sqb = self.pb[4], self.pb[5]
            pss, psq = sumb.ap(), sqb.ap()

            def ln1stat(ch, pss=None, psq=None):
                pss = sumb.ap() if pss is None else pss
                psq = sqb.ap() if psq is None else psq
                s1, s2, tb, tq = bf_state.pop(ch)
                mm(sumb, pss[0:1, 0:TT], self.ones[:, 0:1],
                   self.tmpbf[:, s1 * TT:(s1 + 1) * TT], stop=(ch == NCH - 1),
                   waits=[tb], inc=("PE", 1))
                tq2 = mm(sqb, psq[0:1, 0:TT], self.ones[:, 0:1],
                         self.tmpbf[:, s2 * TT:(s2 + 1) * TT], stop=(ch == NCH - 1),
                         waits=[tq], inc=("PE", 1))
                self.bfs.done(s1, tq2)
                self.bfs.done(s2, tq2)

            for ch in range(NCH):
                bank = self.pb[ch % 4]
                ps = bank.ap()
                for ci in range(8):
                    w = [self.t_wo, self.am_done] if (tt == 0 and ch == 0 and ci == 0) else []
                    mm(bank, ps[:, 0:TT],
                       self.slab1[:, ci * 1024 + ch * 128: ci * 1024 + ch * 128 + 128],
                       self.slab3[:, ci * TOK + tt * TT: ci * TOK + (tt + 1) * TT],
                       stop=(ci == 7), waits=w, inc=("PE", 1) if ci == 7 else None)
                tmm = em.tok("PE")
                ts_ = v(lambda e, ps=ps, ch=ch: e.scalar_tensor_tensor(
                    self.slab4[:, ch * 512:ch * 512 + TT], ps[:, 0:TT],
                    self.bo_c[:, ch:ch + 1], self.slab4[:, ch * 512:ch * 512 + TT],
                    ALU.add, ALU.add), waits=[tmm, tx])
                s_toks.append(ts_)
                bank.consumed(ts_)
                s1, w1_ = self.bfs.acquire()
                s2, w2_ = self.bfs.acquire()
                tb = a(lambda e, ch=ch, s=s1: e.copy(
                    self.tmpbf[:, s * TT:(s + 1) * TT],
                    self.slab4[:, ch * 512:ch * 512 + TT]),
                    waits=[ts_] + w1_)
                tq = v(lambda e, s1=s1, s2=s2: e.tensor_mul(
                    self.tmpbf[:, s2 * TT:(s2 + 1) * TT], self.tmpbf[:, s1 * TT:(s1 + 1) * TT],
                    self.tmpbf[:, s1 * TT:(s1 + 1) * TT]), waits=[tb] + w2_)
                bf_state[ch] = (s1, s2, tb, tq)
                if ch >= 1:
                    ln1stat(ch - 1)
            ln1stat(NCH - 1)
            tstat = em.tok("PE")
            r0, wr0 = self.rfs.acquire()
            r1, wr1 = self.rfs.acquire()
            r2, wr2 = self.rfs.acquire()
            R = lambda i: self.rowf[0:1, i * TT:(i + 1) * TT]
            tm_ = v(lambda e, r0=r0: e.tensor_scalar(R(r0), pss[0:1, 0:TT], 1.0 / 1024.0,
                                                     None, ALU.mult), waits=[tstat] + wr0)
            tv_ = v(lambda e, r1=r1: e.tensor_scalar(R(r1), psq[0:1, 0:TT], 1.0 / 1024.0,
                                                     None, ALU.mult), waits=[tstat] + wr1)
            sumb.consumed(tm_)
            sqb.consumed(tv_)
            tm2 = v(lambda e, r0=r0, r2=r2: e.tensor_mul(R(r2), R(r0), R(r0)),
                    waits=[tm_] + wr2)
            tvv = v(lambda e, r1=r1, r2=r2: e.tensor_sub(R(r1), R(r1), R(r2)),
                    waits=[tv_, tm2])
            tl = a(lambda e, r1=r1: e.activation(R(r1), R(r1), AF.Ln, bias=EPS), waits=[tvv])
            tr_ = a(lambda e, r1=r1: e.activation(R(r1), R(r1), AF.Exp, scale=-0.5),
                    waits=[tl])
            tmr = v(lambda e, r0=r0, r1=r1, r2=r2: e.tensor_mul(R(r2), R(r0), R(r1)),
                    waits=[tr_])
            b0, wb0 = self.rbs.acquire()
            b1, wb1 = self.rbs.acquire()
            RB = lambda i: self.rowb[0:1, i * TT:(i + 1) * TT]
            tbr = v(lambda e, b0=b0, r1=r1: e.tensor_copy(RB(b0), R(r1)), waits=[tr_] + wb0)
            tbm = v(lambda e, b1=b1, r2=r2: e.tensor_copy(RB(b1), R(r2)), waits=[tmr] + wb1)
            self.rfs.done(r0, tmr)
            self.rfs.done(r1, tbr)
            self.rfs.done(r2, tbm)
            rrb, rmb = self.pb[6], self.pb[7]
            prr, prm = rrb.ap(), rmb.ap()
            t_rr = mm(rrb, prr[:, 0:TT], self.ones[0:1, 0:128], RB(b0), stop=True,
                      waits=[tbr], inc=("PE", 1))
            t_rm = mm(rmb, prm[:, 0:TT], self.ones[0:1, 0:128], RB(b1), stop=True,
                      waits=[tbm], inc=("PE", 1))
            self.rbs.done(b0, t_rr)
            self.rbs.done(b1, t_rm)
            for ch in range(NCH):
                f0, wf0 = self.ffs.acquire()
                t1 = v(lambda e, ch=ch, f=f0, prr=prr: e.tensor_mul(
                    self.tmpf[:, f * TT:(f + 1) * TT],
                    self.slab4[:, ch * 512:ch * 512 + TT], prr[:, 0:TT]),
                    waits=[t_rr] + wf0)
                y_done = v(lambda e, ch=ch, f=f0, tt=tt, prm=prm: e.tensor_sub(
                    self.slab5[:, ch * TOK + tt * TT: ch * TOK + (tt + 1) * TT],
                    self.tmpf[:, f * TT:(f + 1) * TT], prm[:, 0:TT]), waits=[t_rm, t1])
                self.ffs.done(f0, y_done)
            rrb.consumed(y_done)
            rmb.consumed(y_done)
        self.y_done = y_done
        self.stage_free = [y_done]

    def phaseB(self):
        em, mm, v, a = self.em, self.mm, self.v, self.a
        bar0 = self.bar()
        self.xp = {k: list(bar0) for k in
                   ("rowpair", "g1g1b1", "g1b1row", "g2b2row", "grep", "sb1s",
                    "m1f", "m1on", "uc", "zt", "m2acat")}
        self.xp["w1s"] = []
        self.m2a_last = []
        prev_out = None
        for hh in range(NTH):
            tb0 = hh * TH
            self.t_m1on = v(lambda e: e.memset(self.kv_bf[0:2, 0:TH], 1.0),
                            waits=([self.am_done, self.em.tok("PE")] if hh == 0 else
                                   [prev_out] + self.bar()) + self.xp["m1on"])
            self.xp["m1on"] = [self.t_m1on]
            tgate = None
            for j in range(TH // 128):
                jb = tb0 + j * 128
                bank = self.pb[j % 2]
                ps = bank.ap()
                for ci in range(8):
                    w = [self.y_done, self.t_casts, self.t_m1on] if (j == 0 and ci == 0) else []
                    if hh > 0 and j == 0 and ci == 0:
                        w = [prev_out]
                    mm(bank, ps[:, 0:4],
                       self.slab5[:, ci * TOK + jb: ci * TOK + jb + 128],
                       self.gatec[:, ci * 4:(ci + 1) * 4], stop=False, waits=w)
                tmm = mm(bank, ps[:, 0:4], self.ones[0:1, 0:128], self.gateb[0:1, 0:4],
                         stop=True, inc=("PE", 1))
                f0, wf0 = self.ffs.acquire()
                Ft = self.tmpf[:, f0 * TT:(f0 + 1) * TT]
                tM1 = v(lambda e, ps=ps, Ft=Ft: e.tensor_reduce(
                    Ft[:, 0:1], ps[:, 0:4], AX.X, ALU.max, negate=True), waits=[tmm] + wf0)
                tmsk = v(lambda e, ps=ps, Ft=Ft: e.tensor_scalar(
                    Ft[:, 1:5], ps[:, 0:4], Ft[:, 0:1], 0.0, ALU.add, ALU.is_ge), waits=[tM1])
                ts2 = v(lambda e, ps=ps, Ft=Ft: e.scalar_tensor_tensor(
                    Ft[:, 5:9], Ft[:, 1:5], -1e9, ps[:, 0:4], ALU.mult, ALU.add), waits=[tmsk])
                tM2 = v(lambda e, Ft=Ft: e.tensor_reduce(
                    Ft[:, 9:10], Ft[:, 5:9], AX.X, ALU.max, negate=True), waits=[ts2])
                tsel = v(lambda e, ps=ps, Ft=Ft: e.tensor_scalar(
                    Ft[:, 14:18], ps[:, 0:4], Ft[:, 9:10], 0.0, ALU.add, ALU.is_ge),
                    waits=[tM2])
                tE = a(lambda e, ps=ps, Ft=Ft: e.activation(
                    Ft[:, 10:14], ps[:, 0:4], AF.Exp, bias=Ft[:, 0:1]), waits=[tsel])
                bank.consumed(tsel)
                bank.consumed(tE)
                tEs = v(lambda e, Ft=Ft: e.tensor_mul(Ft[:, 10:14], Ft[:, 10:14],
                                                      Ft[:, 14:18]), waits=[tE, tsel])
                tZ = v(lambda e, Ft=Ft: e.tensor_reduce(Ft[:, 18:19], Ft[:, 10:14], AX.X,
                                                        ALU.add), waits=[tEs])
                tZi = v(lambda e, Ft=Ft: e.reciprocal(Ft[:, 19:20], Ft[:, 18:19]),
                        waits=[tZ])
                tw = v(lambda e, Ft=Ft: e.tensor_scalar(
                    self.wtm[:, 0:4], Ft[:, 10:14], Ft[:, 19:20], None, ALU.mult),
                    waits=[tZi, getattr(self, "wtm_free", None)])
                self.ffs.done(f0, tw)
                wb2 = self.pb[2]
                pw = wb2.ap()
                tt_ = None
                for e_ in range(NE):
                    tt_ = mm(wb2, pw[0:1, e_ * 128:(e_ + 1) * 128], self.wtm[:, e_:e_ + 1],
                             self.idnb[:], stop=(e_ == NE - 1), waits=[tw],
                             inc=("PE", 1) if e_ == NE - 1 else None)
                self.wtm_free = tt_
                for e_ in range(NE):
                    tgate = v(lambda e, e_=e_, j=j, pw=pw: e.tensor_copy(
                        self.wcat[0:1, e_ * TH + j * 128: e_ * TH + (j + 1) * 128],
                        pw[0:1, e_ * 128:(e_ + 1) * 128]), waits=[tt_])
                wb2.consumed(tgate)
            self.t_gate = tgate
            t_acc = None
            for ch in range(NCH):
                for q in range(NTHT):
                    w = []
                    if ch == 0 and q == 0:
                        w = [self.t_gate]
                    t_acc = v(lambda e, ch=ch, q=q, tb=tb0: e.tensor_scalar(
                        self.slab3[:, ch * TH + q * TT: ch * TH + (q + 1) * TT],
                        self.slab5[:, ch * TOK + tb + q * TT: ch * TOK + tb + (q + 1) * TT],
                        2.0, None, ALU.mult), waits=w)
            acc_tok = t_acc
            wslot = [0]
            prev_w_pe = [None, None]
            for e_ in range(NE):
                acc_tok = self.expert(e_, hh, wslot, prev_w_pe, acc_tok)
            prev_out = self.final_ln(hh, acc_tok)
        self.out_done = prev_out

    def expert(self, e_, hh, wslot, prev_w_pe, acc_tok):
        em, mm, v, a, p = self.em, self.mm, self.v, self.a, self.p
        tb0 = hh * TH
        ws = wslot[0]
        wslot[0] ^= 1
        xp = self.xp
        wdst, woff = (self.slab3, 8192) if ws == 0 else (self.slab1, 0)
        WL = lambda ci, ch: wdst[:, woff + ci * 1024 + ch * 128: woff + ci * 1024 + ch * 128 + 128]

        RP0 = self.rowpair[0:1, 0:C]
        RP1 = self.rowpair[0:1, C:2 * C]
        self.dma("sync", RP0, self.d_b1cat[0:1, e_ * C:(e_ + 1) * C], sem="LR",
                 waits=xp["rowpair"])
        t_p2 = self.dma("sync", RP1, self.d_g1cat[0:1, e_ * C:(e_ + 1) * C], sem="LR")
        tg1 = v(lambda e: e.tensor_copy(self.g1g1b1[0:1, :], RP1),
                waits=[t_p2] + xp["g1g1b1"])
        tgb = v(lambda e: e.tensor_tensor(self.g1b1row[0:1, :], RP1, RP0, ALU.mult),
                waits=[t_p2] + xp["g1b1row"])
        t_stk = self.dma("gpsimd", self.g1g1b1[1:2, 0:C], self.g1b1row[0:1, 0:C],
                         sem="SG", waits=[tgb] + xp["g1g1b1"])
        tsb1 = v(lambda e: e.tensor_reduce(self.sb1s[0:1, 0:1], RP0, AX.X, ALU.add),
                 waits=[t_p2] + xp["sb1s"])
        gb = self.pb[6]
        pg = gb.ap()
        tr1 = mm(gb, pg[:, 0:512], self.ones[0:1, 0:128], self.g1g1b1[0:1, 0:512],
                 stop=True, waits=[tg1], inc=("PE", 1))
        te1 = a(lambda e: e.copy(self.kvf[:, 0:512], pg[:, 0:512]),
                waits=[tr1] + xp["grep"])
        gb.consumed(te1)
        tr2 = mm(gb, pg[:, 0:512], self.ones[0:1, 0:128], self.g1g1b1[0:1, 512:1024],
                 stop=True, waits=[tg1], inc=("PE", 1))
        te2 = a(lambda e: e.copy(self.kvf[:, 512:1024], pg[:, 0:512]), waits=[tr2])
        gb.consumed(te2)
        self.w1s_free = xp["w1s"][0] if xp["w1s"] else None
        t_w1g = self.load_weight(self.d_w1[e_], wdst, woff, scale=self.kvf[:, 0:1024],
                                 w1s=True, extra_wait=prev_w_pe[ws], cast_waits=[te1, te2])
        t_w1s = self.t_w1s
        self.dma("sync", RP0, self.d_g2cat[0:1, e_ * C:(e_ + 1) * C], sem="LR",
                 waits=[tgb, tsb1, tg1])
        t_p4 = self.dma("sync", RP1, self.d_b2cat[0:1, e_ * C:(e_ + 1) * C], sem="LR")
        tg2b2 = v(lambda e: e.tensor_tensor(self.g2b2row[0:1, :], RP0, RP1, ALU.mult),
                  waits=[t_p4] + xp["g2b2row"])
        tg2r = v(lambda e: e.tensor_copy(self.g1b1row[0:1, :], RP0),
                 waits=[t_stk, t_p4])
        gb2 = self.pb[6]
        pg2 = gb2.ap()
        tw21 = mm(gb2, pg2[:, 0:512], self.ones[0:1, 0:128], self.g1b1row[0:1, 0:512],
                  stop=True, waits=[tg2r], inc=("PE", 1))
        tge1 = a(lambda e: e.copy(self.kvf[:, 0:512], pg2[:, 0:512]),
                 waits=[tw21, t_w1g])
        gb2.consumed(tge1)
        tw22 = mm(gb2, pg2[:, 0:512], self.ones[0:1, 0:128], self.g1b1row[0:1, 512:1024],
                  stop=True, waits=[tg2r], inc=("PE", 1))
        tge2 = a(lambda e: e.copy(self.kvf[:, 512:1024], pg2[:, 0:512]), waits=[tw22])
        gb2.consumed(tge2)
        ows = ws ^ 1
        owdst, owoff = (self.slab3, 8192) if ows == 0 else (self.slab1, 0)
        t_w2g = self.load_weight(self.d_w2[e_], owdst, owoff, scale=self.kvf[:, 0:1024],
                                 extra_wait=prev_w_pe[ows], cast_waits=[tge1, tge2])
        W2L = lambda ci, ch: owdst[:, owoff + ci * 1024 + ch * 128: owoff + ci * 1024 + ch * 128 + 128]
        xp["rowpair"] = [tg2r, tg2b2]
        xp["grep"] = [t_w2g]
        xp["g1b1row"] = [tw22]
        xp["sb1s"] = []

        m1toks = []
        sB = self.pb[4]
        tm1s = []
        for q in range(NTHT):
            psS = sB.ap()
            for ci in range(8):
                w = [t_w1s, self.y_done] if ci == 0 else []
                mm(sB, psS[0:1, 0:TT], self.w1s[:, ci:ci + 1],
                   self.slab5[:, ci * TOK + tb0 + q * TT: ci * TOK + tb0 + (q + 1) * TT],
                   stop=(ci == 7), waits=w, inc=("PE", 1) if ci == 7 else None)
            tm1 = v(lambda e, q=q, psS=psS: e.tensor_scalar(
                self.m1f[0:1, q * TT:(q + 1) * TT], psS[0:1, 0:TT],
                self.sb1s[0:1, 0:1], 1.0 / 1024.0, ALU.add, ALU.mult),
                waits=[em.tok("PE"), tsb1] + xp["m1f"])
            tneg = v(lambda e, q=q: e.tensor_scalar(
                self.kv_bf[0:1, q * TT:(q + 1) * TT], self.m1f[0:1, q * TT:(q + 1) * TT],
                -1.0, None, ALU.mult), waits=[tm1] + xp["m1on"])
            sB.consumed(tm1)
            m1toks.append(tneg)
            tm1s.append(tm1)
        self.w1s_next = em.tok("PE")
        xp["w1s"] = [self.w1s_next]
        uc_toks = {}
        sq_toks = {}
        q2b = [self.pb[4], self.pb[5]]
        units = [(ch, q) for ch in range(NCH) for q in range(NTHT)]
        done_per_q = {q: 0 for q in range(NTHT)}

        def u_stat(idx):
            ch, q = units[idx]
            s1, tsq = sq_toks[(ch, q)]
            tqm = mm(q2b[q], q2b[q].ap()[0:1, 0:TT],
                     self.ivg1sq[:, e_ * 8 + ch:e_ * 8 + ch + 1],
                     self.tmpbf[:, s1 * TT:(s1 + 1) * TT],
                     stop=(done_per_q[q] == NCH - 1), waits=[tsq], inc=("PE", 1))
            done_per_q[q] += 1
            self.bfs.done(s1, tqm)

        for idx, (ch, q) in enumerate(units):
            bank = self.pb[idx % 4]
            ps = bank.ap()
            for ci in range(8):
                w = [t_w1g] if idx == 0 and ci == 0 else []
                mm(bank, ps[:, 0:TT], WL(ci, ch),
                   self.slab5[:, ci * TOK + tb0 + q * TT: ci * TOK + tb0 + (q + 1) * TT],
                   stop=False, waits=w)
            tmm = mm(bank, ps[:, 0:TT], self.g1g1b1[0:2, ch * 128:(ch + 1) * 128],
                     self.kv_bf[0:2, q * TT:(q + 1) * TT], stop=True,
                     waits=[m1toks[q], t_stk], inc=("PE", 1))
            ucol = 8192 + ch * TH + q * TT
            tuc = a(lambda e, ps=ps, d=ucol: e.copy(self.slab1[:, d:d + TT], ps[:, 0:TT]),
                    waits=[tmm] + (xp["uc"] if idx == 0 else []))
            bank.consumed(tuc)
            s1, w1_ = self.bfs.acquire()
            tsq = v(lambda e, d=ucol, s=s1: e.tensor_mul(
                self.tmpbf[:, s * TT:(s + 1) * TT], self.slab1[:, d:d + TT],
                self.slab1[:, d:d + TT]), waits=[tuc] + w1_)
            uc_toks[(ch, q)] = tuc
            sq_toks[(ch, q)] = (s1, tsq)
            if idx >= 6:
                u_stat(idx - 6)
        t_umm = em.tok("PE")
        for idx in range(len(units) - 6, len(units)):
            u_stat(idx)
        t_uq = em.tok("PE")
        h_toks = {}
        for q in range(NTHT):
            r0, wr0 = self.rfs.acquire()
            r1_, wr1 = self.rfs.acquire()
            R0 = self.rowf[0:1, r0 * TT:(r0 + 1) * TT]
            R1 = self.rowf[0:1, r1_ * TT:(r1_ + 1) * TT]
            tE2 = v(lambda e, q=q, R0=R0: e.tensor_scalar(
                R0, q2b[q].ap()[0:1, 0:TT], 1.0 / 1024.0, None, ALU.mult),
                waits=[t_uq] + wr0)
            q2b[q].consumed(tE2)
            tm2 = v(lambda e, q=q, R1=R1: e.tensor_mul(
                R1, self.m1f[0:1, q * TT:(q + 1) * TT], self.m1f[0:1, q * TT:(q + 1) * TT]),
                waits=wr1 + [m1toks[q]])
            tvv = v(lambda e, R0=R0, R1=R1: e.tensor_sub(R0, R0, R1), waits=[tE2, tm2])
            tl = a(lambda e, R0=R0: e.activation(R0, R0, AF.Ln, bias=EPS), waits=[tvv])
            tr_ = a(lambda e, R0=R0: e.activation(R0, R0, AF.Exp, scale=-0.5), waits=[tl])
            b0, wb0 = self.rbs.acquire()
            RB0 = self.rowb[0:1, b0 * TT:(b0 + 1) * TT]
            tbr = v(lambda e, RB0=RB0, R0=R0: e.tensor_copy(RB0, R0), waits=[tr_] + wb0)
            self.rfs.done(r0, tbr)
            self.rfs.done(r1_, tvv)
            if q == NTHT - 1:
                xp["m1f"] = [tm2]
            repb = q2b[q]
            prp = repb.ap()
            t_rep = mm(repb, prp[:, 0:TT], self.ones[0:1, 0:128], RB0, stop=True,
                       waits=[tbr], inc=("PE", 1))
            self.rbs.done(b0, t_rep)
            lasth = None
            for ch in range(NCH):
                ucol = 8192 + ch * TH + q * TT
                s1, w1_ = self.bfs.acquire()
                t3 = v(lambda e, d=ucol, s=s1, prp=prp: e.tensor_mul(
                    self.tmpbf[:, s * TT:(s + 1) * TT], self.slab1[:, d:d + TT],
                    prp[:, 0:TT]), waits=[t_rep, uc_toks[(ch, q)]] + w1_)
                th = a(lambda e, s=s1, ch=ch, q=q: e.activation(
                    self.slab2[:, ch * TH + q * TT: ch * TH + (q + 1) * TT],
                    self.tmpbf[:, s * TT:(s + 1) * TT], AF.Relu,
                    bias=self.be1c[:, e_ * 8 + ch:e_ * 8 + ch + 1]), waits=[t3])
                self.bfs.done(s1, th)
                h_toks[(ch, q)] = th
                lasth = th
            repb.consumed(lasth)
        xp["uc"] = [lasth]
        xp["m1on"] = [t_umm]

        zt_toks = {}
        zq_toks = {}
        zsb = {0: (self.pb[4], self.pb[5]), 1: (self.pb[6], self.pb[7])}
        zdone = {q: 0 for q in range(NTHT)}

        def z_stat(idx):
            ch, q = units[idx]
            s1, tsq = zq_toks[(ch, q)]
            sumb, sqb = zsb[q]
            last = zdone[q] == NCH - 1
            mm(sumb, sumb.ap()[0:1, 0:TT], self.ivg2[:, e_ * 8 + ch:e_ * 8 + ch + 1],
               self.slab2[:, 8192 + ch * TH + q * TT: 8192 + ch * TH + q * TT + TT],
               stop=last, waits=[zt_toks[(ch, q)]], inc=("PE", 1))
            tqm = mm(sqb, sqb.ap()[0:1, 0:TT],
                     self.ivg2sq[:, e_ * 8 + ch:e_ * 8 + ch + 1],
                     self.tmpbf[:, s1 * TT:(s1 + 1) * TT], stop=last,
                     waits=[tsq], inc=("PE", 1))
            zdone[q] += 1
            self.bfs.done(s1, tqm)

        for idx, (ch, q) in enumerate(units):
            bank = self.pb[idx % 4]
            ps = bank.ap()
            for ci in range(8):
                w = [h_toks[(ci, q)]]
                if idx == 0 and ci == 0:
                    w.append(t_w2g)
                mm(bank, ps[:, 0:TT], W2L(ci, ch),
                   self.slab2[:, ci * TH + q * TT: ci * TH + (q + 1) * TT],
                   stop=False, waits=w)
            tmm = mm(bank, ps[:, 0:TT], self.g2b2row[0:1, ch * 128:(ch + 1) * 128],
                     self.ones[0:1, 0:TT], stop=True, waits=[tg2b2], inc=("PE", 1))
            zcol = 8192 + ch * TH + q * TT
            tze = a(lambda e, ps=ps, d=zcol: e.copy(self.slab2[:, d:d + TT], ps[:, 0:TT]),
                    waits=[tmm] + (xp["zt"] if idx == 0 else []))
            bank.consumed(tze)
            s1, w1_ = self.bfs.acquire()
            tsq = v(lambda e, d=zcol, s=s1: e.tensor_mul(
                self.tmpbf[:, s * TT:(s + 1) * TT], self.slab2[:, d:d + TT],
                self.slab2[:, d:d + TT]), waits=[tze] + w1_)
            zt_toks[(ch, q)] = tze
            zq_toks[(ch, q)] = (s1, tsq)
            if idx >= 6:
                z_stat(idx - 6)
        t_zmm = em.tok("PE")
        for idx in range(len(units) - 6, len(units)):
            z_stat(idx)
        t_zq = em.tok("PE")
        tacc = acc_tok
        m2a_toks = []
        for q in range(NTHT):
            sumb, sqb = zsb[q]
            r0, wr0 = self.rfs.acquire()
            r1_, wr1 = self.rfs.acquire()
            r2_, wr2 = self.rfs.acquire()
            R0 = self.rowf[0:1, r0 * TT:(r0 + 1) * TT]
            R1 = self.rowf[0:1, r1_ * TT:(r1_ + 1) * TT]
            R2 = self.rowf[0:1, r2_ * TT:(r2_ + 1) * TT]
            tm2_ = v(lambda e, R0=R0, sumb=sumb: e.tensor_scalar(
                R0, sumb.ap()[0:1, 0:TT], 1.0 / 1024.0, None, ALU.mult),
                waits=[t_zq] + wr0)
            tE2 = v(lambda e, R1=R1, sqb=sqb: e.tensor_scalar(
                R1, sqb.ap()[0:1, 0:TT], 1.0 / 1024.0, None, ALU.mult),
                waits=[t_zq] + wr1)
            sumb.consumed(tm2_)
            sqb.consumed(tE2)
            tsq2 = v(lambda e, R0=R0, R2=R2: e.tensor_mul(R2, R0, R0), waits=[tm2_] + wr2)
            tvv = v(lambda e, R1=R1, R2=R2: e.tensor_sub(R1, R1, R2), waits=[tE2, tsq2])
            tl = a(lambda e, R1=R1: e.activation(R1, R1, AF.Ln, bias=EPS), waits=[tvv])
            tr2_ = a(lambda e, R1=R1: e.activation(R1, R1, AF.Exp, scale=-0.5), waits=[tl])
            tA = v(lambda e, R1=R1, R2=R2, q=q: e.tensor_tensor(
                R2, self.wcat[0:1, e_ * TH + q * TT: e_ * TH + (q + 1) * TT],
                R1, ALU.mult), waits=[tr2_, self.t_gate])
            b0, wb0 = self.rbs.acquire()
            RB0 = self.rowb[0:1, b0 * TT:(b0 + 1) * TT]
            tAb = v(lambda e, RB0=RB0, R2=R2: e.tensor_copy(RB0, R2), waits=[tA] + wb0)
            tma = v(lambda e, R0=R0, R2=R2: e.tensor_mul(R0, R0, R2), waits=[tA, tsq2])
            tmab = v(lambda e, R0=R0, q=q: e.tensor_scalar(
                self.m2acat[0:1, e_ * TH + q * TT: e_ * TH + (q + 1) * TT], R0,
                -1.0, None, ALU.mult), waits=[tma] + xp["m2acat"])
            m2a_toks.append(tmab)
            self.rfs.done(r0, tmab)
            self.rfs.done(r1_, tA)
            self.rfs.done(r2_, tma)
            repb = sumb
            prp = repb.ap()
            t_rep = mm(repb, prp[:, 0:TT], self.ones[0:1, 0:128], RB0, stop=True,
                       waits=[tAb], inc=("PE", 1))
            self.rbs.done(b0, t_rep)
            lastc = None
            for ch in range(NCH):
                zcol = 8192 + ch * TH + q * TT
                s1, w1_ = self.bfs.acquire()
                tmul = v(lambda e, d=zcol, s=s1, prp=prp: e.tensor_mul(
                    self.tmpbf[:, s * TT:(s + 1) * TT], self.slab2[:, d:d + TT],
                    prp[:, 0:TT]), waits=[t_rep, zt_toks[(ch, q)]] + w1_)
                tadd = p(lambda e, ch=ch, q=q, s=s1: e.tensor_tensor(
                    self.slab3[:, ch * TH + q * TT: ch * TH + (q + 1) * TT],
                    self.slab3[:, ch * TH + q * TT: ch * TH + (q + 1) * TT],
                    self.tmpbf[:, s * TT:(s + 1) * TT], ALU.add),
                    waits=[tmul, acc_tok])
                self.bfs.done(s1, tadd)
                lastc = tadd
            repb.consumed(lastc)
            tacc = lastc
        xp["zt"] = [lastc]
        self.m2a_last = m2a_toks
        xp["g2b2row"] = [t_zmm]
        xp["g1g1b1"] = [t_umm, tw22]
        prev_w_pe[ws] = t_umm
        self.last_umm = t_umm
        prev_w_pe[ows] = t_zmm
        return tacc

    def final_ln(self, hh, acc_tok):
        em, mm, v, a = self.em, self.mm, self.v, self.a
        tb0 = hh * TH
        last_out = None
        rko = {0: 0, 1: 520}
        w0 = [getattr(self, "prev_rnk", None), self.t_gate,
              getattr(self, "last_umm", None)] + self.m2a_last
        tstk = None
        for q in range(NTHT):
            for e_ in range(NE):
                tstk = self.dma("gpsimd",
                                self.kv_bf[2 * e_:2 * e_ + 1, rko[q]:rko[q] + TT],
                                self.m2acat[0:1, e_ * TH + q * TT: e_ * TH + (q + 1) * TT],
                                sem="SG", waits=w0 if tstk is None else [])
                tstk = self.dma("gpsimd",
                                self.kv_bf[2 * e_ + 1:2 * e_ + 2, rko[q]:rko[q] + TT],
                                self.wcat[0:1, e_ * TH + q * TT: e_ * TH + (q + 1) * TT],
                                sem="SG")
        self.xp["m2acat"] = [tstk]
        units = [(q, ch) for q in range(NTHT) for ch in range(NCH)]
        sf_toks = {}
        sq_toks = {}
        zsb = {0: (self.pb[4], self.pb[5]), 1: (self.pb[6], self.pb[7])}
        fdone = {q: 0 for q in range(NTHT)}

        def fstat(idx):
            q, ch = units[idx]
            s1, tsq = sq_toks[(q, ch)]
            sumb, sqb = zsb[q]
            lastf = fdone[q] == NCH - 1
            mm(sumb, sumb.ap()[0:1, 0:TT], self.ones[:, 0:1],
               self.slab2[:, ch * TH + q * TT: ch * TH + (q + 1) * TT],
               stop=lastf, waits=[sf_toks[(q, ch)]], inc=("PE", 1))
            tqm = mm(sqb, sqb.ap()[0:1, 0:TT], self.ones[:, 0:1],
                     self.tmpbf[:, s1 * TT:(s1 + 1) * TT], stop=lastf,
                     waits=[tsq], inc=("PE", 1))
            fdone[q] += 1
            self.bfs.done(s1, tqm)

        for idx, (q, ch) in enumerate(units):
            bank = self.pb[idx % 4]
            ps = bank.ap()
            trk = mm(bank, ps[:, 0:TT], self.rklb[0:8, ch * 128:(ch + 1) * 128],
                     self.kv_bf[0:8, rko[q]:rko[q] + TT], stop=True,
                     waits=[tstk, self.t_casts], inc=("PE", 1))
            tsf = v(lambda e, ps=ps, ch=ch, q=q: e.tensor_add(
                self.slab2[:, ch * TH + q * TT: ch * TH + (q + 1) * TT], ps[:, 0:TT],
                self.slab3[:, ch * TH + q * TT: ch * TH + (q + 1) * TT]),
                waits=[trk, acc_tok])
            sf_toks[(q, ch)] = tsf
            bank.consumed(tsf)
            s1, w1_ = self.bfs.acquire()
            tsq = v(lambda e, ch=ch, q=q, s=s1: e.tensor_mul(
                self.tmpbf[:, s * TT:(s + 1) * TT],
                self.slab2[:, ch * TH + q * TT: ch * TH + (q + 1) * TT],
                self.slab2[:, ch * TH + q * TT: ch * TH + (q + 1) * TT]),
                waits=[tsf] + w1_)
            sq_toks[(q, ch)] = (s1, tsq)
            if idx >= 6:
                fstat(idx - 6)
        for idx in range(len(units) - 6, len(units)):
            fstat(idx)
        self.prev_rnk = em.tok("PE")
        tstat = em.tok("PE")
        repbanks = {0: (self.pb[4], self.pb[5]), 1: (self.pb[6], self.pb[7])}
        for q in range(NTHT):
            sumb, sqb = zsb[q]
            r0, wr0 = self.rfs.acquire()
            r1_, wr1 = self.rfs.acquire()
            r2_, wr2 = self.rfs.acquire()
            R0 = self.rowf[0:1, r0 * TT:(r0 + 1) * TT]
            R1 = self.rowf[0:1, r1_ * TT:(r1_ + 1) * TT]
            R2 = self.rowf[0:1, r2_ * TT:(r2_ + 1) * TT]
            tm_ = v(lambda e, R0=R0, sumb=sumb: e.tensor_scalar(
                R0, sumb.ap()[0:1, 0:TT], 1.0 / 1024.0, None, ALU.mult),
                waits=[tstat] + wr0)
            tv_ = v(lambda e, R1=R1, sqb=sqb: e.tensor_scalar(
                R1, sqb.ap()[0:1, 0:TT], 1.0 / 1024.0, None, ALU.mult),
                waits=[tstat] + wr1)
            sumb.consumed(tm_)
            sqb.consumed(tv_)
            tm2 = v(lambda e, R0=R0, R2=R2: e.tensor_mul(R2, R0, R0), waits=[tm_] + wr2)
            tvv = v(lambda e, R1=R1, R2=R2: e.tensor_sub(R1, R1, R2), waits=[tv_, tm2])
            tl = a(lambda e, R1=R1: e.activation(R1, R1, AF.Ln, bias=EPS), waits=[tvv])
            tr_ = a(lambda e, R1=R1: e.activation(R1, R1, AF.Exp, scale=-0.5), waits=[tl])
            tmr = v(lambda e, R0=R0, R1=R1, R2=R2: e.tensor_mul(R2, R0, R1), waits=[tr_])
            b0, wb0 = self.rbs.acquire()
            b1, wb1 = self.rbs.acquire()
            RB0 = self.rowb[0:1, b0 * TT:(b0 + 1) * TT]
            RB1 = self.rowb[0:1, b1 * TT:(b1 + 1) * TT]
            tbr = v(lambda e, RB0=RB0, R1=R1: e.tensor_copy(RB0, R1), waits=[tr_] + wb0)
            tbm = v(lambda e, RB1=RB1, R2=R2: e.tensor_copy(RB1, R2), waits=[tmr] + wb1)
            self.rfs.done(r0, tmr)
            self.rfs.done(r1_, tbr)
            self.rfs.done(r2_, tbm)
            rrb, rmb = repbanks[q]
            prr, prm = rrb.ap(), rmb.ap()
            t_rr = mm(rrb, prr[:, 0:TT], self.ones[0:1, 0:128], RB0, stop=True,
                      waits=[tbr], inc=("PE", 1))
            t_rm = mm(rmb, prm[:, 0:TT], self.ones[0:1, 0:128], RB1, stop=True,
                      waits=[tbm], inc=("PE", 1))
            self.rbs.done(b0, t_rr)
            self.rbs.done(b1, t_rm)
            for ch in range(NCH):
                f0, wf0 = self.ffs.acquire()
                f1, wf1 = self.ffs.acquire()
                t1 = v(lambda e, ch=ch, q=q, f=f0, prr=prr: e.tensor_mul(
                    self.tmpf[:, f * TT:(f + 1) * TT],
                    self.slab2[:, ch * TH + q * TT: ch * TH + (q + 1) * TT], prr[:, 0:TT]),
                    waits=[t_rr] + wf0)
                to = v(lambda e, f0=f0, f1=f1, prm=prm: e.tensor_sub(
                    self.tmpf[:, f1 * TT:(f1 + 1) * TT], self.tmpf[:, f0 * TT:(f0 + 1) * TT],
                    prm[:, 0:TT]), waits=[t_rm, t1] + wf1)
                self.ffs.done(f0, to)
                tdo = self.dma("gpsimd", self.d_out[ch * 128:(ch + 1) * 128,
                                                    tb0 + q * TT: tb0 + (q + 1) * TT],
                               self.tmpf[:, f1 * TT:(f1 + 1) * TT], sem="ST", waits=[to])
                self.ffs.done(f1, tdo)
                last_out = tdo
            rrb.consumed(last_out)
            rmb.consumed(last_out)
        return last_out

    def finish(self):
        em = self.em
        if self.dbg is not None:
            slab = {"k": self.slab2, "q": self.slab2, "v": self.slab3,
                    "attnmix": self.slab3, "y": self.slab5}[self.dbg]
            last = {"k": getattr(self, "tk_done", None), "q": getattr(self, "q_done", None),
                    "v": getattr(self, "tv_done", None),
                    "attnmix": getattr(self, "am_done", None),
                    "y": getattr(self, "y_done", None)}[self.dbg]
            tdma = None
            for ch in range(8):
                for w_ in range(TOK // TT):
                    sl = w_ % 2
                    wl = [last] + ([tdma] if tdma else [])
                    tc = self.v(lambda e, ch=ch, w_=w_, s=sl: e.tensor_copy(
                        self.tmpf[:, s * TT:(s + 1) * TT],
                        slab[:, ch * TOK + w_ * TT: ch * TOK + (w_ + 1) * TT]), waits=wl)
                    tdma = self.dma("gpsimd",
                                    self.d_out[ch * 128:(ch + 1) * 128,
                                               w_ * TT:(w_ + 1) * TT],
                                    self.tmpf[:, sl * TT:(sl + 1) * TT], sem="ST",
                                    waits=[tc])
        em.op("gpsimd", lambda e: e.engine_nop(),
              waits=[em.tok("SG"), em.tok("LD"), em.tok("LR"), em.tok("ST"),
                     em.tok("CC")])
        handles = {}
        cms = []
        for name in ("LD", "LR", "PE", "DV", "AC", "PL", "SG", "ST", "CC"):
            cm = self.nc.semaphore(name)
            handles[name] = cm.__enter__()
            cms.append(cm)
        blk = self.nc.Block()
        block = blk.__enter__()
        self.em.replay(block, handles)
        blk.__exit__(None, None, None)
        for cm in reversed(cms):
            cm.__exit__(None, None, None)
        self.ctx.close()


def build(phase_limit=99, dbg=None):
    return MK(dbg=dbg, phase_limit=phase_limit).build()


def prep_inputs(core, inp, tok=TOK):
    asc = np.ascontiguousarray
    f = lambda a: np.asarray(a, dtype=np.float32)
    xf = f(inp["L0_attn_x"]).reshape(-1, C)
    xs = xf[core * tok:(core + 1) * tok]
    cols = lambda b: asc(f(b).reshape(8, 128).T)
    rows = lambda b: f(b).reshape(1, -1)
    catr = lambda key: asc(np.concatenate([f(inp[key][e]).reshape(1, C) for e in range(NE)], axis=1))
    colse = lambda key: asc(np.stack([f(inp[key][e]).reshape(8, 128) for e in range(NE)])
                            .transpose(2, 0, 1).reshape(128, 32))
    def colse2(key):
        v = np.stack([f(inp[key][e]).reshape(8, 128) for e in range(NE)])
        out = np.zeros((128, 32), np.float32)
        for e in range(NE):
            for ch in range(8):
                out[:, e * 8 + ch] = v[e, ch]
        return asc(out)
    gatec = asc(f(inp["L0_ffn_gate_W"]).reshape(8, 128, 4).transpose(1, 0, 2).reshape(128, 32))
    rkl = np.zeros((8, C), np.float32)
    for e in range(NE):
        rkl[2 * e] = f(inp["L0_ffn_g2"][e])
        rkl[2 * e + 1] = f(inp["L0_ffn_be2"][e])
    return dict(
        xT=asc(xs.T),
        wqT=asc(f(inp["L0_attn_Wq"]).T), wkT=asc(f(inp["L0_attn_Wk"]).T),
        wvT=asc(f(inp["L0_attn_Wv"]).T), woT=asc(f(inp["L0_attn_Wo"]).T),
        w1=asc(f(inp["L0_ffn_W1"])), w2=asc(f(inp["L0_ffn_W2"])),
        gatec=gatec, gateb=f(inp["L0_ffn_gate_b"]).reshape(1, 4),
        bq_c=cols(inp["L0_attn_bq"]), bo_c=cols(inp["L0_attn_bo"]),
        bk_r=rows(inp["L0_attn_bk"]), bv_r=rows(inp["L0_attn_bv"]),
        b1cat=catr("L0_ffn_b1"), g1cat=catr("L0_ffn_g1"),
        g2cat=catr("L0_ffn_g2"), b2cat=catr("L0_ffn_b2"),
        g1c=colse2("L0_ffn_g1"), be1c=colse2("L0_ffn_be1"), g2c=colse2("L0_ffn_g2"),
        rkl=asc(rkl),
        idn=np.eye(128, dtype=np.float32),
        cvec=np.tile(np.array([[EPS, -0.5]], np.float32), (128, 1)),
    )


from concourse.bass_utils import run_bass_kernel_spmd


def kernel(**inputs):
    configure(tok=2048, th=1024, rg=[[0, 1, 2, 3], [4, 5, 6, 7]])
    nc = build(phase_limit=99)
    in_maps = [prep_inputs(c, inputs, tok=2048) for c in range(8)]
    res = run_bass_kernel_spmd(nc, in_maps, core_ids=list(range(8)))
    outs = [np.asarray(res.results[c]["outT"]) for c in range(8)]
    full = np.concatenate([o.T for o in outs], axis=0)
    return np.ascontiguousarray(full.reshape(2, 8192, 1024)).astype(np.float32)


# version 13
# speedup vs baseline: 1.0831x; 1.0185x over previous
import numpy as np
import concourse.bass as bass
import concourse.mybir as mybir

F32 = mybir.dt.float32
BF16 = mybir.dt.bfloat16
AF = mybir.ActivationFunctionType
ALU = mybir.AluOpType
AX = mybir.AxisListType

C = 1024
NCH = 8
NE = 4
EPS = 1e-5

TOK = 2048
TT = 512
TH = 1024
RG = [[0, 1, 2, 3], [4, 5, 6, 7]]


def configure(tok=2048, th=1024, rg=None):
    global TOK, TH, RG, NTT, NTH, NTHT, NTC
    TOK, TH = tok, th
    if rg is not None:
        RG = rg
    NTT = TOK // TT
    NTH = TOK // TH
    NTHT = TH // TT
    NTC = TOK // 128


configure()

ENGS = ("sync", "tensor", "vector", "scalar", "gpsimd")


class Em:
    def __init__(self):
        self.prog = {e: [] for e in ENGS}
        self.cnt = {}

    def newsem(self, name):
        self.cnt[name] = 0

    def op(self, eng, fn, inc=None, waits=()):
        ws = []
        for t in waits:
            if t is None:
                continue
            (s, v) = t
            if v > 0:
                ws.append((s, v))
        self.prog[eng].append((tuple(ws), fn, inc))
        if inc is not None:
            self.cnt[inc[0]] += inc[1]
            return (inc[0], self.cnt[inc[0]])
        return None

    def tok(self, sem):
        return (sem, self.cnt[sem])

    def replay(self, block, semh):
        def mk(name):
            def body(e):
                lastw = {}
                for ws, fn, inc in self.prog[name]:
                    for (s, v) in ws:
                        if lastw.get(s, 0) < v:
                            e.wait_ge(semh[s], v)
                            lastw[s] = v
                    ins = fn(e)
                    if inc is not None:
                        ins.then_inc(semh[inc[0]], inc[1])
            return body
        block.sync(mk("sync"))
        block.tensor(mk("tensor"))
        block.vector(mk("vector"))
        block.scalar(mk("scalar"))
        block.gpsimd(mk("gpsimd"))


class PBank:
    def __init__(self, ap):
        self.t = ap
        self.opened = False
        self.pending = []

    def ap(self):
        return self.t.ap()

    def begin(self):
        if not self.opened:
            self.opened = True
            w = self.pending
            self.pending = []
            return True, w
        return False, []

    def consumed(self, tok):
        if tok is not None:
            self.pending.append(tok)
        self.opened = False


class Slots:
    def __init__(self, n):
        self.n = n
        self.last = [None] * n
        self.i = 0

    def acquire(self):
        i = self.i % self.n
        self.i += 1
        w = self.last[i]
        self.last[i] = None
        return i, ([w] if w else [])

    def done(self, i, tok):
        self.last[i] = tok


class Row:
    def __init__(self, t, r, off, n, rows=1):
        self.t, self.r, self.off, self.n, self.rows = t, r, off, n, rows

    def __getitem__(self, idx):
        sp, sf = idx
        p0 = sp.start or 0
        p1 = sp.stop if sp.stop is not None else self.rows
        f0 = sf.start or 0
        f1 = sf.stop if sf.stop is not None else self.n
        return self.t[self.r + p0:self.r + p1, self.off + f0:self.off + f1]


class MK:
    def __init__(self, dbg=None, phase_limit=99):
        self.dbg = dbg
        self.phase_limit = phase_limit
        self.nc = bass.Bass("TRN2", target_bir_lowering=False, debug=False)
        self.em = Em()
        for s in ("LD", "LR", "PE", "DV", "AC", "PL", "SG", "ST", "CC"):
            self.em.newsem(s)

    def dma(self, queue, out, in_, sem="LD", waits=()):
        return self.em.op(queue, lambda e, o=out, i=in_: e.dma_start(out=o, in_=i),
                          inc=(sem, 16), waits=waits)

    def mm(self, bank, out, lhsT, rhs, stop, waits=(), inc=None, tile=None, fstart=None):
        st, bw = bank.begin()
        if fstart is not None:
            st = fstart
        return self.em.op("tensor",
                          lambda e, o=out, l=lhsT, r=rhs, s=st, sp=stop, tp=tile: e.matmul(
                              o, l, r, start=s, stop=sp, tile_position=tp),
                          inc=inc, waits=list(waits) + bw)

    def v(self, fn, waits=()):
        return self.em.op("vector", fn, inc=("DV", 1), waits=waits)

    def a(self, fn, waits=()):
        return self.em.op("scalar", fn, inc=("AC", 1), waits=waits)

    def p(self, fn, waits=()):
        return self.em.op("gpsimd", fn, inc=("PL", 1), waits=waits)

    def bar(self):
        e = self.em
        return [e.tok("PE"), e.tok("DV"), e.tok("AC"), e.tok("PL")]

    def build(self):
        nc = self.nc
        ap = lambda t: t.ap()
        self.d_xT = ap(nc.dram_tensor("xT", [C, TOK], F32, kind="ExternalInput"))
        self.d_wqT = ap(nc.dram_tensor("wqT", [C, C], F32, kind="ExternalInput"))
        self.d_wkT = ap(nc.dram_tensor("wkT", [C, C], F32, kind="ExternalInput"))
        self.d_wvT = ap(nc.dram_tensor("wvT", [C, C], F32, kind="ExternalInput"))
        self.d_woT = ap(nc.dram_tensor("woT", [C, C], F32, kind="ExternalInput"))
        self.d_w1 = ap(nc.dram_tensor("w1", [NE, C, C], F32, kind="ExternalInput"))
        self.d_w2 = ap(nc.dram_tensor("w2", [NE, C, C], F32, kind="ExternalInput"))
        self.d_gatec = ap(nc.dram_tensor("gatec", [128, 32], F32, kind="ExternalInput"))
        self.d_gateb = ap(nc.dram_tensor("gateb", [1, 4], F32, kind="ExternalInput"))
        self.d_bq_c = ap(nc.dram_tensor("bq_c", [128, 8], F32, kind="ExternalInput"))
        self.d_bo_c = ap(nc.dram_tensor("bo_c", [128, 8], F32, kind="ExternalInput"))
        self.d_bk_r = ap(nc.dram_tensor("bk_r", [1, C], F32, kind="ExternalInput"))
        self.d_bv_r = ap(nc.dram_tensor("bv_r", [1, C], F32, kind="ExternalInput"))
        self.d_b1cat = ap(nc.dram_tensor("b1cat", [1, NE * C], F32, kind="ExternalInput"))
        self.d_g1cat = ap(nc.dram_tensor("g1cat", [1, NE * C], F32, kind="ExternalInput"))
        self.d_g2cat = ap(nc.dram_tensor("g2cat", [1, NE * C], F32, kind="ExternalInput"))
        self.d_b2cat = ap(nc.dram_tensor("b2cat", [1, NE * C], F32, kind="ExternalInput"))
        self.d_g1c = ap(nc.dram_tensor("g1c", [128, 32], F32, kind="ExternalInput"))
        self.d_be1c = ap(nc.dram_tensor("be1c", [128, 32], F32, kind="ExternalInput"))
        self.d_g2c = ap(nc.dram_tensor("g2c", [128, 32], F32, kind="ExternalInput"))
        self.d_rkl = ap(nc.dram_tensor("rkl", [8, C], F32, kind="ExternalInput"))
        self.d_idn = ap(nc.dram_tensor("idn", [128, 128], F32, kind="ExternalInput"))
        self.d_cvec = ap(nc.dram_tensor("cvec", [128, 2], F32, kind="ExternalInput"))
        self.d_out = ap(nc.dram_tensor("outT", [C, TOK], F32, kind="ExternalOutput"))
        self.d_ccin = ap(nc.dram_tensor("ccin", [8 * 128, 129], F32, kind="Internal"))
        self.d_ccout = ap(nc.dram_tensor("ccout", [8 * 128, 129], F32, kind="Internal"))

        from contextlib import ExitStack
        self.ctx = ExitStack()
        sb = lambda n, s, d: self.ctx.enter_context(nc.sbuf_tensor(n, s, d))
        SLAB = max(NCH * TOK, 16384)
        self.slab1 = sb("slab1", [128, SLAB], BF16)
        self.slab2 = sb("slab2", [128, SLAB], BF16)
        self.slab3 = sb("slab3", [128, SLAB], BF16)
        self.slab4 = sb("slab4", [128, 4096], F32)
        self.slab5 = sb("slab5", [128, SLAB], BF16)
        self.tmpbf = sb("tmpbf", [128, 6 * TT], BF16)
        self.tmpf = sb("tmpf", [128, 2 * TT], F32)
        self.kvf = sb("kvf", [128, 8 * 130], F32)
        self.kv_bf = sb("kv_bf", [128, 8 * 130], BF16)
        self.g1g1b1 = sb("g1g1b1", [2, C], BF16)
        self.ones = sb("ones", [128, 512], BF16)
        self.idnb = sb("idnb", [128, 128], BF16)
        self.cvec = sb("cvecs", [128, 2], F32)
        self.bq_c = sb("bq_cs", [128, 8], F32)
        self.bo_c = sb("bo_cs", [128, 8], F32)
        self.g1c = sb("g1cs", [128, 32], F32)
        self.be1c = sb("be1cs", [128, 32], F32)
        self.g2c = sb("g2cs", [128, 32], F32)
        self.invg1 = sb("invg1", [128, 32], F32)
        self.ivg1sq = sb("ivg1sq", [128, 32], BF16)
        self.ivg2 = sb("ivg2", [128, 32], BF16)
        self.ivg2sq = sb("ivg2sq", [128, 32], BF16)
        self.gsc = sb("gsc", [128, 32], F32)
        self.gatec = sb("gatecs", [128, 32], BF16)
        self.gatecf = sb("gatecf", [128, 32], F32)
        self.rklb = sb("rklb", [8, C], BF16)
        self.w1s = sb("w1s", [128, 8], BF16)
        self.w1sf = sb("w1sf", [128, 8], F32)
        self.wtm = sb("wtm", [128, 4], BF16)
        self.rowar = sb("rowar", [1, 3 * TT + TH + 2 * C + 8], F32)
        o = 0
        self.rowf = Row(self.rowar, 0, 0, 3 * TT); o += 3 * TT
        self.m1f = Row(self.rowar, 0, o, TH); o += TH
        self.rowpair = Row(self.rowar, 0, o, 2 * C); o += 2 * C
        self.sb1s = Row(self.rowar, 0, o, 4); o += 4
        self.gatebf = Row(self.rowar, 0, o, 4)
        self.rowab = sb("rowab", [1, 2 * NE * TH + 3 * TT + 2 * C + 8], BF16)
        o = 0
        self.wcat = Row(self.rowab, 0, 0, NE * TH); o += NE * TH
        self.m2acat = Row(self.rowab, 0, o, NE * TH); o += NE * TH
        self.rowb = Row(self.rowab, 0, o, 3 * TT); o += 3 * TT
        self.g1b1row = Row(self.rowab, 0, o, C); o += C
        self.g2b2row = Row(self.rowab, 0, o, C); o += C
        self.gateb = Row(self.rowab, 0, o, 4)
        self.bk_r = Row(self.g1g1b1, 0, 0, C)
        self.bv_r = Row(self.kv_bf, 0, 0, C)
        self.rnkr = self.kv_bf
        nc.const_aps.register(F32, EPS, self.cvec[:, 0:1])
        nc.const_aps.register(F32, -0.5, self.cvec[:, 1:2])

        self.pb = [PBank(self.ctx.enter_context(nc.psum_tensor(f"pb{i}", [128, 512], F32)))
                   for i in range(8)]
        self.bfs = Slots(6)
        self.ffs = Slots(2)
        self.rfs = Slots(3)
        self.rbs = Slots(3)
        self.stage_free = []

        self.phase0()
        if self.phase_limit >= 1:
            self.tk_done = self.proj_tm(0, self.t_wk, self.bk_r, self.slab2, True)
            self.tv_done = self.proj_tm(8192, self.t_wv, self.bv_r, self.slab3, False)
            self.t_wq = self.load_weight(self.d_wqT, self.slab5, 0,
                                         extra_wait=self.em.tok("PE"))
        if self.phase_limit >= 2:
            self.phaseA_kv_collective()
            self.phaseA_q()
        if self.phase_limit >= 3:
            self.phaseA_attn()
        if self.phase_limit >= 4:
            self.phaseA_wo_ln1()
        if self.phase_limit >= 5:
            self.phaseB()
        self.finish()
        return nc

    def phase0(self):
        dma, v = self.dma, self.v
        t_idn = dma("sync", self.slab4[0:128, 0:128], self.d_idn)
        t_rkl = dma("sync", self.slab4[0:8, 128:1152], self.d_rkl)
        dma("sync", self.cvec[:], self.d_cvec)
        t_bk = dma("sync", self.slab4[0:1, 1152:2176], self.d_bk_r)
        t_bv = dma("sync", self.slab4[0:1, 2176:3200], self.d_bv_r)
        toks = []
        for (dst, src) in [(self.bq_c[:, :], self.d_bq_c), (self.bo_c[:, :], self.d_bo_c),
                           (self.g1c[:, :], self.d_g1c), (self.be1c[:, :], self.d_be1c),
                           (self.g2c[:, :], self.d_g2c), (self.gatecf[:, :], self.d_gatec),
                           (self.gatebf[0:1, :], self.d_gateb)]:
            toks.append(dma("sync", dst, src))
        t_sm = toks[-1]
        v(lambda e: e.memset(self.ones[:], 1.0))
        v(lambda e: e.memset(self.kvf[:], 0.0))
        t_idc = v(lambda e: e.tensor_copy(self.idnb[:], self.slab4[0:128, 0:128]),
                  waits=[t_sm])
        t_rkc = v(lambda e: e.tensor_copy(self.rklb[:], self.slab4[0:8, 128:1152]),
                  waits=[t_sm])
        t_bkc = v(lambda e: e.tensor_copy(self.bk_r[0:1, :], self.slab4[0:1, 1152:2176]),
                  waits=[t_sm])
        t_bvc = v(lambda e: e.tensor_copy(self.bv_r[0:1, :], self.slab4[0:1, 2176:3200]),
                  waits=[t_sm])
        tc = v(lambda e: e.tensor_copy(self.gatec[:, :], self.gatecf[:, :]), waits=[t_sm])
        tc = v(lambda e: e.tensor_copy(self.gateb[0:1, :], self.gatebf[0:1, :]), waits=[t_sm])
        self.t_casts = tc
        self.t_stage0 = [t_idc, t_rkc, t_bkc, t_bvc]
        t1 = v(lambda e: e.tensor_copy(self.ivg2[:], self.g1c[:]), waits=[t_sm])
        t2 = v(lambda e: e.tensor_copy(self.gsc[:], self.ivg2[:]), waits=[t1])
        t3 = v(lambda e: e.reciprocal(self.invg1[:], self.gsc[:]), waits=[t2])
        t4 = v(lambda e: e.tensor_mul(self.gsc[:], self.invg1[:], self.invg1[:]), waits=[t3])
        t5 = v(lambda e: e.tensor_copy(self.ivg1sq[:], self.gsc[:]), waits=[t4])
        t6 = v(lambda e: e.tensor_copy(self.ivg2[:], self.g2c[:]), waits=[t5])
        t7 = v(lambda e: e.tensor_copy(self.gsc[:], self.ivg2[:]), waits=[t6])
        t8 = v(lambda e: e.reciprocal(self.gsc[:], self.gsc[:]), waits=[t7])
        t9 = v(lambda e: e.tensor_copy(self.ivg2[:], self.gsc[:]), waits=[t8])
        t10 = v(lambda e: e.tensor_mul(self.gsc[:], self.gsc[:], self.gsc[:]), waits=[t9])
        self.t_ginv = v(lambda e: e.tensor_copy(self.ivg2sq[:], self.gsc[:]), waits=[t10])
        XS = 2048 if TOK > 2048 else TOK
        gate = list(self.t_stage0)
        tc = None
        for w in range(4):
            for s in range(2):
                dma("sync", self.slab4[:, s * XS:s * XS + TOK],
                    self.d_xT[(2 * w + s) * 128:(2 * w + s + 1) * 128, :],
                    waits=gate if s == 0 else [])
            tokw = self.em.tok("LD")
            gate = []
            for s in range(2):
                tc = v(lambda e, k=2 * w + s, s=s: e.tensor_copy(
                    self.slab1[:, k * TOK:(k + 1) * TOK],
                    self.slab4[:, s * XS:s * XS + TOK]), waits=[tokw])
                gate.append(tc)
        self.t_xbf = tc
        self.stage_free = [tc] + gate
        self.stage_slot_free = None
        self.t_wk = self.load_weight(self.d_wkT, self.slab5, 0)
        self.t_wv = self.load_weight(self.d_wvT, self.slab5, 8192)

    def load_weight(self, dsrc, dest, doff, scale=None, w1s=False,
                    extra_wait=None, cast_waits=()):
        t = None
        tw = None
        gate = list(self.stage_free)
        if extra_wait is not None:
            gate.append(extra_wait)
        for wave in range(2):
            for j in range(4):
                k = wave * 4 + j
                self.dma("sync", self.slab4[:, j * 1024:(j + 1) * 1024],
                         dsrc[k * 128:(k + 1) * 128, :], waits=gate if j == 0 else [])
            tokw = self.em.tok("LD")
            gate = []
            for j in range(4):
                k = wave * 4 + j
                cw = list(cast_waits) if k == 0 else []
                if w1s:
                    tw = self.v(lambda e, k=k, j=j: e.tensor_reduce(
                        self.w1sf[:, k:k + 1], self.slab4[:, j * 1024:(j + 1) * 1024],
                        AX.X, ALU.add), waits=[tokw] + cw)
                    gate.append(tw)
                if scale is None:
                    t = self.p(lambda e, k=k, j=j: e.tensor_copy(
                        dest[:, doff + k * 1024: doff + (k + 1) * 1024],
                        self.slab4[:, j * 1024:(j + 1) * 1024]), waits=[tokw] + cw)
                else:
                    t = self.p(lambda e, k=k, j=j, g=scale: e.tensor_tensor(
                        dest[:, doff + k * 1024: doff + (k + 1) * 1024],
                        self.slab4[:, j * 1024:(j + 1) * 1024], g, ALU.mult),
                        waits=[tokw] + cw)
                gate.append(t)
        self.stage_free = gate
        if w1s:
            self.t_w1s = self.v(lambda e: e.tensor_copy(self.w1s[:], self.w1sf[:]),
                                waits=[tw, getattr(self, "w1s_free", None)])
        return t

    def proj_tm(self, wofs, w_tok, biasrow, dest, do_elu):
        mm, v, a = self.mm, self.v, self.a
        last = None
        for j in range(NTC):
            for h2 in range(2):
                bank = self.pb[(j * 2 + h2) % 4]
                ps = bank.ap()
                for ci in range(8):
                    w = [w_tok, self.t_xbf] if ci == 0 else []
                    mm(bank, ps[:, 0:TT],
                       self.slab1[:, ci * TOK + j * 128: ci * TOK + (j + 1) * 128],
                       self.slab5[:, wofs + ci * 1024 + h2 * 512: wofs + ci * 1024 + h2 * 512 + 512],
                       stop=False, waits=w)
                tmm = mm(bank, ps[:, 0:TT], self.ones[0:1, 0:128],
                         biasrow[0:1, h2 * 512: h2 * 512 + 512], stop=True, inc=("PE", 1))
                dcol = j * 1024 + h2 * 512
                if do_elu:
                    s1, w1_ = self.bfs.acquire()
                    s2, w2_ = self.bfs.acquire()
                    ta = a(lambda e, ps=ps, s=s1: e.activation(
                        self.tmpbf[:, s * TT:(s + 1) * TT], ps[:, 0:TT], AF.Exp),
                        waits=[tmm] + w1_)
                    t1 = v(lambda e, ps=ps, s=s2: e.tensor_scalar(
                        self.tmpbf[:, s * TT:(s + 1) * TT], ps[:, 0:TT], 0.0, None, ALU.max),
                        waits=[ta] + w2_)
                    tk = v(lambda e, s1=s1, s2=s2, d=dcol: e.scalar_tensor_tensor(
                        dest[:, d:d + 512], self.tmpbf[:, s1 * TT:(s1 + 1) * TT], 1.0,
                        self.tmpbf[:, s2 * TT:(s2 + 1) * TT], ALU.min, ALU.add),
                        waits=[ta, t1])
                    self.bfs.done(s1, tk)
                    self.bfs.done(s2, tk)
                    bank.consumed(ta)
                    bank.consumed(t1)
                    last = tk
                else:
                    tk = a(lambda e, ps=ps, d=dcol: e.copy(dest[:, d:d + 512], ps[:, 0:TT]),
                           waits=[tmm])
                    bank.consumed(tk)
                    last = tk
        return last

    def phaseA_kv_collective(self):
        em, mm, v = self.em, self.mm, self.v
        evac = []
        for p_ in range(8):
            bank = self.pb[p_]
            ps = bank.ap()
            tmm = None
            for j in range(NTC):
                kcol = j * 1024 + p_ * 128
                w = [self.tk_done, self.tv_done] if j == 0 else []
                mm(bank, ps[:, 0:128], self.slab2[:, kcol:kcol + 128],
                   self.slab3[:, kcol:kcol + 128], stop=False, waits=w)
                tmm = mm(bank, ps[:, 128:129], self.slab2[:, kcol:kcol + 128],
                         self.ones[:, 0:1], stop=(j == NTC - 1),
                         inc=("PE", 1) if j == NTC - 1 else None)
            te = v(lambda e, p_=p_, ps=ps: e.tensor_copy(
                self.kvf[:, p_ * 130:p_ * 130 + 129], ps[:, 0:129]), waits=[tmm])
            bank.consumed(te)
            evac.append(self.dma("gpsimd", self.d_ccin[p_ * 128:(p_ + 1) * 128, :],
                                 self.kvf[:, p_ * 130:p_ * 130 + 129], sem="SG", waits=[te]))
        self.t_cc = em.op("gpsimd", lambda e: e.collective_compute(
            "AllReduce", ALU.add, RG, [self.d_ccin.opt()], [self.d_ccout.opt()]),
            inc=("CC", 1), waits=[evac[-1]])

    def phaseA_q(self):
        em, mm, v, a = self.em, self.mm, self.v, self.a
        t_kvPE = em.tok("PE")
        last = None
        for ch in range(NCH):
            for tt in range(NTT):
                bank = self.pb[(ch * NTT + tt) % 4]
                ps = bank.ap()
                for ci in range(8):
                    w = [self.t_wq] if (ch == 0 and tt == 0 and ci == 0) else []
                    mm(bank, ps[:, 0:TT],
                       self.slab5[:, ci * 1024 + ch * 128: ci * 1024 + ch * 128 + 128],
                       self.slab1[:, ci * TOK + tt * TT: ci * TOK + (tt + 1) * TT],
                       stop=(ci == 7), waits=w, inc=("PE", 1) if ci == 7 else None)
                tmm = em.tok("PE")
                s1, w1_ = self.bfs.acquire()
                s2, w2_ = self.bfs.acquire()
                ta = a(lambda e, ps=ps, ch=ch, s=s1: e.activation(
                    self.tmpbf[:, s * TT:(s + 1) * TT], ps[:, 0:TT], AF.Exp,
                    bias=self.bq_c[:, ch:ch + 1]), waits=[tmm] + w1_)
                t1 = v(lambda e, ps=ps, ch=ch, s=s2: e.tensor_scalar(
                    self.tmpbf[:, s * TT:(s + 1) * TT], ps[:, 0:TT],
                    self.bq_c[:, ch:ch + 1], 0.0, ALU.add, ALU.max), waits=[ta] + w2_)
                dcol = ch * TOK + tt * TT
                wq0 = [t_kvPE] if (ch == 0 and tt == 0) else []
                tk = v(lambda e, s1=s1, s2=s2, d=dcol: e.scalar_tensor_tensor(
                    self.slab2[:, d:d + TT], self.tmpbf[:, s1 * TT:(s1 + 1) * TT], 1.0,
                    self.tmpbf[:, s2 * TT:(s2 + 1) * TT], ALU.min, ALU.add),
                    waits=[ta, t1] + wq0)
                self.bfs.done(s1, tk)
                self.bfs.done(s2, tk)
                bank.consumed(ta)
                bank.consumed(t1)
                last = tk
        self.q_done = last
        self.t_wo = self.load_weight(self.d_woT, self.slab1, 0, extra_wait=self.em.tok("PE"))

    def phaseA_attn(self):
        em, mm, v, a = self.em, self.mm, self.v, self.a
        t2 = None
        for p_ in range(8):
            t2 = self.dma("gpsimd", self.kvf[:, p_ * 130:p_ * 130 + 129],
                          self.d_ccout[p_ * 128:(p_ + 1) * 128, :], sem="SG",
                          waits=[self.t_cc] if p_ == 0 else [])
        t_kvbf = v(lambda e: e.tensor_copy(self.kv_bf[:], self.kvf[:]), waits=[t2])
        units = [(p_, h2, tt) for p_ in range(8) for h2 in range(2) for tt in range(NTT)]
        D = 4
        st = {}
        last = None
        rslots = Slots(8)
        rrow = lambda i: self.m2acat[0:1, i * TT:(i + 1) * TT]

        def stageA(u):
            p_, h2, tt = units[u]
            bank = self.pb[u % 6]
            ps = bank.ap()
            lo, hi = h2 * 64, h2 * 64 + 64
            lh = self.kv_bf[lo:hi, p_ * 130 + h2 * 64: p_ * 130 + h2 * 64 + 64]
            ld = self.kv_bf[lo:hi, p_ * 130 + 128: p_ * 130 + 129]
            qs = self.slab2[lo:hi, p_ * TOK + tt * TT: p_ * TOK + (tt + 1) * TT]
            w = [t_kvbf, self.q_done] if u == 0 else []
            mm(bank, ps[lo:hi, 0:TT], lh, qs, stop=True, waits=w,
               tile=(h2 * 64, h2 * 64))
            dnp = 64 if h2 == 0 else 0
            tmm = mm(bank, ps[dnp:dnp + 1, 0:TT], ld, qs, stop=True,
                     inc=("PE", 1), tile=(h2 * 64, dnp), fstart=True)
            rs, wr = self.rfs.acquire()
            td = v(lambda e, ps=ps, rs=rs, dnp=dnp: e.tensor_scalar(
                self.rowf[0:1, rs * TT:rs * TT + TT], ps[dnp:dnp + 1, 0:TT], 1e-6, None,
                ALU.max), waits=[tmm] + wr)
            ri, wri = rslots.acquire()
            def _recip(e, rs=rs, ri=ri):
                with self.nc.allow_low_precision(reason="bf16 recip row feeds bf16 matmul"):
                    return e.reciprocal(rrow(ri), self.rowf[0:1, rs * TT:rs * TT + TT])
            tb = v(_recip, waits=[td] + wri)
            self.rfs.done(rs, tb)
            st[u] = (bank, ps, ri, tb, td)

        def stageC(u):
            nonlocal last
            p_, h2, tt = units[u]
            bank, ps, ri, tb, td = st.pop(u)
            lo, hi = h2 * 64, h2 * 64 + 64
            rbank = self.pb[6 + u % 2]
            rp = rbank.ap()
            trep = mm(rbank, rp[lo:hi, 0:TT], self.ones[0:1, 0:64], rrow(ri), stop=True,
                      waits=[tb], inc=("PE", 1), tile=(0, h2 * 64))
            rslots.done(ri, trep)
            sl, wsl = self.bfs.acquire()
            tev = a(lambda e, rp=rp, s=sl, lo=lo, hi=hi: e.copy(
                self.tmpbf[lo:hi, s * TT:(s + 1) * TT], rp[lo:hi, 0:TT]),
                waits=[trep] + wsl)
            rbank.consumed(tev)
            dcol = p_ * TOK + tt * TT
            tmul = v(lambda e, ps=ps, s=sl, lo=lo, hi=hi, d=dcol: e.tensor_mul(
                self.slab3[lo:hi, d:d + TT], ps[lo:hi, 0:TT],
                self.tmpbf[lo:hi, s * TT:(s + 1) * TT]), waits=[tev, td])
            self.bfs.done(sl, tmul)
            bank.consumed(tmul)
            last = tmul

        for u in range(len(units)):
            stageA(u)
            if u >= D:
                stageC(u - D)
        for u in range(len(units) - D, len(units)):
            stageC(u)
        self.am_done = last

    def phaseA_wo_ln1(self):
        em, mm, v, a = self.em, self.mm, self.v, self.a
        y_done = None
        for tt in range(NTT):
            tx = None
            for ch in range(NCH):
                w = []
                if ch == 0:
                    w = [y_done] if tt > 0 else list(self.stage_free)
                tx = self.dma("sync", self.slab4[:, ch * TT // 1:(ch) * TT + TT][:, 0:TT] if False else self.slab4[:, (ch % 8) * 512:(ch % 8) * 512 + TT],
                              self.d_xT[ch * 128:(ch + 1) * 128, tt * TT:(tt + 1) * TT],
                              waits=w)
            s_toks = []
            bf_state = {}
            sumb, sqb = self.pb[4], self.pb[5]
            pss, psq = sumb.ap(), sqb.ap()

            def ln1stat(ch, pss=None, psq=None):
                pss = sumb.ap() if pss is None else pss
                psq = sqb.ap() if psq is None else psq
                s1, s2, tb, tq = bf_state.pop(ch)
                mm(sumb, pss[0:1, 0:TT], self.ones[:, 0:1],
                   self.tmpbf[:, s1 * TT:(s1 + 1) * TT], stop=(ch == NCH - 1),
                   waits=[tb], inc=("PE", 1))
                tq2 = mm(sqb, psq[0:1, 0:TT], self.ones[:, 0:1],
                         self.tmpbf[:, s2 * TT:(s2 + 1) * TT], stop=(ch == NCH - 1),
                         waits=[tq], inc=("PE", 1))
                self.bfs.done(s1, tq2)
                self.bfs.done(s2, tq2)

            for ch in range(NCH):
                bank = self.pb[ch % 4]
                ps = bank.ap()
                for ci in range(8):
                    w = [self.t_wo, self.am_done] if (tt == 0 and ch == 0 and ci == 0) else []
                    mm(bank, ps[:, 0:TT],
                       self.slab1[:, ci * 1024 + ch * 128: ci * 1024 + ch * 128 + 128],
                       self.slab3[:, ci * TOK + tt * TT: ci * TOK + (tt + 1) * TT],
                       stop=(ci == 7), waits=w, inc=("PE", 1) if ci == 7 else None)
                tmm = em.tok("PE")
                ts_ = v(lambda e, ps=ps, ch=ch: e.scalar_tensor_tensor(
                    self.slab4[:, ch * 512:ch * 512 + TT], ps[:, 0:TT],
                    self.bo_c[:, ch:ch + 1], self.slab4[:, ch * 512:ch * 512 + TT],
                    ALU.add, ALU.add), waits=[tmm, tx])
                s_toks.append(ts_)
                bank.consumed(ts_)
                s1, w1_ = self.bfs.acquire()
                s2, w2_ = self.bfs.acquire()
                tb = a(lambda e, ch=ch, s=s1: e.copy(
                    self.tmpbf[:, s * TT:(s + 1) * TT],
                    self.slab4[:, ch * 512:ch * 512 + TT]),
                    waits=[ts_] + w1_)
                tq = v(lambda e, s1=s1, s2=s2: e.tensor_mul(
                    self.tmpbf[:, s2 * TT:(s2 + 1) * TT], self.tmpbf[:, s1 * TT:(s1 + 1) * TT],
                    self.tmpbf[:, s1 * TT:(s1 + 1) * TT]), waits=[tb] + w2_)
                bf_state[ch] = (s1, s2, tb, tq)
                if ch >= 1:
                    ln1stat(ch - 1)
            ln1stat(NCH - 1)
            tstat = em.tok("PE")
            r0, wr0 = self.rfs.acquire()
            r1, wr1 = self.rfs.acquire()
            r2, wr2 = self.rfs.acquire()
            R = lambda i: self.rowf[0:1, i * TT:(i + 1) * TT]
            tm_ = v(lambda e, r0=r0: e.tensor_scalar(R(r0), pss[0:1, 0:TT], 1.0 / 1024.0,
                                                     None, ALU.mult), waits=[tstat] + wr0)
            tv_ = v(lambda e, r1=r1: e.tensor_scalar(R(r1), psq[0:1, 0:TT], 1.0 / 1024.0,
                                                     None, ALU.mult), waits=[tstat] + wr1)
            sumb.consumed(tm_)
            sqb.consumed(tv_)
            tm2 = v(lambda e, r0=r0, r2=r2: e.tensor_mul(R(r2), R(r0), R(r0)),
                    waits=[tm_] + wr2)
            tvv = v(lambda e, r1=r1, r2=r2: e.tensor_sub(R(r1), R(r1), R(r2)),
                    waits=[tv_, tm2])
            tl = a(lambda e, r1=r1: e.activation(R(r1), R(r1), AF.Ln, bias=EPS), waits=[tvv])
            tr_ = a(lambda e, r1=r1: e.activation(R(r1), R(r1), AF.Exp, scale=-0.5),
                    waits=[tl])
            tmr = v(lambda e, r0=r0, r1=r1, r2=r2: e.tensor_mul(R(r2), R(r0), R(r1)),
                    waits=[tr_])
            b0, wb0 = self.rbs.acquire()
            b1, wb1 = self.rbs.acquire()
            RB = lambda i: self.rowb[0:1, i * TT:(i + 1) * TT]
            tbr = v(lambda e, b0=b0, r1=r1: e.tensor_copy(RB(b0), R(r1)), waits=[tr_] + wb0)
            tbm = v(lambda e, b1=b1, r2=r2: e.tensor_copy(RB(b1), R(r2)), waits=[tmr] + wb1)
            self.rfs.done(r0, tmr)
            self.rfs.done(r1, tbr)
            self.rfs.done(r2, tbm)
            rrb, rmb = self.pb[6], self.pb[7]
            prr, prm = rrb.ap(), rmb.ap()
            t_rr = mm(rrb, prr[:, 0:TT], self.ones[0:1, 0:128], RB(b0), stop=True,
                      waits=[tbr], inc=("PE", 1))
            t_rm = mm(rmb, prm[:, 0:TT], self.ones[0:1, 0:128], RB(b1), stop=True,
                      waits=[tbm], inc=("PE", 1))
            self.rbs.done(b0, t_rr)
            self.rbs.done(b1, t_rm)
            for ch in range(NCH):
                f0, wf0 = self.ffs.acquire()
                t1 = v(lambda e, ch=ch, f=f0, prr=prr: e.tensor_mul(
                    self.tmpf[:, f * TT:(f + 1) * TT],
                    self.slab4[:, ch * 512:ch * 512 + TT], prr[:, 0:TT]),
                    waits=[t_rr] + wf0)
                y_done = v(lambda e, ch=ch, f=f0, tt=tt, prm=prm: e.tensor_sub(
                    self.slab5[:, ch * TOK + tt * TT: ch * TOK + (tt + 1) * TT],
                    self.tmpf[:, f * TT:(f + 1) * TT], prm[:, 0:TT]), waits=[t_rm, t1])
                self.ffs.done(f0, y_done)
            rrb.consumed(y_done)
            rmb.consumed(y_done)
        self.y_done = y_done
        self.stage_free = [y_done]

    def phaseB(self):
        em, mm, v, a = self.em, self.mm, self.v, self.a
        bar0 = self.bar()
        self.xp = {k: list(bar0) for k in
                   ("rowpair", "g1g1b1", "g1b1row", "g2b2row", "grep", "sb1s",
                    "m1f", "m1on", "uc", "zt", "m2acat")}
        self.xp["w1s"] = []
        self.m2a_last = []
        prev_out = None
        for hh in range(NTH):
            tb0 = hh * TH
            self.t_m1on = v(lambda e: e.memset(self.kv_bf[0:2, 0:TH], 1.0),
                            waits=([self.am_done, self.em.tok("PE")] if hh == 0 else
                                   [prev_out] + self.bar()) + self.xp["m1on"])
            self.xp["m1on"] = [self.t_m1on]
            tgate = None
            for j in range(TH // 128):
                jb = tb0 + j * 128
                bank = self.pb[j % 2]
                ps = bank.ap()
                for ci in range(8):
                    w = [self.y_done, self.t_casts, self.t_m1on] if (j == 0 and ci == 0) else []
                    if hh > 0 and j == 0 and ci == 0:
                        w = [prev_out]
                    mm(bank, ps[:, 0:4],
                       self.slab5[:, ci * TOK + jb: ci * TOK + jb + 128],
                       self.gatec[:, ci * 4:(ci + 1) * 4], stop=False, waits=w)
                tmm = mm(bank, ps[:, 0:4], self.ones[0:1, 0:128], self.gateb[0:1, 0:4],
                         stop=True, inc=("PE", 1))
                f0, wf0 = self.ffs.acquire()
                Ft = self.tmpf[:, f0 * TT:(f0 + 1) * TT]
                tM1 = v(lambda e, ps=ps, Ft=Ft: e.tensor_reduce(
                    Ft[:, 0:1], ps[:, 0:4], AX.X, ALU.max, negate=True), waits=[tmm] + wf0)
                tmsk = v(lambda e, ps=ps, Ft=Ft: e.tensor_scalar(
                    Ft[:, 1:5], ps[:, 0:4], Ft[:, 0:1], 0.0, ALU.add, ALU.is_ge), waits=[tM1])
                ts2 = v(lambda e, ps=ps, Ft=Ft: e.scalar_tensor_tensor(
                    Ft[:, 5:9], Ft[:, 1:5], -1e9, ps[:, 0:4], ALU.mult, ALU.add), waits=[tmsk])
                tM2 = v(lambda e, Ft=Ft: e.tensor_reduce(
                    Ft[:, 9:10], Ft[:, 5:9], AX.X, ALU.max, negate=True), waits=[ts2])
                tsel = v(lambda e, ps=ps, Ft=Ft: e.tensor_scalar(
                    Ft[:, 14:18], ps[:, 0:4], Ft[:, 9:10], 0.0, ALU.add, ALU.is_ge),
                    waits=[tM2])
                tE = a(lambda e, ps=ps, Ft=Ft: e.activation(
                    Ft[:, 10:14], ps[:, 0:4], AF.Exp, bias=Ft[:, 0:1]), waits=[tsel])
                bank.consumed(tsel)
                bank.consumed(tE)
                tEs = v(lambda e, Ft=Ft: e.tensor_mul(Ft[:, 10:14], Ft[:, 10:14],
                                                      Ft[:, 14:18]), waits=[tE, tsel])
                tZ = v(lambda e, Ft=Ft: e.tensor_reduce(Ft[:, 18:19], Ft[:, 10:14], AX.X,
                                                        ALU.add), waits=[tEs])
                tZi = v(lambda e, Ft=Ft: e.reciprocal(Ft[:, 19:20], Ft[:, 18:19]),
                        waits=[tZ])
                tw = v(lambda e, Ft=Ft: e.tensor_scalar(
                    self.wtm[:, 0:4], Ft[:, 10:14], Ft[:, 19:20], None, ALU.mult),
                    waits=[tZi, getattr(self, "wtm_free", None)])
                self.ffs.done(f0, tw)
                wb2 = self.pb[2]
                pw = wb2.ap()
                tt_ = None
                for e_ in range(NE):
                    tt_ = mm(wb2, pw[0:1, e_ * 128:(e_ + 1) * 128], self.wtm[:, e_:e_ + 1],
                             self.idnb[:], stop=(e_ == NE - 1), waits=[tw],
                             inc=("PE", 1) if e_ == NE - 1 else None)
                self.wtm_free = tt_
                for e_ in range(NE):
                    tgate = v(lambda e, e_=e_, j=j, pw=pw: e.tensor_copy(
                        self.wcat[0:1, e_ * TH + j * 128: e_ * TH + (j + 1) * 128],
                        pw[0:1, e_ * 128:(e_ + 1) * 128]), waits=[tt_])
                wb2.consumed(tgate)
            self.t_gate = tgate
            t_acc = None
            for ch in range(NCH):
                for q in range(NTHT):
                    w = []
                    if ch == 0 and q == 0:
                        w = [self.t_gate]
                    t_acc = v(lambda e, ch=ch, q=q, tb=tb0: e.tensor_scalar(
                        self.slab3[:, ch * TH + q * TT: ch * TH + (q + 1) * TT],
                        self.slab5[:, ch * TOK + tb + q * TT: ch * TOK + tb + (q + 1) * TT],
                        2.0, None, ALU.mult), waits=w)
            acc_tok = t_acc
            wslot = [0]
            prev_w_pe = [None, None]
            for e_ in range(NE):
                acc_tok = self.expert(e_, hh, wslot, prev_w_pe, acc_tok)
            prev_out = self.final_ln(hh, acc_tok)
        self.out_done = prev_out

    def expert(self, e_, hh, wslot, prev_w_pe, acc_tok):
        em, mm, v, a, p = self.em, self.mm, self.v, self.a, self.p
        tb0 = hh * TH
        ws = wslot[0]
        wslot[0] ^= 1
        xp = self.xp
        wdst, woff = (self.slab3, 8192) if ws == 0 else (self.slab1, 0)
        WL = lambda ci, ch: wdst[:, woff + ci * 1024 + ch * 128: woff + ci * 1024 + ch * 128 + 128]

        RP0 = self.rowpair[0:1, 0:C]
        RP1 = self.rowpair[0:1, C:2 * C]
        self.dma("sync", RP0, self.d_b1cat[0:1, e_ * C:(e_ + 1) * C], sem="LR",
                 waits=xp["rowpair"])
        t_p2 = self.dma("sync", RP1, self.d_g1cat[0:1, e_ * C:(e_ + 1) * C], sem="LR")
        tg1 = v(lambda e: e.tensor_copy(self.g1g1b1[0:1, :], RP1),
                waits=[t_p2] + xp["g1g1b1"])
        tgb = v(lambda e: e.tensor_tensor(self.g1b1row[0:1, :], RP1, RP0, ALU.mult),
                waits=[t_p2] + xp["g1b1row"])
        t_stk = self.dma("gpsimd", self.g1g1b1[1:2, 0:C], self.g1b1row[0:1, 0:C],
                         sem="SG", waits=[tgb] + xp["g1g1b1"])
        tsb1 = v(lambda e: e.tensor_reduce(self.sb1s[0:1, 0:1], RP0, AX.X, ALU.add),
                 waits=[t_p2] + xp["sb1s"])
        gb = self.pb[6]
        pg = gb.ap()
        tr1 = mm(gb, pg[:, 0:512], self.ones[0:1, 0:128], self.g1g1b1[0:1, 0:512],
                 stop=True, waits=[tg1], inc=("PE", 1))
        te1 = a(lambda e: e.copy(self.kvf[:, 0:512], pg[:, 0:512]),
                waits=[tr1] + xp["grep"])
        gb.consumed(te1)
        tr2 = mm(gb, pg[:, 0:512], self.ones[0:1, 0:128], self.g1g1b1[0:1, 512:1024],
                 stop=True, waits=[tg1], inc=("PE", 1))
        te2 = a(lambda e: e.copy(self.kvf[:, 512:1024], pg[:, 0:512]), waits=[tr2])
        gb.consumed(te2)
        self.w1s_free = xp["w1s"][0] if xp["w1s"] else None
        t_w1g = self.load_weight(self.d_w1[e_], wdst, woff, scale=self.kvf[:, 0:1024],
                                 w1s=True, extra_wait=prev_w_pe[ws], cast_waits=[te1, te2])
        t_w1s = self.t_w1s
        self.dma("sync", RP0, self.d_g2cat[0:1, e_ * C:(e_ + 1) * C], sem="LR",
                 waits=[tgb, tsb1, tg1])
        t_p4 = self.dma("sync", RP1, self.d_b2cat[0:1, e_ * C:(e_ + 1) * C], sem="LR")
        tg2b2 = v(lambda e: e.tensor_tensor(self.g2b2row[0:1, :], RP0, RP1, ALU.mult),
                  waits=[t_p4] + xp["g2b2row"])
        tg2r = v(lambda e: e.tensor_copy(self.g1b1row[0:1, :], RP0),
                 waits=[t_stk, t_p4])
        gb2 = self.pb[6]
        pg2 = gb2.ap()
        tw21 = mm(gb2, pg2[:, 0:512], self.ones[0:1, 0:128], self.g1b1row[0:1, 0:512],
                  stop=True, waits=[tg2r], inc=("PE", 1))
        tge1 = a(lambda e: e.copy(self.kvf[:, 0:512], pg2[:, 0:512]),
                 waits=[tw21, t_w1g])
        gb2.consumed(tge1)
        tw22 = mm(gb2, pg2[:, 0:512], self.ones[0:1, 0:128], self.g1b1row[0:1, 512:1024],
                  stop=True, waits=[tg2r], inc=("PE", 1))
        tge2 = a(lambda e: e.copy(self.kvf[:, 512:1024], pg2[:, 0:512]), waits=[tw22])
        gb2.consumed(tge2)
        ows = ws ^ 1
        owdst, owoff = (self.slab3, 8192) if ows == 0 else (self.slab1, 0)
        t_w2g = self.load_weight(self.d_w2[e_], owdst, owoff, scale=self.kvf[:, 0:1024],
                                 extra_wait=prev_w_pe[ows], cast_waits=[tge1, tge2])
        W2L = lambda ci, ch: owdst[:, owoff + ci * 1024 + ch * 128: owoff + ci * 1024 + ch * 128 + 128]
        xp["rowpair"] = [tg2r, tg2b2]
        xp["grep"] = [t_w2g]
        xp["g1b1row"] = [tw22]
        xp["sb1s"] = []

        m1toks = []
        sB = self.pb[4]
        tm1s = []
        for q in range(NTHT):
            psS = sB.ap()
            for ci in range(8):
                w = [t_w1s, self.y_done] if ci == 0 else []
                mm(sB, psS[0:1, 0:TT], self.w1s[:, ci:ci + 1],
                   self.slab5[:, ci * TOK + tb0 + q * TT: ci * TOK + tb0 + (q + 1) * TT],
                   stop=(ci == 7), waits=w, inc=("PE", 1) if ci == 7 else None)
            tm1 = v(lambda e, q=q, psS=psS: e.tensor_scalar(
                self.m1f[0:1, q * TT:(q + 1) * TT], psS[0:1, 0:TT],
                self.sb1s[0:1, 0:1], 1.0 / 1024.0, ALU.add, ALU.mult),
                waits=[em.tok("PE"), tsb1] + xp["m1f"])
            tneg = v(lambda e, q=q: e.tensor_scalar(
                self.kv_bf[0:1, q * TT:(q + 1) * TT], self.m1f[0:1, q * TT:(q + 1) * TT],
                -1.0, None, ALU.mult), waits=[tm1] + xp["m1on"])
            sB.consumed(tm1)
            m1toks.append(tneg)
            tm1s.append(tm1)
        self.w1s_next = em.tok("PE")
        xp["w1s"] = [self.w1s_next]
        uc_toks = {}
        sq_toks = {}
        q2b = [self.pb[4], self.pb[5]]
        units = [(ch, q) for ch in range(NCH) for q in range(NTHT)]
        done_per_q = {q: 0 for q in range(NTHT)}

        def u_stat(idx):
            ch, q = units[idx]
            s1, tsq = sq_toks[(ch, q)]
            tqm = mm(q2b[q], q2b[q].ap()[0:1, 0:TT],
                     self.ivg1sq[:, e_ * 8 + ch:e_ * 8 + ch + 1],
                     self.tmpbf[:, s1 * TT:(s1 + 1) * TT],
                     stop=(done_per_q[q] == NCH - 1), waits=[tsq], inc=("PE", 1))
            done_per_q[q] += 1
            self.bfs.done(s1, tqm)

        for idx, (ch, q) in enumerate(units):
            bank = self.pb[idx % 4]
            ps = bank.ap()
            for ci in range(8):
                w = [t_w1g] if idx == 0 and ci == 0 else []
                mm(bank, ps[:, 0:TT], WL(ci, ch),
                   self.slab5[:, ci * TOK + tb0 + q * TT: ci * TOK + tb0 + (q + 1) * TT],
                   stop=False, waits=w)
            tmm = mm(bank, ps[:, 0:TT], self.g1g1b1[0:2, ch * 128:(ch + 1) * 128],
                     self.kv_bf[0:2, q * TT:(q + 1) * TT], stop=True,
                     waits=[m1toks[q], t_stk], inc=("PE", 1))
            ucol = 8192 + ch * TH + q * TT
            tuc = a(lambda e, ps=ps, d=ucol: e.copy(self.slab1[:, d:d + TT], ps[:, 0:TT]),
                    waits=[tmm] + (xp["uc"] if idx == 0 else []))
            bank.consumed(tuc)
            s1, w1_ = self.bfs.acquire()
            tsq = v(lambda e, d=ucol, s=s1: e.tensor_mul(
                self.tmpbf[:, s * TT:(s + 1) * TT], self.slab1[:, d:d + TT],
                self.slab1[:, d:d + TT]), waits=[tuc] + w1_)
            uc_toks[(ch, q)] = tuc
            sq_toks[(ch, q)] = (s1, tsq)
            if idx >= 5:
                u_stat(idx - 5)
        t_umm = em.tok("PE")
        for idx in range(len(units) - 5, len(units)):
            u_stat(idx)
        t_uq = em.tok("PE")
        h_toks = {}
        for q in range(NTHT):
            r0, wr0 = self.rfs.acquire()
            r1_, wr1 = self.rfs.acquire()
            R0 = self.rowf[0:1, r0 * TT:(r0 + 1) * TT]
            R1 = self.rowf[0:1, r1_ * TT:(r1_ + 1) * TT]
            tE2 = v(lambda e, q=q, R0=R0: e.tensor_scalar(
                R0, q2b[q].ap()[0:1, 0:TT], 1.0 / 1024.0, None, ALU.mult),
                waits=[t_uq] + wr0)
            q2b[q].consumed(tE2)
            tm2 = v(lambda e, q=q, R1=R1: e.tensor_mul(
                R1, self.m1f[0:1, q * TT:(q + 1) * TT], self.m1f[0:1, q * TT:(q + 1) * TT]),
                waits=wr1 + [m1toks[q]])
            tvv = v(lambda e, R0=R0, R1=R1: e.tensor_sub(R0, R0, R1), waits=[tE2, tm2])
            tl = a(lambda e, R0=R0: e.activation(R0, R0, AF.Ln, bias=EPS), waits=[tvv])
            tr_ = a(lambda e, R0=R0: e.activation(R0, R0, AF.Exp, scale=-0.5), waits=[tl])
            b0, wb0 = self.rbs.acquire()
            RB0 = self.rowb[0:1, b0 * TT:(b0 + 1) * TT]
            tbr = v(lambda e, RB0=RB0, R0=R0: e.tensor_copy(RB0, R0), waits=[tr_] + wb0)
            self.rfs.done(r0, tbr)
            self.rfs.done(r1_, tvv)
            if q == NTHT - 1:
                xp["m1f"] = [tm2]
            repb = q2b[q]
            prp = repb.ap()
            t_rep = mm(repb, prp[:, 0:TT], self.ones[0:1, 0:128], RB0, stop=True,
                       waits=[tbr], inc=("PE", 1))
            self.rbs.done(b0, t_rep)
            lasth = None
            for ch in range(NCH):
                ucol = 8192 + ch * TH + q * TT
                s1, w1_ = self.bfs.acquire()
                t3 = v(lambda e, d=ucol, s=s1, prp=prp: e.tensor_mul(
                    self.tmpbf[:, s * TT:(s + 1) * TT], self.slab1[:, d:d + TT],
                    prp[:, 0:TT]), waits=[t_rep, uc_toks[(ch, q)]] + w1_)
                th = a(lambda e, s=s1, ch=ch, q=q: e.activation(
                    self.slab2[:, ch * TH + q * TT: ch * TH + (q + 1) * TT],
                    self.tmpbf[:, s * TT:(s + 1) * TT], AF.Relu,
                    bias=self.be1c[:, e_ * 8 + ch:e_ * 8 + ch + 1]), waits=[t3])
                self.bfs.done(s1, th)
                h_toks[(ch, q)] = th
                lasth = th
            repb.consumed(lasth)
        xp["uc"] = [lasth]
        xp["m1on"] = [t_umm]

        zt_toks = {}
        zq_toks = {}
        zsb = {0: (self.pb[4], self.pb[5]), 1: (self.pb[6], self.pb[7])}
        zdone = {q: 0 for q in range(NTHT)}

        def z_stat(idx):
            ch, q = units[idx]
            s1, tsq = zq_toks[(ch, q)]
            sumb, sqb = zsb[q]
            last = zdone[q] == NCH - 1
            mm(sumb, sumb.ap()[0:1, 0:TT], self.ivg2[:, e_ * 8 + ch:e_ * 8 + ch + 1],
               self.slab2[:, 8192 + ch * TH + q * TT: 8192 + ch * TH + q * TT + TT],
               stop=last, waits=[zt_toks[(ch, q)]], inc=("PE", 1))
            tqm = mm(sqb, sqb.ap()[0:1, 0:TT],
                     self.ivg2sq[:, e_ * 8 + ch:e_ * 8 + ch + 1],
                     self.tmpbf[:, s1 * TT:(s1 + 1) * TT], stop=last,
                     waits=[tsq], inc=("PE", 1))
            zdone[q] += 1
            self.bfs.done(s1, tqm)

        for idx, (ch, q) in enumerate(units):
            bank = self.pb[idx % 4]
            ps = bank.ap()
            for ci in range(8):
                w = [h_toks[(ci, q)]]
                if idx == 0 and ci == 0:
                    w.append(t_w2g)
                mm(bank, ps[:, 0:TT], W2L(ci, ch),
                   self.slab2[:, ci * TH + q * TT: ci * TH + (q + 1) * TT],
                   stop=False, waits=w)
            tmm = mm(bank, ps[:, 0:TT], self.g2b2row[0:1, ch * 128:(ch + 1) * 128],
                     self.ones[0:1, 0:TT], stop=True, waits=[tg2b2], inc=("PE", 1))
            zcol = 8192 + ch * TH + q * TT
            tze = a(lambda e, ps=ps, d=zcol: e.copy(self.slab2[:, d:d + TT], ps[:, 0:TT]),
                    waits=[tmm] + (xp["zt"] if idx == 0 else []))
            bank.consumed(tze)
            s1, w1_ = self.bfs.acquire()
            tsq = v(lambda e, d=zcol, s=s1: e.tensor_mul(
                self.tmpbf[:, s * TT:(s + 1) * TT], self.slab2[:, d:d + TT],
                self.slab2[:, d:d + TT]), waits=[tze] + w1_)
            zt_toks[(ch, q)] = tze
            zq_toks[(ch, q)] = (s1, tsq)
            if idx >= 5:
                z_stat(idx - 5)
        t_zmm = em.tok("PE")
        for idx in range(len(units) - 5, len(units)):
            z_stat(idx)
        t_zq = em.tok("PE")
        tacc = acc_tok
        m2a_toks = []
        for q in range(NTHT):
            sumb, sqb = zsb[q]
            r0, wr0 = self.rfs.acquire()
            r1_, wr1 = self.rfs.acquire()
            r2_, wr2 = self.rfs.acquire()
            R0 = self.rowf[0:1, r0 * TT:(r0 + 1) * TT]
            R1 = self.rowf[0:1, r1_ * TT:(r1_ + 1) * TT]
            R2 = self.rowf[0:1, r2_ * TT:(r2_ + 1) * TT]
            tm2_ = v(lambda e, R0=R0, sumb=sumb: e.tensor_scalar(
                R0, sumb.ap()[0:1, 0:TT], 1.0 / 1024.0, None, ALU.mult),
                waits=[t_zq] + wr0)
            tE2 = v(lambda e, R1=R1, sqb=sqb: e.tensor_scalar(
                R1, sqb.ap()[0:1, 0:TT], 1.0 / 1024.0, None, ALU.mult),
                waits=[t_zq] + wr1)
            sumb.consumed(tm2_)
            sqb.consumed(tE2)
            tsq2 = v(lambda e, R0=R0, R2=R2: e.tensor_mul(R2, R0, R0), waits=[tm2_] + wr2)
            tvv = v(lambda e, R1=R1, R2=R2: e.tensor_sub(R1, R1, R2), waits=[tE2, tsq2])
            tl = a(lambda e, R1=R1: e.activation(R1, R1, AF.Ln, bias=EPS), waits=[tvv])
            tr2_ = a(lambda e, R1=R1: e.activation(R1, R1, AF.Exp, scale=-0.5), waits=[tl])
            tA = v(lambda e, R1=R1, R2=R2, q=q: e.tensor_tensor(
                R2, self.wcat[0:1, e_ * TH + q * TT: e_ * TH + (q + 1) * TT],
                R1, ALU.mult), waits=[tr2_, self.t_gate])
            b0, wb0 = self.rbs.acquire()
            RB0 = self.rowb[0:1, b0 * TT:(b0 + 1) * TT]
            tAb = v(lambda e, RB0=RB0, R2=R2: e.tensor_copy(RB0, R2), waits=[tA] + wb0)
            tma = v(lambda e, R0=R0, R2=R2: e.tensor_mul(R0, R0, R2), waits=[tA, tsq2])
            tmab = v(lambda e, R0=R0, q=q: e.tensor_scalar(
                self.m2acat[0:1, e_ * TH + q * TT: e_ * TH + (q + 1) * TT], R0,
                -1.0, None, ALU.mult), waits=[tma] + xp["m2acat"])
            m2a_toks.append(tmab)
            self.rfs.done(r0, tmab)
            self.rfs.done(r1_, tA)
            self.rfs.done(r2_, tma)
            repb = sumb
            prp = repb.ap()
            t_rep = mm(repb, prp[:, 0:TT], self.ones[0:1, 0:128], RB0, stop=True,
                       waits=[tAb], inc=("PE", 1))
            self.rbs.done(b0, t_rep)
            lastc = None
            for ch in range(NCH):
                zcol = 8192 + ch * TH + q * TT
                s1, w1_ = self.bfs.acquire()
                tmul = v(lambda e, d=zcol, s=s1, prp=prp: e.tensor_mul(
                    self.tmpbf[:, s * TT:(s + 1) * TT], self.slab2[:, d:d + TT],
                    prp[:, 0:TT]), waits=[t_rep, zt_toks[(ch, q)]] + w1_)
                tadd = p(lambda e, ch=ch, q=q, s=s1: e.tensor_tensor(
                    self.slab3[:, ch * TH + q * TT: ch * TH + (q + 1) * TT],
                    self.slab3[:, ch * TH + q * TT: ch * TH + (q + 1) * TT],
                    self.tmpbf[:, s * TT:(s + 1) * TT], ALU.add),
                    waits=[tmul, acc_tok])
                self.bfs.done(s1, tadd)
                lastc = tadd
            repb.consumed(lastc)
            tacc = lastc
        xp["zt"] = [lastc]
        self.m2a_last = m2a_toks
        xp["g2b2row"] = [t_zmm]
        xp["g1g1b1"] = [t_umm, tw22]
        prev_w_pe[ws] = t_umm
        self.last_umm = t_umm
        prev_w_pe[ows] = t_zmm
        return tacc

    def final_ln(self, hh, acc_tok):
        em, mm, v, a = self.em, self.mm, self.v, self.a
        tb0 = hh * TH
        last_out = None
        rko = {0: 0, 1: 520}
        w0 = [getattr(self, "prev_rnk", None), self.t_gate,
              getattr(self, "last_umm", None)] + self.m2a_last
        tstk = None
        for q in range(NTHT):
            for e_ in range(NE):
                tstk = self.dma("gpsimd",
                                self.kv_bf[2 * e_:2 * e_ + 1, rko[q]:rko[q] + TT],
                                self.m2acat[0:1, e_ * TH + q * TT: e_ * TH + (q + 1) * TT],
                                sem="SG", waits=w0 if tstk is None else [])
                tstk = self.dma("gpsimd",
                                self.kv_bf[2 * e_ + 1:2 * e_ + 2, rko[q]:rko[q] + TT],
                                self.wcat[0:1, e_ * TH + q * TT: e_ * TH + (q + 1) * TT],
                                sem="SG")
        self.xp["m2acat"] = [tstk]
        units = [(q, ch) for q in range(NTHT) for ch in range(NCH)]
        sf_toks = {}
        sq_toks = {}
        zsb = {0: (self.pb[4], self.pb[5]), 1: (self.pb[6], self.pb[7])}
        fdone = {q: 0 for q in range(NTHT)}

        def fstat(idx):
            q, ch = units[idx]
            s1, tsq = sq_toks[(q, ch)]
            sumb, sqb = zsb[q]
            lastf = fdone[q] == NCH - 1
            mm(sumb, sumb.ap()[0:1, 0:TT], self.ones[:, 0:1],
               self.slab2[:, ch * TH + q * TT: ch * TH + (q + 1) * TT],
               stop=lastf, waits=[sf_toks[(q, ch)]], inc=("PE", 1))
            tqm = mm(sqb, sqb.ap()[0:1, 0:TT], self.ones[:, 0:1],
                     self.tmpbf[:, s1 * TT:(s1 + 1) * TT], stop=lastf,
                     waits=[tsq], inc=("PE", 1))
            fdone[q] += 1
            self.bfs.done(s1, tqm)

        for idx, (q, ch) in enumerate(units):
            bank = self.pb[idx % 4]
            ps = bank.ap()
            trk = mm(bank, ps[:, 0:TT], self.rklb[0:8, ch * 128:(ch + 1) * 128],
                     self.kv_bf[0:8, rko[q]:rko[q] + TT], stop=True,
                     waits=[tstk, self.t_casts], inc=("PE", 1))
            tsf = v(lambda e, ps=ps, ch=ch, q=q: e.tensor_add(
                self.slab2[:, ch * TH + q * TT: ch * TH + (q + 1) * TT], ps[:, 0:TT],
                self.slab3[:, ch * TH + q * TT: ch * TH + (q + 1) * TT]),
                waits=[trk, acc_tok])
            sf_toks[(q, ch)] = tsf
            bank.consumed(tsf)
            s1, w1_ = self.bfs.acquire()
            tsq = v(lambda e, ch=ch, q=q, s=s1: e.tensor_mul(
                self.tmpbf[:, s * TT:(s + 1) * TT],
                self.slab2[:, ch * TH + q * TT: ch * TH + (q + 1) * TT],
                self.slab2[:, ch * TH + q * TT: ch * TH + (q + 1) * TT]),
                waits=[tsf] + w1_)
            sq_toks[(q, ch)] = (s1, tsq)
            if idx >= 5:
                fstat(idx - 5)
        for idx in range(len(units) - 5, len(units)):
            fstat(idx)
        self.prev_rnk = em.tok("PE")
        tstat = em.tok("PE")
        repbanks = {0: (self.pb[4], self.pb[5]), 1: (self.pb[6], self.pb[7])}
        for q in range(NTHT):
            sumb, sqb = zsb[q]
            r0, wr0 = self.rfs.acquire()
            r1_, wr1 = self.rfs.acquire()
            r2_, wr2 = self.rfs.acquire()
            R0 = self.rowf[0:1, r0 * TT:(r0 + 1) * TT]
            R1 = self.rowf[0:1, r1_ * TT:(r1_ + 1) * TT]
            R2 = self.rowf[0:1, r2_ * TT:(r2_ + 1) * TT]
            tm_ = v(lambda e, R0=R0, sumb=sumb: e.tensor_scalar(
                R0, sumb.ap()[0:1, 0:TT], 1.0 / 1024.0, None, ALU.mult),
                waits=[tstat] + wr0)
            tv_ = v(lambda e, R1=R1, sqb=sqb: e.tensor_scalar(
                R1, sqb.ap()[0:1, 0:TT], 1.0 / 1024.0, None, ALU.mult),
                waits=[tstat] + wr1)
            sumb.consumed(tm_)
            sqb.consumed(tv_)
            tm2 = v(lambda e, R0=R0, R2=R2: e.tensor_mul(R2, R0, R0), waits=[tm_] + wr2)
            tvv = v(lambda e, R1=R1, R2=R2: e.tensor_sub(R1, R1, R2), waits=[tv_, tm2])
            tl = a(lambda e, R1=R1: e.activation(R1, R1, AF.Ln, bias=EPS), waits=[tvv])
            tr_ = a(lambda e, R1=R1: e.activation(R1, R1, AF.Exp, scale=-0.5), waits=[tl])
            tmr = v(lambda e, R0=R0, R1=R1, R2=R2: e.tensor_mul(R2, R0, R1), waits=[tr_])
            b0, wb0 = self.rbs.acquire()
            b1, wb1 = self.rbs.acquire()
            RB0 = self.rowb[0:1, b0 * TT:(b0 + 1) * TT]
            RB1 = self.rowb[0:1, b1 * TT:(b1 + 1) * TT]
            tbr = v(lambda e, RB0=RB0, R1=R1: e.tensor_copy(RB0, R1), waits=[tr_] + wb0)
            tbm = v(lambda e, RB1=RB1, R2=R2: e.tensor_copy(RB1, R2), waits=[tmr] + wb1)
            self.rfs.done(r0, tmr)
            self.rfs.done(r1_, tbr)
            self.rfs.done(r2_, tbm)
            rrb, rmb = repbanks[q]
            prr, prm = rrb.ap(), rmb.ap()
            t_rr = mm(rrb, prr[:, 0:TT], self.ones[0:1, 0:128], RB0, stop=True,
                      waits=[tbr], inc=("PE", 1))
            t_rm = mm(rmb, prm[:, 0:TT], self.ones[0:1, 0:128], RB1, stop=True,
                      waits=[tbm], inc=("PE", 1))
            self.rbs.done(b0, t_rr)
            self.rbs.done(b1, t_rm)
            for ch in range(NCH):
                f0, wf0 = self.ffs.acquire()
                f1, wf1 = self.ffs.acquire()
                t1 = v(lambda e, ch=ch, q=q, f=f0, prr=prr: e.tensor_mul(
                    self.tmpf[:, f * TT:(f + 1) * TT],
                    self.slab2[:, ch * TH + q * TT: ch * TH + (q + 1) * TT], prr[:, 0:TT]),
                    waits=[t_rr] + wf0)
                to = v(lambda e, f0=f0, f1=f1, prm=prm: e.tensor_sub(
                    self.tmpf[:, f1 * TT:(f1 + 1) * TT], self.tmpf[:, f0 * TT:(f0 + 1) * TT],
                    prm[:, 0:TT]), waits=[t_rm, t1] + wf1)
                self.ffs.done(f0, to)
                tdo = self.dma("gpsimd", self.d_out[ch * 128:(ch + 1) * 128,
                                                    tb0 + q * TT: tb0 + (q + 1) * TT],
                               self.tmpf[:, f1 * TT:(f1 + 1) * TT], sem="ST", waits=[to])
                self.ffs.done(f1, tdo)
                last_out = tdo
            rrb.consumed(last_out)
            rmb.consumed(last_out)
        return last_out

    def finish(self):
        em = self.em
        if self.dbg is not None:
            slab = {"k": self.slab2, "q": self.slab2, "v": self.slab3,
                    "attnmix": self.slab3, "y": self.slab5}[self.dbg]
            last = {"k": getattr(self, "tk_done", None), "q": getattr(self, "q_done", None),
                    "v": getattr(self, "tv_done", None),
                    "attnmix": getattr(self, "am_done", None),
                    "y": getattr(self, "y_done", None)}[self.dbg]
            tdma = None
            for ch in range(8):
                for w_ in range(TOK // TT):
                    sl = w_ % 2
                    wl = [last] + ([tdma] if tdma else [])
                    tc = self.v(lambda e, ch=ch, w_=w_, s=sl: e.tensor_copy(
                        self.tmpf[:, s * TT:(s + 1) * TT],
                        slab[:, ch * TOK + w_ * TT: ch * TOK + (w_ + 1) * TT]), waits=wl)
                    tdma = self.dma("gpsimd",
                                    self.d_out[ch * 128:(ch + 1) * 128,
                                               w_ * TT:(w_ + 1) * TT],
                                    self.tmpf[:, sl * TT:(sl + 1) * TT], sem="ST",
                                    waits=[tc])
        em.op("gpsimd", lambda e: e.engine_nop(),
              waits=[em.tok("SG"), em.tok("LD"), em.tok("LR"), em.tok("ST"),
                     em.tok("CC")])
        handles = {}
        cms = []
        for name in ("LD", "LR", "PE", "DV", "AC", "PL", "SG", "ST", "CC"):
            cm = self.nc.semaphore(name)
            handles[name] = cm.__enter__()
            cms.append(cm)
        blk = self.nc.Block()
        block = blk.__enter__()
        self.em.replay(block, handles)
        blk.__exit__(None, None, None)
        for cm in reversed(cms):
            cm.__exit__(None, None, None)
        self.ctx.close()


def build(phase_limit=99, dbg=None):
    return MK(dbg=dbg, phase_limit=phase_limit).build()


def prep_inputs(core, inp, tok=TOK):
    asc = np.ascontiguousarray
    f = lambda a: np.asarray(a, dtype=np.float32)
    xf = f(inp["L0_attn_x"]).reshape(-1, C)
    xs = xf[core * tok:(core + 1) * tok]
    cols = lambda b: asc(f(b).reshape(8, 128).T)
    rows = lambda b: f(b).reshape(1, -1)
    catr = lambda key: asc(np.concatenate([f(inp[key][e]).reshape(1, C) for e in range(NE)], axis=1))
    colse = lambda key: asc(np.stack([f(inp[key][e]).reshape(8, 128) for e in range(NE)])
                            .transpose(2, 0, 1).reshape(128, 32))
    def colse2(key):
        v = np.stack([f(inp[key][e]).reshape(8, 128) for e in range(NE)])
        out = np.zeros((128, 32), np.float32)
        for e in range(NE):
            for ch in range(8):
                out[:, e * 8 + ch] = v[e, ch]
        return asc(out)
    gatec = asc(f(inp["L0_ffn_gate_W"]).reshape(8, 128, 4).transpose(1, 0, 2).reshape(128, 32))
    rkl = np.zeros((8, C), np.float32)
    for e in range(NE):
        rkl[2 * e] = f(inp["L0_ffn_g2"][e])
        rkl[2 * e + 1] = f(inp["L0_ffn_be2"][e])
    return dict(
        xT=asc(xs.T),
        wqT=asc(f(inp["L0_attn_Wq"]).T), wkT=asc(f(inp["L0_attn_Wk"]).T),
        wvT=asc(f(inp["L0_attn_Wv"]).T), woT=asc(f(inp["L0_attn_Wo"]).T),
        w1=asc(f(inp["L0_ffn_W1"])), w2=asc(f(inp["L0_ffn_W2"])),
        gatec=gatec, gateb=f(inp["L0_ffn_gate_b"]).reshape(1, 4),
        bq_c=cols(inp["L0_attn_bq"]), bo_c=cols(inp["L0_attn_bo"]),
        bk_r=rows(inp["L0_attn_bk"]), bv_r=rows(inp["L0_attn_bv"]),
        b1cat=catr("L0_ffn_b1"), g1cat=catr("L0_ffn_g1"),
        g2cat=catr("L0_ffn_g2"), b2cat=catr("L0_ffn_b2"),
        g1c=colse2("L0_ffn_g1"), be1c=colse2("L0_ffn_be1"), g2c=colse2("L0_ffn_g2"),
        rkl=asc(rkl),
        idn=np.eye(128, dtype=np.float32),
        cvec=np.tile(np.array([[EPS, -0.5]], np.float32), (128, 1)),
    )


from concourse.bass_utils import run_bass_kernel_spmd


def kernel(**inputs):
    configure(tok=2048, th=1024, rg=[[0, 1, 2, 3], [4, 5, 6, 7]])
    nc = build(phase_limit=99)
    in_maps = [prep_inputs(c, inputs, tok=2048) for c in range(8)]
    res = run_bass_kernel_spmd(nc, in_maps, core_ids=list(range(8)))
    outs = [np.asarray(res.results[c]["outT"]) for c in range(8)]
    full = np.concatenate([o.T for o in outs], axis=0)
    return np.ascontiguousarray(full.reshape(2, 8192, 1024)).astype(np.float32)
